# version 01
# baseline (speedup 1.0000x reference)
import contextlib

import numpy as np
import ml_dtypes

import concourse.bass as bass
import concourse.mybir as mybir
from concourse.bass_utils import run_bass_kernel_spmd

F32 = mybir.dt.float32
BF16 = mybir.dt.bfloat16
U32 = mybir.dt.uint32
AF = mybir.ActivationFunctionType
ALU = mybir.AluOpType
AX = mybir.AxisListType.X
BF = ml_dtypes.bfloat16
AP = bass.AP

B, S, D = 4, 8192, 2048
HD = 128
NHQ, NHV = 8, 16
CH = 128
NCH = S // CH
TA = 512
NTA = S // TA
SEG = 1024
NSEG = S // SEG
TB = 512
NTB = (S // 2) // TB
FQKV = 4096
NF = FQKV // 128
EPS = 1e-6
LNEPS = 1e-5
F32EPS = float(np.finfo(np.float32).eps)
MNEG = -1.0e9
LN128 = float(np.log(128.0))

ENGINES = ("sync", "tensor", "vector", "scalar", "gpsimd")
DR_ALL = True


class KB:
    def __init__(self, dev_dump=False, stop_after_a=False, nta_lim=NTA, lvl=99):
        self.dev = dev_dump
        self.stop_a = stop_after_a
        self.nta_lim = nta_lim
        self.lvl = lvl
        self.nc = bass.Bass("TRN2", target_bir_lowering=False, debug=False,
                            num_devices=8, detect_race_conditions=False)
        self.ops = {e: [] for e in ENGINES}
        names = ["pe", "dv", "ac", "po"]
        for cls in ("qx", "qw", "qh", "qb", "qo"):
            names += [cls + str(i) for i in range(4)]
        names += ["cc" + str(i) for i in range(NSEG)]
        self.count = {s: 0 for s in names}
        self.rot = {cls: 0 for cls in ("qx", "qw", "qh", "qb", "qo")}
        self.sems = {}
        self.octx = contextlib.ExitStack()

    def emit(self, eng, fn, waits=(), inc=None, amt=1, dr=False):
        wl = []
        for w in waits:
            if isinstance(w, tuple):
                if w[1] > 0:
                    wl.append(w)
            elif self.count[w] > 0:
                wl.append((w, self.count[w]))
        if inc is not None:
            self.count[inc] += amt
            self.ops[eng].append((fn, wl, (inc, amt), dr))
        else:
            self.ops[eng].append((fn, wl, None, dr))

    def dma(self, eng, fn, cls, waits=()):
        i = self.rot[cls]
        self.rot[cls] = (i + 1) % 4
        sem = cls + str(i)
        sw = [(sem, self.count[sem])] if self.count[sem] > 0 else []
        self.emit(eng, fn, waits=list(waits) + sw, inc=sem, amt=16)
        return (sem, self.count[sem])

    def qall(self, cls):
        return [(cls + str(i), self.count[cls + str(i)]) for i in range(4)
                if self.count[cls + str(i)] > 0]

    def mm(self, out_ap, lhsT, rhs, start=True, stop=True, waits=(), inc=None):
        self.emit("tensor",
                  lambda e, o=out_ap, l=lhsT, r=rhs, s1=start, s2=stop:
                  e.matmul(o, l, r, start=s1, stop=s2), waits=waits, inc=inc)

    def realize(self):
        block = self.octx.enter_context(self.nc.Block())
        sems = self.sems

        def make_body(eng_name):
            ops = self.ops[eng_name]

            def body(eng):
                wm = {}
                for fn, waits, inc, dr in ops:
                    for sn, v in waits:
                        if wm.get(sn, 0) < v:
                            eng.wait_ge(sems[sn], v)
                            wm[sn] = v
                    ins = fn(eng)
                    if dr or (eng_name == "vector" and DR_ALL):
                        dins = eng.drain()
                        if ins is not None:
                            ins = dins
                    if inc is not None:
                        sn, a = inc
                        if ins is not None:
                            ins.then_inc(sems[sn], a)
                        else:
                            eng.sem_inc(sems[sn], a)

            return body

        block.sync(make_body("sync"))
        block.tensor(make_body("tensor"))
        block.vector(make_body("vector"))
        block.scalar(make_body("scalar"))
        block.gpsimd(make_body("gpsimd"))

    @staticmethod
    def r3(ap_):
        return ap_.rearrange("p (h c) -> p h c", c=128)

    @staticmethod
    def r4(ap_):
        return ap_.rearrange("p (h d c) -> p h d c", d=2, c=128)

    @staticmethod
    def bc_ap(t, p0, plen, base, dims):
        a = t[p0:p0 + plen, base:base + 1]
        return AP(a.tensor, a.offset, [list(a.ap[0])] + [list(d) for d in dims])

    def dupq4(self, t, base, hstep):
        return self.bc_ap(t, 0, 128, base, [[hstep, NHQ], [0, 2], [1, 128]])

    def colbc(self, t, nh, rep, base=0):
        return self.bc_ap(t, 0, 128, base, [[1, nh], [0, rep]])

    def coldup(self, t, base=0):
        return self.bc_ap(t, 0, 128, base, [[1, NHQ], [0, 2]])

    def build(self):
        nc = self.nc
        dt = nc.dram_tensor
        s = self
        s.x_in = dt("x", [S, D], F32, kind="ExternalInput").ap()
        s.xh_in = dt("xh", [S // 2, D], F32, kind="ExternalInput").ap()
        s.wqkvT = dt("wqkvT", [D, FQKV], F32, kind="ExternalInput").ap()
        s.wbaT = dt("wbaT", [D, 32], F32, kind="ExternalInput").ap()
        s.convw_in = dt("convw", [FQKV, 4], F32, kind="ExternalInput").ap()
        s.dtb_in = dt("dtb", [1, NHV], F32, kind="ExternalInput").ap()
        s.alog_in = dt("alog", [1, NHV], F32, kind="ExternalInput").ap()
        s.normw_in = dt("normw", [HD, 1], F32, kind="ExternalInput").ap()
        s.wzT_in = dt("wzT", [D, 4096], F32, kind="ExternalInput").ap()
        s.woutT_in = dt("woutT", [4096, D], F32, kind="ExternalInput").ap()
        s.wgu_in = dt("wgu", [16384, D], F32, kind="ExternalInput").ap()
        s.wp_in = dt("wp", [D, 8192], F32, kind="ExternalInput").ap()
        s.hoff_in = dt("hoff", [1, 1], U32, kind="ExternalInput").ap()
        s.out = dt("out", [S // 2, D], F32, kind="ExternalOutput").ap()

        s.wqkv_bf = dt("wqkv_bf", [D, FQKV], BF16, kind="Internal").ap()
        s.wba_bf = dt("wba_bf", [D, 32], BF16, kind="Internal").ap()
        s.wz_bf = dt("wz_bf", [D, 4096], BF16, kind="Internal").ap()
        s.wout_bf = dt("wout_bf", [4096, D], BF16, kind="Internal").ap()
        s.wguh_d = dt("wguh", [D, 16384], BF16, kind="Internal").ap()
        s.wph_d = dt("wph", [8192, D], BF16, kind="Internal").ap()
        s.hseg = [dt(f"hseg{i}", [2048, SEG], BF16, kind="Internal").ap()
                  for i in range(NSEG)]
        s.gseg = [dt(f"gseg{i}", [4096, SEG], BF16, kind="Internal").ap()
                  for i in range(NSEG)]
        s.bigH = dt("bigH", [4096, S], BF16, kind="Internal").ap()
        s.bounce = dt("bounce", [CH, 144], BF16, kind="Internal").ap()
        if s.dev:
            s.hdump = dt("hdump", [2048, S], BF16, kind="ExternalOutput").ap()
            s.x2dump = dt("x2dump", [S // 2, D], F32, kind="ExternalOutput").ap()
            s.hrdump = dt("hrdump", [128, 64 * TB], BF16, kind="ExternalOutput").ap()
            s.xl2dump = dt("xl2dump", [128, 16 * TB], BF16, kind="ExternalOutput").ap()
            s.wgudump = dt("wgudump", [D, 16384], BF16, kind="ExternalOutput").ap()

        for name in s.count:
            s.sems[name] = s.octx.enter_context(nc.semaphore(name))
        sb = lambda n, sh, d: s.octx.enter_context(nc.sbuf_tensor(n, sh, d))
        s.psW = s.octx.enter_context(nc.psum_tensor("psW", [128, 2048], F32))
        s.psW2 = s.octx.enter_context(nc.psum_tensor("psW2", [128, 2048], F32))

        s.ident = sb("ident", [128, 128], BF16)
        s.identf = sb("identf", [128, 128], F32)
        s.uincl = sb("uincl", [128, 128], F32)
        s.onescol = sb("onescol", [128, 1], BF16)
        s.onescolf = sb("onescolf", [128, 1], F32)
        s.onesrow = sb("onesrow", [1, 128], F32)
        s.onesrowb = sb("onesrowb", [1, 128], BF16)
        s.maskL = sb("maskL", [128, 512], BF16)
        s.maskU = sb("maskU", [128, 512], BF16)
        s.maskUI = sb("maskUI", [128, 512], BF16)
        s.normw_c = sb("normw_c", [HD, 1], F32)
        s.dtbB = sb("dtbB", [128, NHV], F32)
        s.neaB = sb("neaB", [128, NHV], F32)
        s.rowstage = sb("rowstage", [65, 2048], BF16)
        s.ones65 = sb("ones65", [65, 128], BF16)
        s.tinyf = sb("tinyf", [128, 96], F32)
        s.constc = sb("constc", [128, 8], F32)

        for i, v in enumerate([LNEPS, 1.0, EPS, F32EPS]):
            nc.const_aps.register(F32, float(v), s.constc[:, i:i + 1])
        s.setup_consts()
        s.phase_a()
        if s.stop_a:
            ccw = [("cc%d" % i, s.count["cc%d" % i]) for i in range(NSEG)
                   if s.count["cc%d" % i] > 0]
            s.emit("gpsimd", lambda e: None,
                   waits=s.qall("qh") + s.qall("qb") + ccw, inc="po")
        else:
            s.phase_b()
        s.realize()
        s.octx.close()
        return nc

    def setup_consts(self):
        s, C = self, self.count

        def c_memsets(e):
            e.memset(s.ident[:], 1.0)
            e.memset(s.identf[:], 1.0)
            e.memset(s.uincl[:], 1.0)
            e.memset(s.onescol[:], 1.0)
            e.memset(s.onescolf[:], 1.0)
            e.memset(s.onesrow[:], 1.0)
            e.memset(s.onesrowb[:], 1.0)
            e.memset(s.ones65[:], 1.0)
            e.memset(s.constc[:, 0:1], LNEPS)
            e.memset(s.constc[:, 1:2], 1.0)
            e.memset(s.constc[:, 2:3], EPS)
            e.memset(s.constc[:, 3:4], F32EPS)
            e.memset(s.maskL[:], 0.0)
            e.memset(s.maskU[:], 0.0)
            e.memset(s.maskUI[:], 0.0)
            return e.engine_nop()

        def c_selects(e):
            e.affine_select(out=s.ident[:], in_=s.ident[:], compare_op=ALU.is_equal,
                            fill=0.0, base=0, pattern=[[-1, 128]], channel_multiplier=1)
            e.affine_select(out=s.identf[:], in_=s.identf[:], compare_op=ALU.is_equal,
                            fill=0.0, base=0, pattern=[[-1, 128]], channel_multiplier=1)
            e.affine_select(out=s.uincl[:], in_=s.uincl[:], compare_op=ALU.is_ge,
                            fill=0.0, base=0, pattern=[[1, 128]], channel_multiplier=-1)
            for r in range(4):
                sl = slice(r * 128, r * 128 + 128)
                e.affine_select(out=s.maskL[:, sl], in_=s.maskL[:, sl],
                                compare_op=ALU.is_ge, fill=MNEG, base=-1,
                                pattern=[[-1, 128]], channel_multiplier=1)
                e.affine_select(out=s.maskU[:, sl], in_=s.maskU[:, sl],
                                compare_op=ALU.is_ge, fill=MNEG, base=-1,
                                pattern=[[1, 128]], channel_multiplier=-1)
                e.affine_select(out=s.maskUI[:, sl], in_=s.maskUI[:, sl],
                                compare_op=ALU.is_ge, fill=MNEG, base=0,
                                pattern=[[1, 128]], channel_multiplier=-1)
            return e.engine_nop()

        s.emit("gpsimd", c_memsets, inc="po")
        s.emit("gpsimd", c_selects, waits=["po"], inc="po")
        t1 = s.dma("sync", lambda e: e.dma_start(out=s.normw_c[:], in_=s.normw_in), "qx")
        t2 = s.dma("sync", lambda e: e.dma_start(out=s.tinyf[0:1, 0:NHV], in_=s.dtb_in), "qx")
        t3 = s.dma("sync", lambda e: e.dma_start(out=s.tinyf[0:1, 16:32], in_=s.alog_in), "qx")
        s.mm(s.psW[:, 0:NHV], s.onesrow[:], s.tinyf[0:1, 0:NHV],
             waits=[t1, t2, t3, "po"])
        s.mm(s.psW[:, 128:128 + NHV], s.onesrow[:], s.tinyf[0:1, 16:32], inc="pe")
        pev = C["pe"]
        s.emit("vector", lambda e: e.tensor_copy(out=s.dtbB[:], in_=s.psW[:, 0:NHV]),
               waits=[("pe", pev)], inc="dv")
        s.emit("scalar", lambda e: e.activation(out=s.neaB[:],
                                                in_=s.psW[:, 128:128 + NHV],
                                                func=AF.Exp),
               waits=[("pe", pev)], inc="ac")
        s.emit("vector", lambda e: e.tensor_scalar_mul(out=s.neaB[:], in0=s.neaB[:],
                                                       scalar1=-1.0),
               waits=["ac"], inc="dv")

    def cvt(self, src_ap, dst_ap, cols):
        s, C = self, self.count
        i = s.cvt_i
        bu = s.wbuf[i % 2]
        w_st = [("ac", s.cvt_ac[-1])] if s.cvt_ac else []
        w_bu = [s.cvt_qh[-2]] if len(s.cvt_qh) >= 2 else []
        tl = s.dma("sync", lambda e, a=src_ap, c=cols: e.dma_start(
            out=s.wstg[:, 0:c], in_=a), "qw", waits=w_st)
        s.emit("scalar", lambda e, b=bu, c=cols: e.activation(
            out=b[:, 0:c], in_=s.wstg[:, 0:c], func=AF.Copy),
            waits=[tl] + w_bu, inc="ac")
        s.cvt_ac.append(C["ac"])
        av = C["ac"]
        ts = s.dma("gpsimd", lambda e, b=bu, d=dst_ap, c=cols: e.dma_start(
            out=d, in_=b[:, 0:c]), "qh", waits=[("ac", av)])
        s.cvt_qh.append(ts)
        s.cvt_i += 1

    def ln_tile(self, xb, lnc, xout, sq_t, tok, eps):
        s = self
        s.emit("vector", lambda e: e.reduce_sum(out=lnc[:, 0:1], in_=xb[:], axis=AX),
               waits=[tok], inc="dv")
        s.emit("scalar", lambda e: e.activation(out=sq_t[:], in_=xb[:], func=AF.Square),
               waits=[tok], inc="ac")
        s.emit("vector", lambda e: e.reduce_sum(out=lnc[:, 1:2], in_=sq_t[:], axis=AX),
               waits=["ac"], inc="dv")
        s.emit("vector", lambda e: e.tensor_scalar_mul(out=lnc[:, 2:3], in0=lnc[:, 0:1],
                                                       scalar1=1.0 / D), inc="dv")
        s.emit("vector", lambda e: e.tensor_mul(out=lnc[:, 3:4], in0=lnc[:, 2:3],
                                                in1=lnc[:, 2:3]), inc="dv")
        s.emit("vector", lambda e: e.scalar_tensor_tensor(
            out=lnc[:, 4:5], in0=lnc[:, 1:2], scalar=1.0 / D, in1=lnc[:, 3:4],
            op0=ALU.mult, op1=ALU.subtract), inc="dv")
        s.emit("scalar", lambda e: e.activation(out=lnc[:, 5:6], in_=lnc[:, 4:5],
                                                func=AF.Ln, bias=eps),
               waits=["dv"], inc="ac")
        s.emit("scalar", lambda e: e.activation(out=lnc[:, 6:7], in_=lnc[:, 5:6],
                                                func=AF.Exp, scale=-0.5), inc="ac")
        s.emit("vector", lambda e: e.scalar_tensor_tensor(
            out=lnc[:, 7:8], in0=lnc[:, 2:3], scalar=-1.0, in1=lnc[:, 6:7],
            op0=ALU.mult, op1=ALU.mult), waits=["ac"], inc="dv")
        s.emit("scalar", lambda e: e.activation(out=xout[:], in_=xb[:],
                                                func=AF.Identity,
                                                bias=lnc[:, 7:8], scale=lnc[:, 6:7]),
               waits=["dv"], inc="ac")

    def transpose16b(self, src_bf, dst, ps, evac_extra):
        s, C = self, self.count
        for cg in range(4):
            w0 = [("ac", C["ac"]), ("dv", C["dv"]), ("po", C["po"])]
            for k in range(4):
                c = cg * 4 + k
                s.mm(ps[:, k * 128:(k + 1) * 128], src_bf[:, c * 128:(c + 1) * 128],
                     s.ident[:], waits=w0, inc="pe" if k == 3 else None)
                w0 = []
            pev = C["pe"]
            for k in range(4):
                c = cg * 4 + k
                s.emit("scalar", lambda e, c=c, k=k, d=dst: e.activation(
                    out=d(c), in_=ps[:, k * 128:(k + 1) * 128], func=AF.Copy),
                    waits=[("pe", pev)] + (evac_extra if cg == 0 and k == 0 else []),
                    inc="ac")

    def transpose16(self, src_bf, dst, dst_col, ps):
        s, C = self, self.count
        for cg in range(4):
            w0 = [("ac", C["ac"]), ("dv", C["dv"]), ("po", C["po"])]
            for k in range(4):
                c = cg * 4 + k
                s.mm(ps[:, k * 128:(k + 1) * 128], src_bf[:, c * 128:(c + 1) * 128],
                     s.ident[:], waits=w0, inc="pe" if k == 3 else None)
                w0 = []
            pev = C["pe"]
            for k in range(4):
                c = cg * 4 + k
                s.emit("scalar", lambda e, c=c, k=k: e.activation(
                    out=dst(c), in_=ps[:, k * 128:(k + 1) * 128], func=AF.Copy),
                    waits=[("pe", pev)], inc="ac")

    def phase_a(self):
        s, nc, C = self, self.nc, self.count
        actx = contextlib.ExitStack()
        sa = lambda n, sh, d: actx.enter_context(nc.sbuf_tensor(n, sh, d))

        s.wstg = sa("wstg", [128, 2048], F32)
        s.wbuf = [sa(f"wbuf{i}", [128, 2048], BF16) for i in range(2)]
        s.cwcol = sa("cwcol", [128, 128], F32)
        s.xstg = sa("xstg", [128, D], F32)
        s.lncol = sa("lncol", [128, 8], F32)
        s.xlnT = sa("xlnT", [128, 16 * TA], BF16)
        s.prest = sa("prest", [128, 32 * (TA + 4)], BF16)
        s.convacc = sa("convacc", [128, TA], F32)
        s.act = sa("act", [128, 32 * TA], BF16)
        s.bast = sa("bast", [128, TA], F32)
        s.batok = sa("batok", [128, 32], F32)
        s.gcol = sa("gcol", [128, NHV], F32)
        s.bcol = sa("bcol", [128, NHV], F32)
        s.gccol = sa("gccol", [128, NHV], F32)
        s.glastb = sa("glastb", [128, NHV], F32)
        s.gamcol = sa("gamcol", [128, NHV], F32)
        s.lnsk = sa("lnsk", [128, NHQ], F32)
        s.lnsq = sa("lnsq", [128, NHQ], F32)
        s.cmt = [sa(f"cm{v}", [128, NHV], F32) for v in range(3)]
        s.r3col = sa("r3col", [128, NHV], BF16)
        s.nr2col = sa("nr2col", [128, NHV], F32)
        s.qtok = sa("qtok", [128, NHQ * 128], BF16)
        s.ktok = sa("ktok", [128, NHQ * 128], BF16)
        s.vbtok = sa("vbtok", [128, NHV * 128], BF16)
        s.sqsc = sa("sqsc", [128, NHV * 128], BF16)
        s.msqq = sa("msqq", [128, NHQ], F32)
        s.msqk = sa("msqk", [128, NHQ], F32)
        s.dec = [sa(f"dec{v}", [128, NHV * 128], BF16) for v in range(3)]
        s.Mst = sa("Mst", [128, NHV * 128], BF16)
        s.Ust = sa("Ust", [128, NHV * 128], BF16)
        s.Pw = [sa(f"Pw{i}", [128, NHV * 128], BF16) for i in range(2)]
        s.Uw = [sa(f"Uw{i}", [128, NHV * 128], BF16) for i in range(2)]
        s.sqg = sa("sqg", [128, NHV * 128], BF16)
        s.qs = sa("qs", [128, NHV * 128], BF16)
        s.zst = sa("zst", [128, NHV * 128], BF16)
        s.yst = [sa(f"yst{i}", [128, NHV * 128], BF16) for i in range(2)]
        s.oib = sa("oib", [128, NHV * 128], BF16)
        s.rstf = sa("rstf", [128, NHV * 64], F32)
        s.Sst = sa("Sst", [128, NHV * 128], F32)
        s.Sbf = sa("Sbf", [128, NHV * 128], BF16)
        s.glrow = sa("glrow", [1, NHV], F32)

        s.emit("gpsimd", lambda e: e.memset(s.Sst[:], 0.0), waits=["po"], inc="po")
        s.emit("gpsimd", lambda e: e.memset(s.Sbf[:], 0.0), inc="po")
        s.emit("gpsimd", lambda e: e.memset(s.prest[:], 0.0), inc="po")
        s.po_init = C["po"]
        s.cw_tok = s.dma("sync", lambda e: e.dma_start(
            out=s.cwcol[:].rearrange("p (t j) -> p t j", j=4),
            in_=s.convw_in.rearrange("(t p) j -> p t j", p=128)), "qx")

        s.cvt_i, s.cvt_ac, s.cvt_qh = 0, [], []
        for i in range(D // 128):
            for j in range(FQKV // 2048):
                s.cvt(s.wqkvT[i * 128:(i + 1) * 128, j * 2048:(j + 1) * 2048],
                      s.wqkv_bf[i * 128:(i + 1) * 128, j * 2048:(j + 1) * 2048], 2048)
        for i in range(D // 128):
            s.cvt(s.wbaT[i * 128:(i + 1) * 128, :], s.wba_bf[i * 128:(i + 1) * 128, :], 32)

        s.xstg_free = 0
        s.wbuf_pe = {}
        s.hseg_toks = []

        for ta in range(self.nta_lim):
            for sub in range(4):
                tok0 = ta * TA + sub * 128
                w_x = [("ac", s.xstg_free)] if s.xstg_free else []
                xt = s.dma("sync", lambda e, t=tok0: e.dma_start(
                    out=s.xstg[:], in_=s.x_in[t:t + 128, :]), "qx", waits=w_x)
                s.ln_tile(s.xstg, s.lncol, s.qs, s.sqsc[:, 0:D], xt, LNEPS)
                s.xstg_free = C["ac"]
                s.transpose16(
                    s.qs,
                    lambda c, su=sub: s.xlnT[:, c * TA + su * 128: c * TA + (su + 1) * 128],
                    0, s.psW2[:, 0:512])
            xlnT_ready = C["ac"]

            evac_half = [0, 0]
            for f in range(NF + 1):
                bu = s.wbuf[f % 2]
                w_bu = [("pe", s.wbuf_pe[f - 2])] if (f - 2) in s.wbuf_pe else []
                if ta == 0 and f < 2:
                    w_bu += [("ac", s.cvt_ac[-1]), s.cvt_qh[-1]]
                if f < NF:
                    src = s.wqkv_bf.rearrange("(ct p) f -> p ct f", p=128)[
                        :, :, f * 128:(f + 1) * 128]
                    ncol = 128
                else:
                    src = s.wba_bf.rearrange("(ct p) f -> p ct f", p=128)
                    ncol = 32
                wt = s.dma("sync", lambda e, a=src, b=bu, n=ncol: e.dma_start(
                    out=b[:].rearrange("p (ct f) -> p ct f", ct=16)[:, :, 0:n], in_=a),
                    "qw", waits=w_bu)
                half = f % 2
                ps = s.psW[:, half * 512: half * 512 + 512]
                for c in range(16):
                    w = ([wt, ("ac", max(xlnT_ready, evac_half[half]))]
                         if c == 0 else [])
                    if ta == 0 and f <= 1 and c == 0:
                        w.append(("dv", C["dv"]))
                    s.mm(ps[0:ncol, 0:TA], bu[:, c * 128:c * 128 + ncol],
                         s.xlnT[:, c * TA:(c + 1) * TA], start=(c == 0),
                         stop=(c == 15), waits=w, inc="pe" if c == 15 else None)
                s.wbuf_pe[f] = C["pe"]
                pev = C["pe"]
                if f < NF:
                    wtail = ([("dv", s.conv_tail_dv)]
                             if getattr(s, "conv_tail_dv", 0) else [])
                    s.emit("scalar", lambda e, f=f, p=ps: e.activation(
                        out=s.prest[:, f * (TA + 4) + 4: f * (TA + 4) + 4 + TA],
                        in_=p[:, 0:TA], func=AF.Copy),
                        waits=[("pe", pev)] + wtail, inc="ac")
                    evac_half[half] = C["ac"]
                else:
                    s.emit("vector", lambda e, p=ps: e.tensor_copy(
                        out=s.bast[0:32, 0:TA], in_=p[0:32, 0:TA]),
                        waits=[("pe", pev)], inc="dv")
            pre_ac = C["ac"]

            for f in range(NF if self.lvl >= 1 else 0):
                base = f * (TA + 4)
                w0 = ([("ac", pre_ac), s.cw_tok, ("dv", C["dv"]),
                       ("po", s.po_init)] if f == 0 else [])
                s.emit("scalar", lambda e, b=base, f=f: e.activation(
                    out=s.convacc[:], in_=s.prest[:, b + 1: b + 1 + TA],
                    func=AF.Identity,
                    scale=s.cwcol[:, f * 4: f * 4 + 1]), waits=w0, inc="ac")
                for j in (1, 2, 3):
                    s.emit("vector", lambda e, b=base, f=f, j=j: e.scalar_tensor_tensor(
                        out=s.convacc[:], in0=s.prest[:, b + 1 + j: b + 1 + j + TA],
                        scalar=s.cwcol[:, f * 4 + j: f * 4 + j + 1], in1=s.convacc[:],
                        op0=ALU.mult, op1=ALU.add),
                        waits=["ac"] if j == 1 else [], inc="dv")
                s.emit("scalar", lambda e, f=f: e.activation(
                    out=s.act[:, f * TA:(f + 1) * TA], in_=s.convacc[:], func=AF.Silu),
                    waits=["dv"], inc="ac")
                s.emit("vector", lambda e, b=base: e.tensor_copy(
                    out=s.prest[:, b + 1: b + 4],
                    in_=s.prest[:, b + 4 + TA - 3: b + 4 + TA]),
                    waits=["ac"], inc="dv")
            if f == NF - 1:
                s.conv_tail_dv = C["dv"]

            for sub in range(4):
                if self.lvl >= 2:
                    self.chunk(ta, sub)

        self.fence = ([(k, C[k]) for k in ("pe", "dv", "ac", "po")]
                      + s.qall("qx") + s.qall("qw") + s.qall("qh") + s.qall("qb"))
        actx.close()

    def chunk(self, ta, sub):
        s, C = self, self.count
        n = ta * 4 + sub
        off = sub * 128
        aq = lambda h: s.act[:, h * TA + off: h * TA + off + 128]
        ak = lambda h: s.act[:, (8 + h) * TA + off: (8 + h) * TA + off + 128]
        av = lambda ih: s.act[:, (16 + ih) * TA + off: (16 + ih) * TA + off + 128]
        T = s.tinyf

        s.mm(s.psW2[:, 0:32], s.bast[0:32, off:off + 128], s.identf[0:32, 0:32],
             waits=["dv", "ac"], inc="pe")
        pev = C["pe"]
        s.emit("vector", lambda e: e.tensor_copy(out=s.batok[:], in_=s.psW2[:, 0:32]),
               waits=[("pe", pev)], inc="dv")
        s.emit("scalar", lambda e: e.activation(out=T[:, 32:48], in_=s.batok[:, 0:16],
                                                func=AF.Tanh, scale=0.5),
               waits=["dv"], inc="ac")
        s.emit("vector", lambda e: e.tensor_scalar(out=s.bcol[:], in0=T[:, 32:48],
                                                   scalar1=0.5, scalar2=0.5,
                                                   op0=ALU.mult, op1=ALU.add),
               waits=["ac"], inc="dv")
        s.emit("vector", lambda e: e.tensor_add(out=T[:, 0:16], in0=s.batok[:, 16:32],
                                                in1=s.dtbB[:]), inc="dv")
        s.emit("scalar", lambda e: e.activation(out=T[:, 16:32], in_=T[:, 0:16],
                                                func=AF.Abs), waits=["dv"], inc="ac")
        s.emit("scalar", lambda e: e.activation(out=T[:, 32:48], in_=T[:, 16:32],
                                                func=AF.Exp, scale=-1.0), inc="ac")
        s.emit("scalar", lambda e: e.activation(out=T[:, 48:64], in_=T[:, 32:48],
                                                func=AF.Ln, bias=1.0), inc="ac")
        s.emit("vector", lambda e: e.tensor_scalar_max(out=T[:, 64:80], in0=T[:, 0:16],
                                                       scalar1=0.0),
               waits=["ac"], inc="dv")
        s.emit("vector", lambda e: e.tensor_add(out=T[:, 64:80], in0=T[:, 64:80],
                                                in1=T[:, 48:64]), inc="dv")
        s.emit("vector", lambda e: e.tensor_mul(out=s.gcol[:], in0=T[:, 64:80],
                                                in1=s.neaB[:]), inc="dv")
        dvv = C["dv"]
        s.mm(s.psW2[:, 64:80], s.uincl[:], s.gcol[:], waits=[("dv", dvv)])
        s.mm(s.psW2[0:1, 96:112], s.onescolf[:], s.gcol[:], inc="pe")
        pev = C["pe"]
        s.emit("vector", lambda e: e.tensor_copy(out=s.gccol[:], in_=s.psW2[:, 64:80]),
               waits=[("pe", pev)], inc="dv")
        s.emit("scalar", lambda e: e.activation(out=s.glrow[:], in_=s.psW2[0:1, 96:112],
                                                func=AF.Copy),
               waits=[("pe", pev)], inc="ac")
        s.mm(s.psW2[:, 64:80], s.onesrow[:], s.glrow[:], waits=["ac", "dv"], inc="pe")
        pev = C["pe"]
        s.emit("vector", lambda e: e.tensor_copy(out=s.glastb[:], in_=s.psW2[:, 64:80]),
               waits=[("pe", pev)], inc="dv")
        s.emit("scalar", lambda e: e.activation(out=s.gamcol[:], in_=s.glastb[:],
                                                func=AF.Exp), waits=["dv"], inc="ac")

        if s.lvl < 3:
            return
        w0 = ["ac", "dv"]
        for h in range(NHQ):
            s.mm(s.psW[:, h * 128:(h + 1) * 128], aq(h), s.ident[:], waits=w0)
            w0 = []
            s.mm(s.psW[:, 1024 + h * 128: 1024 + (h + 1) * 128], ak(h), s.ident[:],
                 inc="pe" if h == NHQ - 1 else None)
        pev = C["pe"]
        s.emit("scalar", lambda e: e.activation(out=s.qtok[:], in_=s.psW[:, 0:1024],
                                                func=AF.Copy),
               waits=[("pe", pev)], inc="ac")
        s.emit("scalar", lambda e: e.activation(out=s.ktok[:], in_=s.psW[:, 1024:2048],
                                                func=AF.Copy), inc="ac")
        acv = C["ac"]
        for ih in range(NHV):
            s.mm(s.psW[:, ih * 128:(ih + 1) * 128], av(ih), s.ident[:],
                 waits=[("ac", acv)] if ih == 0 else [],
                 inc="pe" if ih == NHV - 1 else None)
        pev = C["pe"]
        for ih in range(NHV):
            s.emit("vector", lambda e, ih=ih: e.tensor_scalar_mul(
                out=s.vbtok[:, ih * 128:(ih + 1) * 128],
                in0=s.psW[:, ih * 128:(ih + 1) * 128],
                scalar1=s.bcol[:, ih:ih + 1]),
                waits=[("pe", pev)] if ih == 0 else [], inc="dv")
        vb_dv = C["dv"]

        s.emit("scalar", lambda e: e.activation(out=s.sqsc[:, 0:1024], in_=s.qtok[:],
                                                func=AF.Square), inc="ac")
        s.emit("vector", lambda e: e.reduce_sum(
            out=s.msqq[:], in_=s.sqsc[:, 0:1024].rearrange("p (h c) -> p h c", c=128),
            axis=AX), waits=["ac"], inc="dv")
        s.emit("scalar", lambda e: e.activation(out=s.sqsc[:, 1024:2048], in_=s.ktok[:],
                                                func=AF.Square), inc="ac")
        s.emit("vector", lambda e: e.reduce_sum(
            out=s.msqk[:], in_=s.sqsc[:, 1024:2048].rearrange("p (h c) -> p h c", c=128),
            axis=AX), waits=["ac"], inc="dv")
        s.emit("scalar", lambda e: e.activation(out=T[:, 0:8], in_=s.msqk[:], func=AF.Ln,
                                                scale=1.0 / 128, bias=EPS),
               waits=["dv"], inc="ac")
        s.emit("scalar", lambda e: e.activation(out=T[:, 8:16], in_=s.msqq[:],
                                                func=AF.Ln, scale=1.0 / 128, bias=EPS),
               inc="ac")
        s.emit("vector", lambda e: e.tensor_scalar(out=s.lnsk[:], in0=T[:, 0:8],
                                                   scalar1=-0.5, scalar2=-0.5 * LN128,
                                                   op0=ALU.mult, op1=ALU.add),
               waits=["ac"], inc="dv")
        s.emit("vector", lambda e: e.tensor_scalar(out=s.lnsq[:], in0=T[:, 8:16],
                                                   scalar1=-0.5, scalar2=-LN128,
                                                   op0=ALU.mult, op1=ALU.add), inc="dv")
        lnskd = s.coldup(s.lnsk)
        lnsqd = s.coldup(s.lnsq)
        rr2 = lambda x: x.rearrange("p (h d) -> p h d", d=2)
        s.emit("scalar", lambda e: e.activation(out=T[:, 16:32], in_=s.bcol[:],
                                                func=AF.Ln), waits=["dv"], inc="ac")
        s.emit("vector", lambda e: e.tensor_add(out=s.cmt[0][:], in0=s.gccol[:],
                                                in1=T[:, 16:32]), waits=["ac"], inc="dv")
        s.emit("vector", lambda e, a=lnskd: e.tensor_tensor(
            out=rr2(s.cmt[0][:]), in0=rr2(s.cmt[0][:]), in1=a, op=ALU.add), inc="dv")
        s.emit("vector", lambda e: e.tensor_scalar_max(out=s.cmt[0][:], in0=s.cmt[0][:],
                                                       scalar1=0.8 * MNEG), inc="dv")
        s.emit("vector", lambda e, a=lnskd: e.scalar_tensor_tensor(
            out=rr2(s.cmt[1][:]), in0=rr2(s.gccol[:]), scalar=-1.0, in1=a,
            op0=ALU.mult, op1=ALU.add), inc="dv")
        s.emit("vector", lambda e, a=lnsqd: e.tensor_tensor(
            out=rr2(s.cmt[2][:]), in0=rr2(s.gccol[:]), in1=a, op=ALU.add), inc="dv")
        s.emit("scalar", lambda e: e.activation(out=s.nr2col[:], in_=s.cmt[0][:],
                                                func=AF.Exp), waits=["dv"], inc="ac")
        s.emit("vector", lambda e: e.tensor_scalar_mul(out=s.nr2col[:], in0=s.nr2col[:],
                                                       scalar1=-1.0),
               waits=["ac"], inc="dv")
        s.emit("vector", lambda e: e.tensor_add(out=T[:, 32:48], in0=s.cmt[1][:],
                                                in1=s.glastb[:]), inc="dv")
        s.emit("scalar", lambda e: e.activation(out=s.r3col[:], in_=T[:, 32:48],
                                                func=AF.Exp), waits=["dv"], inc="ac")

        if s.lvl < 4:
            return

        masks = [s.maskL, s.maskU, s.maskUI]
        colv = [s.cmt[0], s.cmt[1], s.cmt[1]]
        rowv = [s.cmt[1], s.cmt[0], s.cmt[2]]
        for v in range(3):
            pst = s.psW if v < 2 else s.psW2
            for half in range(2):
                w0 = [("dv", C["dv"]), ("ac", C["ac"])]
                for hh in range(8):
                    h = half * 8 + hh
                    s.mm(s.psW2[0:1, 1024 + hh * 128: 1024 + (hh + 1) * 128],
                         rowv[v][:, h:h + 1], s.identf[:], waits=w0,
                         inc="pe" if hh == 7 else None)
                    w0 = []
                pev = C["pe"]
                s.emit("scalar", lambda e: e.activation(
                    out=s.rstf[0:1, 0:1024], in_=s.psW2[0:1, 1024:2048],
                    func=AF.Copy), waits=[("pe", pev)], inc="ac")
                acv = C["ac"]
                for j in range(2):
                    s.mm(pst[:, half * 1024 + j * 512: half * 1024 + (j + 1) * 512],
                         s.onesrow[:], s.rstf[0:1, j * 512:(j + 1) * 512],
                         start=True, stop=(v == 2),
                         waits=[("ac", acv)] if j == 0 else [],
                         inc="pe" if j == 1 else None)
            if v == 2:
                pev = C["pe"]
                s.emit("scalar", lambda e: e.activation(out=s.sqg[:], in_=s.psW2[:],
                                                        func=AF.Exp),
                       waits=[("pe", pev)], inc="ac")
                mAP = s.bc_ap(s.maskUI, 0, 128, 0, [[0, NHV], [1, 128]])
                s.emit("vector", lambda e, a=mAP: e.tensor_tensor(
                    out=s.r3(s.psW2[:]), in0=s.r3(s.psW2[:]), in1=a, op=ALU.add),
                    waits=["ac"], inc="dv")
            else:
                for j in range(4):
                    s.mm(pst[:, j * 512:(j + 1) * 512], s.ident[:], masks[v][:],
                         start=False, stop=True,
                         inc="pe" if j == 3 else None)
            pev = C["pe"]
            cd = s.colbc(colv[v], NHV, 128)
            s.emit("vector", lambda e, a=cd, p=pst: e.tensor_tensor(
                out=s.r3(p[:]), in0=s.r3(p[:]), in1=a, op=ALU.add),
                waits=[("pe", pev)], inc="dv")
            s.emit("scalar", lambda e, v=v, p=pst: e.activation(
                out=s.dec[v][:], in_=p[:], func=AF.Exp),
                   waits=["dv"], inc="ac")
        dec_ac = C["ac"]

        if s.lvl < 5:
            return
        w0 = [("ac", dec_ac)]
        for h in range(NHQ):
            s.mm(s.psW2[:, h * 128:(h + 1) * 128], ak(h), ak(h), waits=w0)
            w0 = []
            s.mm(s.psW2[:, 1024 + h * 128: 1024 + (h + 1) * 128], ak(h), aq(h),
                 inc="pe" if h == NHQ - 1 else None)
        pev = C["pe"]
        kkd = s.dupq4(s.psW2, 0, 128)
        qkd = s.dupq4(s.psW2, 1024, 128)
        s.emit("vector", lambda e, a=kkd: e.tensor_tensor(
            out=s.r4(s.Mst[:]), in0=a, in1=s.r4(s.dec[0][:]), op=ALU.mult),
            waits=[("pe", pev)], inc="dv")
        s.emit("vector", lambda e, a=kkd: e.tensor_tensor(
            out=s.r4(s.Ust[:]), in0=a, in1=s.r4(s.dec[1][:]), op=ALU.mult), inc="dv")
        s.emit("vector", lambda e, a=qkd: e.tensor_tensor(
            out=s.r4(s.sqsc[:]), in0=a, in1=s.r4(s.dec[2][:]), op=ALU.mult), inc="dv")
        build_dv = C["dv"]

        if s.lvl < 6:
            return
        w0 = [("dv", build_dv), ("ac", dec_ac), ("po", s.po_init)]
        for ih in range(NHV):
            s.mm(s.psW[:, ih * 128:(ih + 1) * 128], ak(ih // 2),
                 s.Sbf[:, ih * 128:(ih + 1) * 128], waits=w0,
                 inc="pe" if ih == NHV - 1 else None)
            w0 = []
        pev = C["pe"]
        nr2d = s.colbc(s.nr2col, NHV, 128)
        s.emit("vector", lambda e, a=nr2d: e.tensor_tensor(
            out=s.r3(s.yst[0][:]), in0=s.r3(s.psW[:]), in1=a, op=ALU.mult),
            waits=[("pe", pev), ("dv", vb_dv)], inc="dv")
        s.emit("vector", lambda e: e.tensor_add(out=s.zst[:], in0=s.yst[0][:],
                                                in1=s.vbtok[:]), inc="dv")
        z_dv = C["dv"]

        if s.lvl < 7:
            return
        w0 = [("dv", z_dv)]
        for ih in range(NHV):
            sl = slice(ih * 128, (ih + 1) * 128)
            s.mm(s.psW[:, sl], s.Ust[:, sl], s.zst[:, sl], waits=w0,
                 inc="pe" if ih == NHV - 1 else None)
            w0 = []
        pev = C["pe"]
        s.emit("vector", lambda e: e.scalar_tensor_tensor(
            out=s.yst[0][:], in0=s.psW[:], scalar=-1.0, in1=s.zst[:],
            op0=ALU.mult, op1=ALU.add), waits=[("pe", pev)], inc="dv")
        y_dv = C["dv"]
        Mp, Up = s.Mst, s.Ust
        ycur, pp = 0, 0
        for lev in range(1, 7):
            w_m2 = [("ac", C["ac"]), ("dv", C["dv"])]
            for ih in range(NHV):
                sl = slice(ih * 128, (ih + 1) * 128)
                s.mm(s.psW2[:, sl], Mp[:, sl], Up[:, sl], waits=w_m2,
                     inc="pe" if ih == NHV - 1 else None)
                w_m2 = []
            pev = C["pe"]
            s.emit("scalar", lambda e, p=pp: e.activation(out=s.Uw[p][:], in_=s.psW2[:],
                                                          func=AF.Copy),
                   waits=[("pe", pev)], inc="ac")
            if lev < 6:
                w_u2 = [("dv", y_dv)]
                for ih in range(NHV):
                    sl = slice(ih * 128, (ih + 1) * 128)
                    s.mm(s.psW[:, sl], Up[:, sl], Mp[:, sl], waits=w_u2,
                         inc="pe" if ih == NHV - 1 else None)
                    w_u2 = []
                pev = C["pe"]
                s.emit("scalar", lambda e, p=pp: e.activation(out=s.Pw[p][:],
                                                              in_=s.psW[:],
                                                              func=AF.Copy),
                       waits=[("pe", pev)], inc="ac")
            m2_ac = C["ac"]
            w_ap = [("ac", m2_ac)]
            for ih in range(NHV):
                sl = slice(ih * 128, (ih + 1) * 128)
                s.mm(s.psW[:, sl], s.Uw[pp][:, sl], s.yst[ycur][:, sl], waits=w_ap,
                     inc="pe" if ih == NHV - 1 else None)
                w_ap = []
            pev = C["pe"]
            s.emit("vector", lambda e, a=ycur: e.tensor_add(
                out=s.yst[1 - a][:], in0=s.psW[:], in1=s.yst[a][:]),
                waits=[("pe", pev)], inc="dv")
            y_dv = C["dv"]
            ycur = 1 - ycur
            Mp, Up = s.Pw[pp], s.Uw[pp]
            pp = 1 - pp
        vnew = s.yst[ycur]

        if s.lvl < 8:
            return
        qdup = s.dupq4(s.act, off, TA)
        s.emit("vector", lambda e, a=qdup: e.tensor_tensor(
            out=s.r4(s.qs[:]), in0=a, in1=s.r4(s.sqg[:]), op=ALU.mult),
            waits=["ac"], inc="dv")
        qs_dv = C["dv"]
        w0 = [("dv", qs_dv)]
        for ih in range(NHV):
            sl = slice(ih * 128, (ih + 1) * 128)
            s.mm(s.psW[:, sl], s.Sbf[:, sl], s.qs[:, sl], start=True, stop=False,
                 waits=w0)
            w0 = []
            s.mm(s.psW[:, sl], vnew[:, sl], s.sqsc[:, sl], start=False, stop=True,
                 inc="pe" if ih == NHV - 1 else None)
        pev = C["pe"]
        whs = [s.hseg_toks[-1]] if s.hseg_toks else []
        s.emit("scalar", lambda e: e.activation(out=s.oib[:], in_=s.psW[:],
                                                func=AF.Copy),
               waits=[("pe", pev)] + whs + s.qall("qh")[-1:], inc="ac")
        s.emit("vector", lambda e: None, waits=["ac"], inc="dv")
        oi_dv = C["dv"]

        if s.lvl < 9:
            return
        s.emit("scalar", lambda e: e.activation(out=s.sqg[:], in_=s.oib[:],
                                                func=AF.Square),
               waits=[("dv", oi_dv)], inc="ac")
        acv = C["ac"]
        for j in range(4):
            s.mm(s.psW2[0:1, j * 512:(j + 1) * 512], s.onescol[:],
                 s.sqg[:, j * 512:(j + 1) * 512],
                 waits=[("ac", acv), ("dv", C["dv"])] if j == 0 else [],
                 inc="pe" if j == 3 else None)
        pev = C["pe"]
        s.emit("scalar", lambda e: e.activation(out=s.rowstage[0:1, :],
                                                in_=s.psW2[0:1, 0:2048], func=AF.Copy),
               waits=[("pe", pev)], inc="ac")
        acv = C["ac"]
        for j in range(4):
            s.mm(s.psW[:, j * 512:(j + 1) * 512], s.onesrowb[:],
                 s.rowstage[0:1, j * 512:(j + 1) * 512],
                 waits=[("ac", acv), ("dv", oi_dv)] if j == 0 else [],
                 inc="pe" if j == 3 else None)
        pev = C["pe"]
        for jh in range(2):
            s.emit("scalar", lambda e, j=jh: e.activation(
                out=s.rstf[:], in_=s.psW[:, j * 1024:(j + 1) * 1024], func=AF.Ln,
                scale=1.0 / 128, bias=EPS),
                waits=[("pe", pev)] if jh == 0 else [], inc="ac")
            s.emit("scalar", lambda e, j=jh: e.activation(
                out=s.sqg[:, j * 1024:(j + 1) * 1024], in_=s.rstf[:],
                func=AF.Exp, scale=-0.5), inc="ac")
        s.emit("vector", lambda e: e.scalar_tensor_tensor(
            out=s.oib[:], in0=s.oib[:], scalar=s.normw_c[:], in1=s.sqg[:],
            op0=ALU.mult, op1=ALU.mult), waits=["ac"], inc="dv")
        h_dv = C["dv"]
        seg, col0 = n // 8, (n % 8) * 128
        s.hseg_toks.append(s.dma("gpsimd", lambda e, g=seg, c=col0: e.dma_start(
            out=s.hseg[g][:, c:c + 128].rearrange("(h e) t -> e h t", e=128),
            in_=s.oib[:].rearrange("p (h t) -> p h t", t=128)),
            "qh", waits=[("dv", h_dv)]))
        if s.dev:
            s.dma("gpsimd", lambda e, c=n * 128: e.dma_start(
                out=s.hdump[:, c:c + 128].rearrange("(h e) t -> e h t", e=128),
                in_=s.oib[:].rearrange("p (h t) -> p h t", t=128)), "qh")
        if n % 8 == 7:
            toks = s.hseg_toks[-8:]
            uniq = {}
            for sem, vv in toks:
                uniq[sem] = max(uniq.get(sem, 0), vv)
            s.emit("gpsimd", lambda e, g=seg: e.collective_compute(
                "AllGather", ALU.bypass,
                replica_groups=[[0, 1], [2, 3], [4, 5], [6, 7]],
                ins=[s.hseg[g]], outs=[s.gseg[g]]),
                waits=sorted(uniq.items()), inc="cc%d" % seg)

        if s.lvl < 10:
            return
        r3d = s.colbc(s.r3col, NHV, 128)
        vn3 = s.yst[1 - ycur]
        s.emit("vector", lambda e, a=r3d, vn=vnew, o=vn3: e.tensor_tensor(
            out=s.r3(o[:]), in0=s.r3(vn[:]), in1=a, op=ALU.mult), inc="dv")
        v3_dv = C["dv"]
        w0 = [("dv", v3_dv), ("ac", C["ac"])]
        for ih in range(NHV):
            sl = slice(ih * 128, (ih + 1) * 128)
            s.mm(s.psW2[:, sl], s.ktok[:, (ih // 2) * 128:(ih // 2) * 128 + 128],
                 vn3[:, sl], waits=w0, inc="pe" if ih == NHV - 1 else None)
            w0 = []
        pev = C["pe"]
        gamd = s.colbc(s.gamcol, NHV, 128)
        s.emit("vector", lambda e, a=gamd: e.tensor_tensor(
            out=s.r3(s.Sst[:]), in0=s.r3(s.Sst[:]), in1=a, op=ALU.mult), inc="dv")
        s.emit("vector", lambda e: e.tensor_add(out=s.Sst[:], in0=s.Sst[:],
                                                in1=s.psW2[:]),
               waits=[("pe", pev)], inc="dv")
        s.emit("vector", lambda e: e.tensor_copy(out=s.Sbf[:], in_=s.Sst[:]), inc="dv")

    def phase_b(self):
        s, nc, C = self, self.nc, self.count
        fence = list(self.fence)
        bctx = contextlib.ExitStack()
        sbb = lambda n, sh, d: bctx.enter_context(nc.sbuf_tensor(n, sh, d))

        xstgB = [sbb(f"xstgB{i}", [128, 2048], F32) for i in range(4)]
        lncB = sbb("lncB", [128, 8], F32)
        xln2T = sbb("xln2T", [128, 16 * TB], BF16)
        xlnbB = sbb("xlnbB", [128, 2048], BF16)
        sqBB = sbb("sqBB", [128, 2048], BF16)
        wbufB = [sbb(f"wbufB{i}", [128, 2048], BF16) for i in range(2)]
        wbufB2 = [sbb(f"wbufB2{i}", [128, 2048], BF16) for i in range(2)]
        aofb = sbb("aofb", [128, 512], BF16)
        rstdc = sbb("rstdc", [128, 8], F32)
        outst = [sbb(f"outst{i}", [128, 128], F32) for i in range(2)]
        hoffs = sbb("hoffs", [1, 1], U32)
        wob = sbb("wob", [128, 32 * 256], BF16)

        g1 = contextlib.ExitStack()
        htil = g1.enter_context(nc.sbuf_tensor("htil", [128, 32 * TB], BF16))
        szt = g1.enter_context(nc.sbuf_tensor("szt", [128, 32 * TB], BF16))
        xln1T = g1.enter_context(nc.sbuf_tensor("xln1T", [128, 16 * TB], BF16))
        g1.close()
        qctx = contextlib.ExitStack()
        wstgB = qctx.enter_context(nc.sbuf_tensor("wstgB", [128, 2048], F32))
        wqB = qctx.enter_context(nc.sbuf_tensor("wqB", [128, 2048], BF16))
        wq2B = qctx.enter_context(nc.sbuf_tensor("wq2B", [128, 2048], BF16))
        qscB = qctx.enter_context(nc.sbuf_tensor("qscB", [128, 96], F32))
        qscBh = qctx.enter_context(nc.sbuf_tensor("qscBh", [128, 32], BF16))
        wtT = qctx.enter_context(nc.sbuf_tensor("wtT", [128, 2048], BF16))

        hoff_q = s.dma("sync", lambda e: e.dma_start(out=hoffs[:], in_=s.hoff_in),
                       "qx", waits=fence)
        s.emit("scalar", lambda e: None, waits=fence, inc="ac")
        s.emit("gpsimd", lambda e: None, waits=fence, inc="po")
        s.emit("vector", lambda e: None, waits=fence, inc="dv")
        s.emit("tensor", lambda e: None, waits=fence, inc="pe")

        s.cvt_i, s.cvt_ac, s.cvt_qh = 0, [C["ac"]], []
        s.wstg = wstgB
        s.wbuf = wbufB
        for i in range(D // 128):
            for j in range(2):
                s.cvt(s.wzT_in[i * 128:(i + 1) * 128, j * 2048:(j + 1) * 2048],
                      s.wz_bf[i * 128:(i + 1) * 128, j * 2048:(j + 1) * 2048], 2048)
        for i in range(4096 // 128):
            s.cvt(s.woutT_in[i * 128:(i + 1) * 128, :],
                  s.wout_bf[i * 128:(i + 1) * 128, :], 2048)

        grp = lambda x: x.rearrange("p (g c) -> p g c", c=64)
        gAP = lambda base: s.bc_ap(qscB, 0, 128, base, [[1, 32], [0, 64]])
        gAPh = lambda: s.bc_ap(qscBh, 0, 128, 0, [[1, 32], [0, 64]])

        s.qt_pe = 0
        s.qt_store = None
        def quant_tile(src_ap, dst_ap):
            w_st = [("ac", s.cvt_ac[-1])]
            tl = s.dma("sync", lambda e, a=src_ap: e.dma_start(out=wstgB[:], in_=a),
                       "qw", waits=w_st)
            s.emit("scalar", lambda e: e.activation(out=wqB[:], in_=wstgB[:],
                                                    func=AF.Copy),
                   waits=[tl, ("dv", C["dv"])] + s.qall("qh"), inc="ac")
            s.cvt_ac.append(C["ac"])
            s.emit("vector", lambda e: e.reduce_sum(
                out=qscB[:, 0:32], in_=grp(wqB[:]), axis=AX,
                apply_absolute_value=True), waits=["ac"], inc="dv")
            s.emit("vector", lambda e: e.tensor_scalar(
                out=qscB[:, 32:64], in0=qscB[:, 0:32], scalar1=1.0 / 64, scalar2=1e-8,
                op0=ALU.mult, op1=ALU.max), inc="dv")
            s.emit("vector", lambda e: e.tensor_copy(out=qscBh[:],
                                                     in_=qscB[:, 32:64]), inc="dv")
            s.emit("vector", lambda e: e.tensor_copy(out=qscB[:, 32:64],
                                                     in_=qscBh[:]), inc="dv")
            s.emit("vector", lambda e: e.tensor_scalar_mul(
                out=qscB[:, 0:32], in0=qscB[:, 32:64], scalar1=0.5), inc="dv")
            s.emit("vector", lambda e: e.tensor_scalar_mul(
                out=qscB[:, 64:96], in0=qscB[:, 32:64], scalar1=-0.5), inc="dv")
            s.emit("vector", lambda e, a=gAP(0): e.tensor_tensor(
                out=grp(wq2B[:]), in0=grp(wqB[:]), in1=a, op=ALU.is_gt),
                waits=[("pe", s.qt_pe)], inc="dv")
            s.emit("vector", lambda e, a=gAP(64): e.tensor_tensor(
                out=grp(sqBB[:]), in0=grp(wqB[:]), in1=a, op=ALU.is_lt), inc="dv")
            s.emit("vector", lambda e: e.tensor_sub(out=wq2B[:], in0=wq2B[:],
                                                    in1=sqBB[:]), inc="dv")
            s.emit("vector", lambda e, a=gAP(32): e.tensor_tensor(
                out=grp(wq2B[:]), in0=grp(wq2B[:]), in1=a, op=ALU.mult), inc="dv")
            s.qt_wait_store = [s.qt_store] if s.qt_store else []
            s.transpose16b(wq2B, lambda c: wtT[:, c * 128:(c + 1) * 128],
                           s.psW[:, 0:512], s.qt_wait_store)
            s.qt_pe = C["pe"]
            acv = C["ac"]
            s.qt_store = s.dma("gpsimd", lambda e, d=dst_ap: e.dma_start(
                out=d, in_=wtT[:].rearrange("p (ct f) -> p ct f", f=128)),
                  "qh", waits=[("ac", acv)])

        for rt in range(16384 // 128):
            quant_tile(s.wgu_in[rt * 128:(rt + 1) * 128, :],
                       s.wguh_d[:, rt * 128:(rt + 1) * 128].rearrange(
                           "(ct p) f -> p ct f", p=128))
        for rt in range(D // 128):
            for q4 in range(4):
                quant_tile(
                    s.wp_in[rt * 128:(rt + 1) * 128, q4 * 2048:(q4 + 1) * 2048],
                    s.wph_d[q4 * 2048:(q4 + 1) * 2048,
                            rt * 128:(rt + 1) * 128].rearrange(
                                "(ct p) f -> p ct f", p=128))
        quant_done = ([("ac", C["ac"]), ("dv", C["dv"])] + s.qall("qh")
                      + s.qall("qw"))
        if s.dev:
            s.dma("gpsimd", lambda e: e.dma_start(out=s.wgudump, in_=s.wguh_d),
                  "qb", waits=quant_done)
        qctx.close()
        hres = bctx.enter_context(nc.sbuf_tensor("hres", [128, 64 * TB], BF16))
        wpb = bctx.enter_context(nc.sbuf_tensor("wpb", [128, 64 * 128], BF16))

        repack_q = []
        for sg in range(NSEG):
            repack_q.append(s.dma("gpsimd", lambda e, g=sg: e.dma_start(
                out=s.bigH[:, g * SEG:(g + 1) * SEG], in_=s.gseg[g]),
                "qb", waits=[("cc%d" % sg, 1)]))

        s.out_toks = []
        hload_pe = 0
        for bt in range(NTB):
            tok0 = bt * TB
            def hload(e, b=bt):
                v = e.value_load(hoffs[0:1, 0:1])
                src2 = s.bigH.rearrange("(ct p) t -> p ct t", p=128)
                return e.dma_start(
                    out=htil[:].rearrange("p (ct t) -> p ct t", t=TB),
                    in_=src2[:, :, bass.ts(v + b, TB)])
            h_q = s.dma("sync", hload,
                        "qx", waits=[hoff_q, ("pe", hload_pe)] + repack_q
                        + (quant_done if bt == 0 else []))

            for sub in range(4):
                t0 = tok0 + sub * 128
                xb = xstgB[sub]
                w_x = [("dv", C["dv"])]
                xt = s.dma("sync", lambda e, t=t0, x=xb: e.dma_start(
                    out=x[:], in_=s.xh_in[t:t + 128, :]), "qx", waits=w_x)
                s.ln_tile(xb, lncB, xlnbB, sqBB, xt, LNEPS)
                s.transpose16(
                    xlnbB,
                    lambda c, su=sub: xln1T[:, c * TB + su * 128: c * TB + (su + 1) * 128],
                    0, s.psW2[:, 0:512])
            x1_ac = C["ac"]

            zf_pe = {}
            for zf in range(32):
                bu = wbufB[zf % 2]
                w_bu = [("pe", zf_pe[zf - 2])] if (zf - 2) in zf_pe else []
                src = s.wz_bf.rearrange("(ct p) f -> p ct f", p=128)[
                    :, :, zf * 128:(zf + 1) * 128]
                wt = s.dma("sync", lambda e, a=src, b=bu: e.dma_start(
                    out=b[:, 0:2048].rearrange("p (ct f) -> p ct f", ct=16), in_=a),
                    "qw", waits=w_bu)
                half = zf % 2
                ps = s.psW[:, half * 512: half * 512 + 512]
                for c in range(16):
                    w = [wt, ("ac", C["ac"]), ("dv", C["dv"])] if c == 0 else []
                    s.mm(ps[:, 0:TB], bu[:, c * 128:(c + 1) * 128],
                         xln1T[:, c * TB:(c + 1) * TB], start=(c == 0), stop=(c == 15),
                         waits=w, inc="pe" if c == 15 else None)
                zf_pe[zf] = C["pe"]
                pev = C["pe"]
                s.emit("scalar", lambda e, z=zf, p=ps: e.activation(
                    out=szt[:, z * TB:(z + 1) * TB], in_=p[:, 0:TB], func=AF.Silu),
                    waits=[("pe", pev)], inc="ac")
            sz_ac = C["ac"]
            s.emit("vector", lambda e: e.tensor_mul(out=htil[:], in0=htil[:],
                                                    in1=szt[:]),
                   waits=[("ac", sz_ac), h_q], inc="dv")
            gate_dv = C["dv"]

            for dq in range(8):
                src = s.wout_bf.rearrange("(ft p) d -> p ft d", p=128)[
                    :, :, dq * 256:(dq + 1) * 256]
                wt = s.dma("sync", lambda e, a=src: e.dma_start(
                    out=wob[:].rearrange("p (ft d) -> p ft d", d=256), in_=a),
                    "qw", waits=[("pe", C["pe"])])
                for dt in range(2):
                    ps = s.psW[:, dt * 512:(dt + 1) * 512]
                    for f in range(32):
                        w = ([wt, ("dv", gate_dv), ("ac", C["ac"])]
                             if f == 0 else [])
                        s.mm(ps[:, 0:TB], wob[:, f * 256 + dt * 128: f * 256 + dt * 128 + 128],
                             htil[:, f * TB:(f + 1) * TB], start=(f == 0),
                             stop=(f == 31), waits=w, inc="pe" if f == 31 else None)
                    pev = C["pe"]
                    s.emit("scalar", lambda e, p=ps: e.activation(
                        out=aofb[:], in_=p[:, 0:TB], func=AF.Copy),
                        waits=[("pe", pev)], inc="ac")
                    acv = C["ac"]
                    dcol = dq * 256 + dt * 128
                    for st in range(4):
                        s.mm(s.psW2[:, st * 128:(st + 1) * 128],
                             aofb[:, st * 128:(st + 1) * 128], s.ident[:],
                             waits=[("ac", acv), ("dv", C["dv"])] if st == 0 else [],
                             inc="pe" if st == 3 else None)
                    pev = C["pe"]
                    for st in range(4):
                        s.emit("vector", lambda e, st=st, dc=dcol: e.tensor_add(
                            out=xstgB[st][:, dc:dc + 128],
                            in0=s.psW2[:, st * 128:(st + 1) * 128],
                            in1=xstgB[st][:, dc:dc + 128]),
                            waits=[("pe", pev)] if st == 0 else [], inc="dv")
            res_dv = C["dv"]
            if s.dev:
                for st in range(4):
                    s.dma("gpsimd", lambda e, st=st, t=tok0: e.dma_start(
                        out=s.x2dump[t + st * 128: t + (st + 1) * 128, :],
                        in_=xstgB[st][:]), "qh", waits=[("dv", res_dv)])

            for sub in range(4):
                s.ln_tile2(xstgB[sub], lncB, xlnbB, sqBB, res_dv, LNEPS)
                s.transpose16(
                    xlnbB,
                    lambda c, su=sub: xln2T[:, c * TB + su * 128: c * TB + (su + 1) * 128],
                    0, s.psW2[:, 0:512])
            x2_ac = C["ac"]

            for gi in range(64):
                bg = wbufB[gi % 2]
                bu2 = wbufB2[gi % 2]
                w_bu = [("pe", zf_pe[("g", gi - 2)])] if ("g", gi - 2) in zf_pe else []
                srcg = s.wguh_d.rearrange("(ct p) f -> p ct f", p=128)[
                    :, :, gi * 128:(gi + 1) * 128]
                srcu = s.wguh_d.rearrange("(ct p) f -> p ct f", p=128)[
                    :, :, 8192 + gi * 128: 8192 + (gi + 1) * 128]
                wt1 = s.dma("sync", lambda e, a=srcg, b=bg: e.dma_start(
                    out=b[:, 0:2048].rearrange("p (ct f) -> p ct f", ct=16), in_=a),
                    "qw", waits=w_bu)
                wt2 = s.dma("sync", lambda e, a=srcu, b=bu2: e.dma_start(
                    out=b[:, 0:2048].rearrange("p (ct f) -> p ct f", ct=16), in_=a),
                    "qw")
                for c in range(16):
                    w = [wt1, wt2, ("ac", max(x2_ac, C["ac"])),
                         ("dv", C["dv"])] if c == 0 else []
                    s.mm(s.psW[:, 0:TB], bg[:, c * 128:(c + 1) * 128],
                         xln2T[:, c * TB:(c + 1) * TB], start=(c == 0), stop=(c == 15),
                         waits=w)
                    s.mm(s.psW[:, 512:512 + TB], bu2[:, c * 128:(c + 1) * 128],
                         xln2T[:, c * TB:(c + 1) * TB], start=(c == 0), stop=(c == 15),
                         inc="pe" if c == 15 else None)
                zf_pe[("g", gi)] = C["pe"]
                pev = C["pe"]
                s.emit("scalar", lambda e: e.activation(out=aofb[:], in_=s.psW[:, 0:512],
                                                        func=AF.Silu),
                       waits=[("pe", pev), ("dv", C["dv"])], inc="ac")
                s.emit("vector", lambda e, g=gi: e.tensor_mul(
                    out=hres[:, g * TB:(g + 1) * TB], in0=aofb[:],
                    in1=s.psW[:, 512:1024]), waits=["ac"], inc="dv")
                s.emit("scalar", lambda e, g=gi: e.activation(
                    out=sqBB[:, 0:512], in_=hres[:, g * TB:(g + 1) * TB],
                    func=AF.Square), waits=["dv", ("pe", C["pe"])], inc="ac")
                s.mm(s.psW2[0:1, 0:512], s.onescol[:], sqBB[:, 0:512],
                     start=(gi == 0), stop=(gi == 63), waits=[("ac", C["ac"])],
                     inc="pe" if gi == 63 else None)
            pev = C["pe"]
            s.emit("scalar", lambda e: e.activation(out=s.rowstage[0:1, 0:512],
                                                    in_=s.psW2[0:1, 0:512],
                                                    func=AF.Copy),
                   waits=[("pe", pev)], inc="ac")
            acv = C["ac"]
            for st in range(4):
                s.mm(s.psW2[:, 512 + st: 513 + st],
                     s.rowstage[0:1, st * 128:(st + 1) * 128], s.onesrowb[0:1, 0:1],
                     waits=[("ac", acv)] if st == 0 else [],
                     inc="pe" if st == 3 else None)
            pev = C["pe"]
            s.emit("scalar", lambda e: e.activation(out=rstdc[:, 0:4],
                                                    in_=s.psW2[:, 512:516], func=AF.Ln,
                                                    scale=1.0 / 8192, bias=F32EPS),
                   waits=[("pe", pev)], inc="ac")
            s.emit("scalar", lambda e: e.activation(out=rstdc[:, 4:8],
                                                    in_=rstdc[:, 0:4], func=AF.Exp,
                                                    scale=-0.5), inc="ac")
            rstd_ac = C["ac"]

            if s.dev and bt == 0:
                s.dma("gpsimd", lambda e: e.dma_start(out=s.hrdump, in_=hres[:]),
                      "qh", waits=[("dv", C["dv"]), ("ac", C["ac"])])
                s.dma("gpsimd", lambda e: e.dma_start(out=s.xl2dump, in_=xln2T[:]),
                      "qh")
            for dt16 in range(16):
                src = s.wph_d.rearrange("(ht p) d -> p ht d", p=128)[
                    :, :, dt16 * 128:(dt16 + 1) * 128]
                wt = s.dma("sync", lambda e, a=src: e.dma_start(
                    out=wpb[:].rearrange("p (ht f) -> p ht f", ht=64), in_=a),
                    "qw", waits=[("pe", C["pe"])])
                for hf in range(64):
                    w = [wt, ("ac", rstd_ac)] if hf == 0 else []
                    s.mm(s.psW[:, 0:TB], wpb[:, hf * 128:(hf + 1) * 128],
                         hres[:, hf * TB:(hf + 1) * TB], start=(hf == 0),
                         stop=(hf == 63), waits=w, inc="pe" if hf == 63 else None)
                pev = C["pe"]
                s.emit("scalar", lambda e: e.activation(out=aofb[:], in_=s.psW[:, 0:512],
                                                        func=AF.Copy),
                       waits=[("pe", pev)], inc="ac")
                acv = C["ac"]
                for st in range(4):
                    s.mm(s.psW2[:, 1024 + st * 128: 1024 + (st + 1) * 128],
                         aofb[:, st * 128:(st + 1) * 128], s.ident[:],
                         waits=[("ac", acv), ("dv", C["dv"])] if st == 0 else [],
                         inc="pe" if st == 3 else None)
                pev = C["pe"]
                for st in range(4):
                    ob = outst[st % 2]
                    w = [("pe", pev)]
                    if len(s.out_toks) >= 2:
                        w.append(s.out_toks[-2])
                    s.emit("vector", lambda e, st=st, d=dt16, o=ob: e.scalar_tensor_tensor(
                        out=o[:], in0=s.psW2[:, 1024 + st * 128: 1024 + (st + 1) * 128],
                        scalar=rstdc[:, 4 + st:5 + st],
                        in1=xstgB[st][:, d * 128:(d + 1) * 128],
                        op0=ALU.mult, op1=ALU.add), waits=w, inc="dv")
                    dvv = C["dv"]
                    s.out_toks.append(s.dma(
                        "gpsimd", lambda e, st=st, d=dt16, t=tok0, o=ob: e.dma_start(
                            out=s.out[t + st * 128: t + (st + 1) * 128,
                                      d * 128:(d + 1) * 128], in_=o[:]),
                        "qo", waits=[("dv", dvv)]))
            hload_pe = C["pe"]

        s.emit("gpsimd", lambda e: None, waits=s.qall("qo"), inc="po")
        bctx.close()

    def ln_tile2(self, xb, lnc, xout, sq_t, dvw, eps):
        s = self
        s.emit("vector", lambda e: e.reduce_sum(out=lnc[:, 0:1], in_=xb[:], axis=AX),
               waits=[("dv", dvw)], inc="dv")
        s.emit("scalar", lambda e: e.activation(out=sq_t[:], in_=xb[:], func=AF.Square),
               waits=[("dv", dvw)], inc="ac")
        s.emit("vector", lambda e: e.reduce_sum(out=lnc[:, 1:2], in_=sq_t[:], axis=AX),
               waits=["ac"], inc="dv")
        s.emit("vector", lambda e: e.tensor_scalar_mul(out=lnc[:, 2:3], in0=lnc[:, 0:1],
                                                       scalar1=1.0 / D), inc="dv")
        s.emit("vector", lambda e: e.tensor_mul(out=lnc[:, 3:4], in0=lnc[:, 2:3],
                                                in1=lnc[:, 2:3]), inc="dv")
        s.emit("vector", lambda e: e.scalar_tensor_tensor(
            out=lnc[:, 4:5], in0=lnc[:, 1:2], scalar=1.0 / D, in1=lnc[:, 3:4],
            op0=ALU.mult, op1=ALU.subtract), inc="dv")
        s.emit("scalar", lambda e: e.activation(out=lnc[:, 5:6], in_=lnc[:, 4:5],
                                                func=AF.Ln, bias=eps),
               waits=["dv"], inc="ac")
        s.emit("scalar", lambda e: e.activation(out=lnc[:, 6:7], in_=lnc[:, 5:6],
                                                func=AF.Exp, scale=-0.5), inc="ac")
        s.emit("vector", lambda e: e.scalar_tensor_tensor(
            out=lnc[:, 7:8], in0=lnc[:, 2:3], scalar=-1.0, in1=lnc[:, 6:7],
            op0=ALU.mult, op1=ALU.mult), waits=["ac"], inc="dv")
        s.emit("scalar", lambda e: e.activation(out=xout[:], in_=xb[:],
                                                func=AF.Identity,
                                                bias=lnc[:, 7:8], scale=lnc[:, 6:7]),
               waits=["dv"], inc="ac")


_NC_CACHE = {}


def _get_nc(dev=False, stop_a=False, nta_lim=NTA, lvl=99):
    key = (dev, stop_a, nta_lim, lvl)
    if key not in _NC_CACHE:
        _NC_CACHE[key] = KB(dev_dump=dev, stop_after_a=stop_a,
                            nta_lim=nta_lim, lvl=lvl).build()
    return _NC_CACHE[key]


def _core_inputs(inputs, b, hf):
    x = np.ascontiguousarray(inputs["L0_attn_x"][b])
    Wqkvz = inputs["L0_attn_W_qkvz"]
    Wba = inputs["L0_attn_W_ba"]
    hq = range(8 * hf, 8 * hf + 8)
    hv = list(range(16 * hf, 16 * hf + 16))
    rows_q = np.concatenate([np.arange(768 * h, 768 * h + 128) for h in hq])
    rows_k = np.concatenate([np.arange(768 * h + 128, 768 * h + 256) for h in hq])
    rows_v = np.concatenate([np.arange(768 * h + 256, 768 * h + 512) for h in hq])
    rows_qkv = np.concatenate([rows_q, rows_k, rows_v])
    rows_ba = np.concatenate([[4 * h, 4 * h + 1] for h in hq]
                             + [[4 * h + 2, 4 * h + 3] for h in hq])
    rows_z = np.concatenate([np.arange(768 * h + 512, 768 * h + 768)
                             for h in range(16)])
    ch_q = np.concatenate([np.arange(128 * h, 128 * h + 128) for h in hq])
    ch = np.concatenate([ch_q, 2048 + ch_q,
                         np.concatenate([4096 + np.arange(256 * h, 256 * h + 256)
                                         for h in hq])])
    return {
        "x": x,
        "xh": np.ascontiguousarray(x[hf * 4096:(hf + 1) * 4096]),
        "wqkvT": np.ascontiguousarray(Wqkvz[rows_qkv].T),
        "wbaT": np.ascontiguousarray(Wba[rows_ba].T),
        "convw": np.ascontiguousarray(inputs["L0_attn_conv_w"][ch]),
        "dtb": np.ascontiguousarray(inputs["L0_attn_dt_bias"][hv][None, :]),
        "alog": np.ascontiguousarray(inputs["L0_attn_A_log"][hv][None, :]),
        "normw": np.ascontiguousarray(inputs["L0_attn_norm_w"][:, None]),
        "wzT": np.ascontiguousarray(Wqkvz[rows_z].T),
        "woutT": np.ascontiguousarray(inputs["L0_attn_W_out"].T),
        "wgu": np.ascontiguousarray(inputs["L0_ffn_w_gate_up"]),
        "wp": np.ascontiguousarray(inputs["L0_ffn_w_proj"]),
        "hoff": np.array([[hf * NTB]], dtype=np.uint32),
    }


def kernel(**inputs):
    nc = _get_nc(False)
    in_maps = []
    for c in range(8):
        in_maps.append(_core_inputs(inputs, c // 2, c % 2))
    res = run_bass_kernel_spmd(nc, in_maps, core_ids=list(range(8)))
    out = np.empty((B, S, D), np.float32)
    for c in range(8):
        b, hf = c // 2, c % 2
        out[b, hf * 4096:(hf + 1) * 4096] = res.results[c]["out"]
    return out


# version 02
# speedup vs baseline: 1.0158x; 1.0158x over previous
import contextlib

import numpy as np
import ml_dtypes

import concourse.bass as bass
import concourse.mybir as mybir
from concourse.bass_utils import run_bass_kernel_spmd

F32 = mybir.dt.float32
BF16 = mybir.dt.bfloat16
U32 = mybir.dt.uint32
AF = mybir.ActivationFunctionType
ALU = mybir.AluOpType
AX = mybir.AxisListType.X
BF = ml_dtypes.bfloat16
AP = bass.AP

B, S, D = 4, 8192, 2048
HD = 128
NHQ, NHV = 8, 16
CH = 128
NCH = S // CH
TA = 512
NTA = S // TA
SEG = 1024
NSEG = S // SEG
TB = 512
NTB = (S // 2) // TB
FQKV = 4096
NF = FQKV // 128
EPS = 1e-6
LNEPS = 1e-5
F32EPS = float(np.finfo(np.float32).eps)
MNEG = -1.0e9
LN128 = float(np.log(128.0))

ENGINES = ("sync", "tensor", "vector", "scalar", "gpsimd")
DR_ALL = False


class KB:
    def __init__(self, dev_dump=False, stop_after_a=False, nta_lim=NTA, lvl=99):
        self.dev = dev_dump
        self.stop_a = stop_after_a
        self.nta_lim = nta_lim
        self.lvl = lvl
        self.nc = bass.Bass("TRN2", target_bir_lowering=False, debug=False,
                            num_devices=8, detect_race_conditions=False)
        self.ops = {e: [] for e in ENGINES}
        names = ["pe", "dv", "ac", "po"]
        for cls in ("qx", "qw", "qh", "qb", "qo"):
            names += [cls + str(i) for i in range(4)]
        names += ["cc" + str(i) for i in range(NSEG)]
        self.count = {s: 0 for s in names}
        self.rot = {cls: 0 for cls in ("qx", "qw", "qh", "qb", "qo")}
        self.sems = {}
        self.octx = contextlib.ExitStack()

    def emit(self, eng, fn, waits=(), inc=None, amt=1, dr=False):
        wl = []
        for w in waits:
            if isinstance(w, tuple):
                if w[1] > 0:
                    wl.append(w)
            elif self.count[w] > 0:
                wl.append((w, self.count[w]))
        if inc is not None:
            self.count[inc] += amt
            self.ops[eng].append((fn, wl, (inc, amt), dr))
        else:
            self.ops[eng].append((fn, wl, None, dr))

    def dma(self, eng, fn, cls, waits=()):
        i = self.rot[cls]
        self.rot[cls] = (i + 1) % 4
        sem = cls + str(i)
        sw = [(sem, self.count[sem])] if self.count[sem] > 0 else []
        self.emit(eng, fn, waits=list(waits) + sw, inc=sem, amt=16)
        return (sem, self.count[sem])

    def qall(self, cls):
        return [(cls + str(i), self.count[cls + str(i)]) for i in range(4)
                if self.count[cls + str(i)] > 0]

    def mm(self, out_ap, lhsT, rhs, start=True, stop=True, waits=(), inc=None):
        self.emit("tensor",
                  lambda e, o=out_ap, l=lhsT, r=rhs, s1=start, s2=stop:
                  e.matmul(o, l, r, start=s1, stop=s2), waits=waits, inc=inc)

    def realize(self):
        block = self.octx.enter_context(self.nc.Block())
        sems = self.sems

        def make_body(eng_name):
            ops = self.ops[eng_name]

            def body(eng):
                wm = {}
                for fn, waits, inc, dr in ops:
                    for sn, v in waits:
                        if wm.get(sn, 0) < v:
                            eng.wait_ge(sems[sn], v)
                            wm[sn] = v
                    ins = fn(eng)
                    if dr or (eng_name == "vector" and DR_ALL):
                        dins = eng.drain()
                        if ins is not None:
                            ins = dins
                    if inc is not None:
                        sn, a = inc
                        if ins is not None:
                            ins.then_inc(sems[sn], a)
                        else:
                            eng.sem_inc(sems[sn], a)

            return body

        block.sync(make_body("sync"))
        block.tensor(make_body("tensor"))
        block.vector(make_body("vector"))
        block.scalar(make_body("scalar"))
        block.gpsimd(make_body("gpsimd"))

    @staticmethod
    def r3(ap_):
        return ap_.rearrange("p (h c) -> p h c", c=128)

    @staticmethod
    def r4(ap_):
        return ap_.rearrange("p (h d c) -> p h d c", d=2, c=128)

    @staticmethod
    def bc_ap(t, p0, plen, base, dims):
        a = t[p0:p0 + plen, base:base + 1]
        return AP(a.tensor, a.offset, [list(a.ap[0])] + [list(d) for d in dims])

    def dupq4(self, t, base, hstep):
        return self.bc_ap(t, 0, 128, base, [[hstep, NHQ], [0, 2], [1, 128]])

    def colbc(self, t, nh, rep, base=0):
        return self.bc_ap(t, 0, 128, base, [[1, nh], [0, rep]])

    def coldup(self, t, base=0):
        return self.bc_ap(t, 0, 128, base, [[1, NHQ], [0, 2]])

    def build(self):
        nc = self.nc
        dt = nc.dram_tensor
        s = self
        s.x_in = dt("x", [S, D], F32, kind="ExternalInput").ap()
        s.xh_in = dt("xh", [S // 2, D], F32, kind="ExternalInput").ap()
        s.wqkvT = dt("wqkvT", [D, FQKV], F32, kind="ExternalInput").ap()
        s.wbaT = dt("wbaT", [D, 32], F32, kind="ExternalInput").ap()
        s.convw_in = dt("convw", [FQKV, 4], F32, kind="ExternalInput").ap()
        s.dtb_in = dt("dtb", [1, NHV], F32, kind="ExternalInput").ap()
        s.alog_in = dt("alog", [1, NHV], F32, kind="ExternalInput").ap()
        s.normw_in = dt("normw", [HD, 1], F32, kind="ExternalInput").ap()
        s.wzT_in = dt("wzT", [D, 4096], F32, kind="ExternalInput").ap()
        s.woutT_in = dt("woutT", [4096, D], F32, kind="ExternalInput").ap()
        s.wgu_in = dt("wgu", [16384, D], F32, kind="ExternalInput").ap()
        s.wp_in = dt("wp", [D, 8192], F32, kind="ExternalInput").ap()
        s.hoff_in = dt("hoff", [1, 1], U32, kind="ExternalInput").ap()
        s.out = dt("out", [S // 2, D], F32, kind="ExternalOutput").ap()

        s.wqkv_bf = dt("wqkv_bf", [D, FQKV], BF16, kind="Internal").ap()
        s.wba_bf = dt("wba_bf", [D, 32], BF16, kind="Internal").ap()
        s.wz_bf = dt("wz_bf", [D, 4096], BF16, kind="Internal").ap()
        s.wout_bf = dt("wout_bf", [4096, D], BF16, kind="Internal").ap()
        s.wguh_d = dt("wguh", [D, 16384], BF16, kind="Internal").ap()
        s.wph_d = dt("wph", [8192, D], BF16, kind="Internal").ap()
        s.hseg = [dt(f"hseg{i}", [2048, SEG], BF16, kind="Internal").ap()
                  for i in range(NSEG)]
        s.gseg = [dt(f"gseg{i}", [4096, SEG], BF16, kind="Internal").ap()
                  for i in range(NSEG)]
        s.bigH = dt("bigH", [4096, S], BF16, kind="Internal").ap()
        s.bounce = dt("bounce", [CH, 144], BF16, kind="Internal").ap()
        if s.dev:
            s.hdump = dt("hdump", [2048, S], BF16, kind="ExternalOutput").ap()
            s.x2dump = dt("x2dump", [S // 2, D], F32, kind="ExternalOutput").ap()
            s.hrdump = dt("hrdump", [128, 64 * TB], BF16, kind="ExternalOutput").ap()
            s.xl2dump = dt("xl2dump", [128, 16 * TB], BF16, kind="ExternalOutput").ap()
            s.wgudump = dt("wgudump", [D, 16384], BF16, kind="ExternalOutput").ap()

        for name in s.count:
            s.sems[name] = s.octx.enter_context(nc.semaphore(name))
        sb = lambda n, sh, d: s.octx.enter_context(nc.sbuf_tensor(n, sh, d))
        s.psW = s.octx.enter_context(nc.psum_tensor("psW", [128, 2048], F32))
        s.psW2 = s.octx.enter_context(nc.psum_tensor("psW2", [128, 2048], F32))

        s.ident = sb("ident", [128, 128], BF16)
        s.identf = sb("identf", [128, 128], F32)
        s.uincl = sb("uincl", [128, 128], F32)
        s.onescol = sb("onescol", [128, 1], BF16)
        s.onescolf = sb("onescolf", [128, 1], F32)
        s.onesrow = sb("onesrow", [1, 128], F32)
        s.onesrowb = sb("onesrowb", [1, 128], BF16)
        s.maskL = sb("maskL", [128, 512], BF16)
        s.maskU = sb("maskU", [128, 512], BF16)
        s.maskUI = sb("maskUI", [128, 512], BF16)
        s.normw_c = sb("normw_c", [HD, 1], F32)
        s.dtbB = sb("dtbB", [128, NHV], F32)
        s.neaB = sb("neaB", [128, NHV], F32)
        s.rowstage = sb("rowstage", [65, 2048], BF16)
        s.ones65 = sb("ones65", [65, 128], BF16)
        s.tinyf = sb("tinyf", [128, 96], F32)
        s.constc = sb("constc", [128, 8], F32)

        for i, v in enumerate([LNEPS, 1.0, EPS, F32EPS]):
            nc.const_aps.register(F32, float(v), s.constc[:, i:i + 1])
        s.setup_consts()
        s.phase_a()
        if s.stop_a:
            ccw = [("cc%d" % i, s.count["cc%d" % i]) for i in range(NSEG)
                   if s.count["cc%d" % i] > 0]
            s.emit("gpsimd", lambda e: None,
                   waits=s.qall("qh") + s.qall("qb") + ccw, inc="po")
        else:
            s.phase_b()
        s.realize()
        s.octx.close()
        return nc

    def setup_consts(self):
        s, C = self, self.count

        def c_memsets(e):
            e.memset(s.ident[:], 1.0)
            e.memset(s.identf[:], 1.0)
            e.memset(s.uincl[:], 1.0)
            e.memset(s.onescol[:], 1.0)
            e.memset(s.onescolf[:], 1.0)
            e.memset(s.onesrow[:], 1.0)
            e.memset(s.onesrowb[:], 1.0)
            e.memset(s.ones65[:], 1.0)
            e.memset(s.constc[:, 0:1], LNEPS)
            e.memset(s.constc[:, 1:2], 1.0)
            e.memset(s.constc[:, 2:3], EPS)
            e.memset(s.constc[:, 3:4], F32EPS)
            e.memset(s.maskL[:], 0.0)
            e.memset(s.maskU[:], 0.0)
            e.memset(s.maskUI[:], 0.0)
            return e.engine_nop()

        def c_selects(e):
            e.affine_select(out=s.ident[:], in_=s.ident[:], compare_op=ALU.is_equal,
                            fill=0.0, base=0, pattern=[[-1, 128]], channel_multiplier=1)
            e.affine_select(out=s.identf[:], in_=s.identf[:], compare_op=ALU.is_equal,
                            fill=0.0, base=0, pattern=[[-1, 128]], channel_multiplier=1)
            e.affine_select(out=s.uincl[:], in_=s.uincl[:], compare_op=ALU.is_ge,
                            fill=0.0, base=0, pattern=[[1, 128]], channel_multiplier=-1)
            for r in range(4):
                sl = slice(r * 128, r * 128 + 128)
                e.affine_select(out=s.maskL[:, sl], in_=s.maskL[:, sl],
                                compare_op=ALU.is_ge, fill=MNEG, base=-1,
                                pattern=[[-1, 128]], channel_multiplier=1)
                e.affine_select(out=s.maskU[:, sl], in_=s.maskU[:, sl],
                                compare_op=ALU.is_ge, fill=MNEG, base=-1,
                                pattern=[[1, 128]], channel_multiplier=-1)
                e.affine_select(out=s.maskUI[:, sl], in_=s.maskUI[:, sl],
                                compare_op=ALU.is_ge, fill=MNEG, base=0,
                                pattern=[[1, 128]], channel_multiplier=-1)
            return e.engine_nop()

        s.emit("gpsimd", c_memsets, inc="po")
        s.emit("gpsimd", c_selects, waits=["po"], inc="po")
        t1 = s.dma("sync", lambda e: e.dma_start(out=s.normw_c[:], in_=s.normw_in), "qx")
        t2 = s.dma("sync", lambda e: e.dma_start(out=s.tinyf[0:1, 0:NHV], in_=s.dtb_in), "qx")
        t3 = s.dma("sync", lambda e: e.dma_start(out=s.tinyf[0:1, 16:32], in_=s.alog_in), "qx")
        s.mm(s.psW[:, 0:NHV], s.onesrow[:], s.tinyf[0:1, 0:NHV],
             waits=[t1, t2, t3, "po"])
        s.mm(s.psW[:, 128:128 + NHV], s.onesrow[:], s.tinyf[0:1, 16:32], inc="pe")
        pev = C["pe"]
        s.emit("vector", lambda e: e.tensor_copy(out=s.dtbB[:], in_=s.psW[:, 0:NHV]),
               waits=[("pe", pev)], inc="dv", dr=True)
        s.emit("scalar", lambda e: e.activation(out=s.neaB[:],
                                                in_=s.psW[:, 128:128 + NHV],
                                                func=AF.Exp),
               waits=[("pe", pev)], inc="ac")
        s.emit("vector", lambda e: e.tensor_scalar_mul(out=s.neaB[:], in0=s.neaB[:],
                                                       scalar1=-1.0),
               waits=["ac"], inc="dv", dr=True)

    def cvt(self, src_ap, dst_ap, cols):
        s, C = self, self.count
        i = s.cvt_i
        bu = s.wbuf[i % 2]
        w_st = [("ac", s.cvt_ac[-1])] if s.cvt_ac else []
        w_bu = [s.cvt_qh[-2]] if len(s.cvt_qh) >= 2 else []
        tl = s.dma("sync", lambda e, a=src_ap, c=cols: e.dma_start(
            out=s.wstg[:, 0:c], in_=a), "qw", waits=w_st)
        s.emit("scalar", lambda e, b=bu, c=cols: e.activation(
            out=b[:, 0:c], in_=s.wstg[:, 0:c], func=AF.Copy),
            waits=[tl] + w_bu, inc="ac")
        s.cvt_ac.append(C["ac"])
        av = C["ac"]
        ts = s.dma("gpsimd", lambda e, b=bu, d=dst_ap, c=cols: e.dma_start(
            out=d, in_=b[:, 0:c]), "qh", waits=[("ac", av)])
        s.cvt_qh.append(ts)
        s.cvt_i += 1

    def ln_tile(self, xb, lnc, xout, sq_t, tok, eps):
        s = self
        s.emit("vector", lambda e: e.reduce_sum(out=lnc[:, 0:1], in_=xb[:], axis=AX),
               waits=[tok], inc="dv")
        s.emit("scalar", lambda e: e.activation(out=sq_t[:], in_=xb[:], func=AF.Square),
               waits=[tok], inc="ac")
        s.emit("vector", lambda e: e.reduce_sum(out=lnc[:, 1:2], in_=sq_t[:], axis=AX),
               waits=["ac"], inc="dv")
        s.emit("vector", lambda e: e.tensor_scalar_mul(out=lnc[:, 2:3], in0=lnc[:, 0:1],
                                                       scalar1=1.0 / D), inc="dv", dr=True)
        s.emit("vector", lambda e: e.tensor_mul(out=lnc[:, 3:4], in0=lnc[:, 2:3],
                                                in1=lnc[:, 2:3]), inc="dv", dr=True)
        s.emit("vector", lambda e: e.scalar_tensor_tensor(
            out=lnc[:, 4:5], in0=lnc[:, 1:2], scalar=1.0 / D, in1=lnc[:, 3:4],
            op0=ALU.mult, op1=ALU.subtract), inc="dv")
        s.emit("scalar", lambda e: e.activation(out=lnc[:, 5:6], in_=lnc[:, 4:5],
                                                func=AF.Ln, bias=eps),
               waits=["dv"], inc="ac")
        s.emit("scalar", lambda e: e.activation(out=lnc[:, 6:7], in_=lnc[:, 5:6],
                                                func=AF.Exp, scale=-0.5), inc="ac")
        s.emit("vector", lambda e: e.scalar_tensor_tensor(
            out=lnc[:, 7:8], in0=lnc[:, 2:3], scalar=-1.0, in1=lnc[:, 6:7],
            op0=ALU.mult, op1=ALU.mult), waits=["ac"], inc="dv")
        s.emit("scalar", lambda e: e.activation(out=xout[:], in_=xb[:],
                                                func=AF.Identity,
                                                bias=lnc[:, 7:8], scale=lnc[:, 6:7]),
               waits=["dv"], inc="ac")

    def transpose16b(self, src_bf, dst, ps, evac_extra):
        s, C = self, self.count
        for cg in range(4):
            w0 = [("ac", C["ac"]), ("dv", C["dv"]), ("po", C["po"])]
            for k in range(4):
                c = cg * 4 + k
                s.mm(ps[:, k * 128:(k + 1) * 128], src_bf[:, c * 128:(c + 1) * 128],
                     s.ident[:], waits=w0, inc="pe" if k == 3 else None)
                w0 = []
            pev = C["pe"]
            for k in range(4):
                c = cg * 4 + k
                s.emit("scalar", lambda e, c=c, k=k, d=dst: e.activation(
                    out=d(c), in_=ps[:, k * 128:(k + 1) * 128], func=AF.Copy),
                    waits=[("pe", pev)] + (evac_extra if cg == 0 and k == 0 else []),
                    inc="ac")

    def transpose16(self, src_bf, dst, dst_col, ps):
        s, C = self, self.count
        for cg in range(4):
            w0 = [("ac", C["ac"]), ("dv", C["dv"]), ("po", C["po"])]
            for k in range(4):
                c = cg * 4 + k
                s.mm(ps[:, k * 128:(k + 1) * 128], src_bf[:, c * 128:(c + 1) * 128],
                     s.ident[:], waits=w0, inc="pe" if k == 3 else None)
                w0 = []
            pev = C["pe"]
            for k in range(4):
                c = cg * 4 + k
                s.emit("scalar", lambda e, c=c, k=k: e.activation(
                    out=dst(c), in_=ps[:, k * 128:(k + 1) * 128], func=AF.Copy),
                    waits=[("pe", pev)], inc="ac")

    def phase_a(self):
        s, nc, C = self, self.nc, self.count
        actx = contextlib.ExitStack()
        sa = lambda n, sh, d: actx.enter_context(nc.sbuf_tensor(n, sh, d))

        s.wstg = sa("wstg", [128, 2048], F32)
        s.wbuf = [sa(f"wbuf{i}", [128, 2048], BF16) for i in range(2)]
        s.cwcol = sa("cwcol", [128, 128], F32)
        s.xstg = sa("xstg", [128, D], F32)
        s.lncol = sa("lncol", [128, 8], F32)
        s.xlnT = sa("xlnT", [128, 16 * TA], BF16)
        s.prest = sa("prest", [128, 32 * (TA + 4)], BF16)
        s.convacc = sa("convacc", [128, TA], F32)
        s.act = sa("act", [128, 32 * TA], BF16)
        s.bast = sa("bast", [128, TA], F32)
        s.batok = sa("batok", [128, 32], F32)
        s.gcol = sa("gcol", [128, NHV], F32)
        s.bcol = sa("bcol", [128, NHV], F32)
        s.gccol = sa("gccol", [128, NHV], F32)
        s.glastb = sa("glastb", [128, NHV], F32)
        s.gamcol = sa("gamcol", [128, NHV], F32)
        s.lnsk = sa("lnsk", [128, NHQ], F32)
        s.lnsq = sa("lnsq", [128, NHQ], F32)
        s.cmt = [sa(f"cm{v}", [128, NHV], F32) for v in range(3)]
        s.r3col = sa("r3col", [128, NHV], BF16)
        s.nr2col = sa("nr2col", [128, NHV], F32)
        s.qtok = sa("qtok", [128, NHQ * 128], BF16)
        s.ktok = sa("ktok", [128, NHQ * 128], BF16)
        s.vbtok = sa("vbtok", [128, NHV * 128], BF16)
        s.sqsc = sa("sqsc", [128, NHV * 128], BF16)
        s.msqq = sa("msqq", [128, NHQ], F32)
        s.msqk = sa("msqk", [128, NHQ], F32)
        s.dec = [sa(f"dec{v}", [128, NHV * 128], BF16) for v in range(3)]
        s.Mst = sa("Mst", [128, NHV * 128], BF16)
        s.Ust = sa("Ust", [128, NHV * 128], BF16)
        s.Pw = [sa(f"Pw{i}", [128, NHV * 128], BF16) for i in range(2)]
        s.Uw = [sa(f"Uw{i}", [128, NHV * 128], BF16) for i in range(2)]
        s.sqg = sa("sqg", [128, NHV * 128], BF16)
        s.qs = sa("qs", [128, NHV * 128], BF16)
        s.zst = sa("zst", [128, NHV * 128], BF16)
        s.yst = [sa(f"yst{i}", [128, NHV * 128], BF16) for i in range(2)]
        s.oib = sa("oib", [128, NHV * 128], BF16)
        s.rstf = sa("rstf", [128, NHV * 64], F32)
        s.Sst = sa("Sst", [128, NHV * 128], F32)
        s.Sbf = sa("Sbf", [128, NHV * 128], BF16)
        s.glrow = sa("glrow", [1, NHV], F32)

        s.emit("gpsimd", lambda e: e.memset(s.Sst[:], 0.0), waits=["po"], inc="po")
        s.emit("gpsimd", lambda e: e.memset(s.Sbf[:], 0.0), inc="po")
        s.emit("gpsimd", lambda e: e.memset(s.prest[:], 0.0), inc="po")
        s.po_init = C["po"]
        s.cw_tok = s.dma("sync", lambda e: e.dma_start(
            out=s.cwcol[:].rearrange("p (t j) -> p t j", j=4),
            in_=s.convw_in.rearrange("(t p) j -> p t j", p=128)), "qx")

        s.cvt_i, s.cvt_ac, s.cvt_qh = 0, [], []
        for i in range(D // 128):
            for j in range(FQKV // 2048):
                s.cvt(s.wqkvT[i * 128:(i + 1) * 128, j * 2048:(j + 1) * 2048],
                      s.wqkv_bf[i * 128:(i + 1) * 128, j * 2048:(j + 1) * 2048], 2048)
        for i in range(D // 128):
            s.cvt(s.wbaT[i * 128:(i + 1) * 128, :], s.wba_bf[i * 128:(i + 1) * 128, :], 32)

        s.xstg_free = 0
        s.wbuf_pe = {}
        s.hseg_toks = []

        for ta in range(self.nta_lim):
            for sub in range(4):
                tok0 = ta * TA + sub * 128
                w_x = [("ac", s.xstg_free)] if s.xstg_free else []
                xt = s.dma("sync", lambda e, t=tok0: e.dma_start(
                    out=s.xstg[:], in_=s.x_in[t:t + 128, :]), "qx", waits=w_x)
                s.ln_tile(s.xstg, s.lncol, s.qs, s.sqsc[:, 0:D], xt, LNEPS)
                s.xstg_free = C["ac"]
                s.transpose16(
                    s.qs,
                    lambda c, su=sub: s.xlnT[:, c * TA + su * 128: c * TA + (su + 1) * 128],
                    0, s.psW2[:, 0:512])
            xlnT_ready = C["ac"]

            evac_half = [0, 0]
            for f in range(NF + 1):
                bu = s.wbuf[f % 2]
                w_bu = [("pe", s.wbuf_pe[f - 2])] if (f - 2) in s.wbuf_pe else []
                if ta == 0 and f < 2:
                    w_bu += [("ac", s.cvt_ac[-1]), s.cvt_qh[-1]]
                if f < NF:
                    src = s.wqkv_bf.rearrange("(ct p) f -> p ct f", p=128)[
                        :, :, f * 128:(f + 1) * 128]
                    ncol = 128
                else:
                    src = s.wba_bf.rearrange("(ct p) f -> p ct f", p=128)
                    ncol = 32
                wt = s.dma("sync", lambda e, a=src, b=bu, n=ncol: e.dma_start(
                    out=b[:].rearrange("p (ct f) -> p ct f", ct=16)[:, :, 0:n], in_=a),
                    "qw", waits=w_bu)
                half = f % 2
                ps = s.psW[:, half * 512: half * 512 + 512]
                for c in range(16):
                    w = ([wt, ("ac", max(xlnT_ready, evac_half[half]))]
                         if c == 0 else [])
                    if ta == 0 and f <= 1 and c == 0:
                        w.append(("dv", C["dv"]))
                    s.mm(ps[0:ncol, 0:TA], bu[:, c * 128:c * 128 + ncol],
                         s.xlnT[:, c * TA:(c + 1) * TA], start=(c == 0),
                         stop=(c == 15), waits=w, inc="pe" if c == 15 else None)
                s.wbuf_pe[f] = C["pe"]
                pev = C["pe"]
                if f < NF:
                    wtail = ([("dv", s.conv_tail_dv)]
                             if getattr(s, "conv_tail_dv", 0) else [])
                    s.emit("scalar", lambda e, f=f, p=ps: e.activation(
                        out=s.prest[:, f * (TA + 4) + 4: f * (TA + 4) + 4 + TA],
                        in_=p[:, 0:TA], func=AF.Copy),
                        waits=[("pe", pev)] + wtail, inc="ac")
                    evac_half[half] = C["ac"]
                else:
                    s.emit("vector", lambda e, p=ps: e.tensor_copy(
                        out=s.bast[0:32, 0:TA], in_=p[0:32, 0:TA]),
                        waits=[("pe", pev)], inc="dv")
            pre_ac = C["ac"]

            for f in range(NF if self.lvl >= 1 else 0):
                base = f * (TA + 4)
                w0 = ([("ac", pre_ac), s.cw_tok, ("dv", C["dv"]),
                       ("po", s.po_init)] if f == 0 else [])
                s.emit("scalar", lambda e, b=base, f=f: e.activation(
                    out=s.convacc[:], in_=s.prest[:, b + 1: b + 1 + TA],
                    func=AF.Identity,
                    scale=s.cwcol[:, f * 4: f * 4 + 1]), waits=w0, inc="ac")
                for j in (1, 2, 3):
                    s.emit("vector", lambda e, b=base, f=f, j=j: e.scalar_tensor_tensor(
                        out=s.convacc[:], in0=s.prest[:, b + 1 + j: b + 1 + j + TA],
                        scalar=s.cwcol[:, f * 4 + j: f * 4 + j + 1], in1=s.convacc[:],
                        op0=ALU.mult, op1=ALU.add),
                        waits=["ac"] if j == 1 else [], inc="dv")
                s.emit("scalar", lambda e, f=f: e.activation(
                    out=s.act[:, f * TA:(f + 1) * TA], in_=s.convacc[:], func=AF.Silu),
                    waits=["dv"], inc="ac")
                s.emit("vector", lambda e, b=base: e.tensor_copy(
                    out=s.prest[:, b + 1: b + 4],
                    in_=s.prest[:, b + 4 + TA - 3: b + 4 + TA]),
                    waits=["ac"], inc="dv")
            if f == NF - 1:
                s.conv_tail_dv = C["dv"]

            for sub in range(4):
                if self.lvl >= 2:
                    self.chunk(ta, sub)

        self.fence = ([(k, C[k]) for k in ("pe", "dv", "ac", "po")]
                      + s.qall("qx") + s.qall("qw") + s.qall("qh") + s.qall("qb"))
        actx.close()

    def chunk(self, ta, sub):
        s, C = self, self.count
        n = ta * 4 + sub
        off = sub * 128
        aq = lambda h: s.act[:, h * TA + off: h * TA + off + 128]
        ak = lambda h: s.act[:, (8 + h) * TA + off: (8 + h) * TA + off + 128]
        av = lambda ih: s.act[:, (16 + ih) * TA + off: (16 + ih) * TA + off + 128]
        T = s.tinyf

        s.mm(s.psW2[:, 0:32], s.bast[0:32, off:off + 128], s.identf[0:32, 0:32],
             waits=["dv", "ac"], inc="pe")
        pev = C["pe"]
        s.emit("vector", lambda e: e.tensor_copy(out=s.batok[:], in_=s.psW2[:, 0:32]),
               waits=[("pe", pev)], inc="dv", dr=True)
        s.emit("scalar", lambda e: e.activation(out=T[:, 32:48], in_=s.batok[:, 0:16],
                                                func=AF.Tanh, scale=0.5),
               waits=["dv"], inc="ac")
        s.emit("vector", lambda e: e.tensor_scalar(out=s.bcol[:], in0=T[:, 32:48],
                                                   scalar1=0.5, scalar2=0.5,
                                                   op0=ALU.mult, op1=ALU.add),
               waits=["ac"], inc="dv")
        s.emit("vector", lambda e: e.tensor_add(out=T[:, 0:16], in0=s.batok[:, 16:32],
                                                in1=s.dtbB[:]), inc="dv")
        s.emit("scalar", lambda e: e.activation(out=T[:, 16:32], in_=T[:, 0:16],
                                                func=AF.Abs), waits=["dv"], inc="ac")
        s.emit("scalar", lambda e: e.activation(out=T[:, 32:48], in_=T[:, 16:32],
                                                func=AF.Exp, scale=-1.0), inc="ac")
        s.emit("scalar", lambda e: e.activation(out=T[:, 48:64], in_=T[:, 32:48],
                                                func=AF.Ln, bias=1.0), inc="ac")
        s.emit("vector", lambda e: e.tensor_scalar_max(out=T[:, 64:80], in0=T[:, 0:16],
                                                       scalar1=0.0),
               waits=["ac"], inc="dv", dr=True)
        s.emit("vector", lambda e: e.tensor_add(out=T[:, 64:80], in0=T[:, 64:80],
                                                in1=T[:, 48:64]), inc="dv", dr=True)
        s.emit("vector", lambda e: e.tensor_mul(out=s.gcol[:], in0=T[:, 64:80],
                                                in1=s.neaB[:]), inc="dv")
        dvv = C["dv"]
        s.mm(s.psW2[:, 64:80], s.uincl[:], s.gcol[:], waits=[("dv", dvv)])
        s.mm(s.psW2[0:1, 96:112], s.onescolf[:], s.gcol[:], inc="pe")
        pev = C["pe"]
        s.emit("vector", lambda e: e.tensor_copy(out=s.gccol[:], in_=s.psW2[:, 64:80]),
               waits=[("pe", pev)], inc="dv")
        s.emit("scalar", lambda e: e.activation(out=s.glrow[:], in_=s.psW2[0:1, 96:112],
                                                func=AF.Copy),
               waits=[("pe", pev)], inc="ac")
        s.mm(s.psW2[:, 64:80], s.onesrow[:], s.glrow[:], waits=["ac", "dv"], inc="pe")
        pev = C["pe"]
        s.emit("vector", lambda e: e.tensor_copy(out=s.glastb[:], in_=s.psW2[:, 64:80]),
               waits=[("pe", pev)], inc="dv")
        s.emit("scalar", lambda e: e.activation(out=s.gamcol[:], in_=s.glastb[:],
                                                func=AF.Exp), waits=["dv"], inc="ac")

        if s.lvl < 3:
            return
        w0 = ["ac", "dv"]
        for h in range(NHQ):
            s.mm(s.psW[:, h * 128:(h + 1) * 128], aq(h), s.ident[:], waits=w0)
            w0 = []
            s.mm(s.psW[:, 1024 + h * 128: 1024 + (h + 1) * 128], ak(h), s.ident[:],
                 inc="pe" if h == NHQ - 1 else None)
        pev = C["pe"]
        s.emit("scalar", lambda e: e.activation(out=s.qtok[:], in_=s.psW[:, 0:1024],
                                                func=AF.Copy),
               waits=[("pe", pev)], inc="ac")
        s.emit("scalar", lambda e: e.activation(out=s.ktok[:], in_=s.psW[:, 1024:2048],
                                                func=AF.Copy), inc="ac")
        acv = C["ac"]
        for ih in range(NHV):
            s.mm(s.psW[:, ih * 128:(ih + 1) * 128], av(ih), s.ident[:],
                 waits=[("ac", acv)] if ih == 0 else [],
                 inc="pe" if ih == NHV - 1 else None)
        pev = C["pe"]
        for ih in range(NHV):
            s.emit("vector", lambda e, ih=ih: e.tensor_scalar_mul(
                out=s.vbtok[:, ih * 128:(ih + 1) * 128],
                in0=s.psW[:, ih * 128:(ih + 1) * 128],
                scalar1=s.bcol[:, ih:ih + 1]),
                waits=[("pe", pev)] if ih == 0 else [], inc="dv")
        vb_dv = C["dv"]

        s.emit("scalar", lambda e: e.activation(out=s.sqsc[:, 0:1024], in_=s.qtok[:],
                                                func=AF.Square), inc="ac")
        s.emit("vector", lambda e: e.reduce_sum(
            out=s.msqq[:], in_=s.sqsc[:, 0:1024].rearrange("p (h c) -> p h c", c=128),
            axis=AX), waits=["ac"], inc="dv")
        s.emit("scalar", lambda e: e.activation(out=s.sqsc[:, 1024:2048], in_=s.ktok[:],
                                                func=AF.Square), inc="ac")
        s.emit("vector", lambda e: e.reduce_sum(
            out=s.msqk[:], in_=s.sqsc[:, 1024:2048].rearrange("p (h c) -> p h c", c=128),
            axis=AX), waits=["ac"], inc="dv")
        s.emit("scalar", lambda e: e.activation(out=T[:, 0:8], in_=s.msqk[:], func=AF.Ln,
                                                scale=1.0 / 128, bias=EPS),
               waits=["dv"], inc="ac")
        s.emit("scalar", lambda e: e.activation(out=T[:, 8:16], in_=s.msqq[:],
                                                func=AF.Ln, scale=1.0 / 128, bias=EPS),
               inc="ac")
        s.emit("vector", lambda e: e.tensor_scalar(out=s.lnsk[:], in0=T[:, 0:8],
                                                   scalar1=-0.5, scalar2=-0.5 * LN128,
                                                   op0=ALU.mult, op1=ALU.add),
               waits=["ac"], inc="dv")
        s.emit("vector", lambda e: e.tensor_scalar(out=s.lnsq[:], in0=T[:, 8:16],
                                                   scalar1=-0.5, scalar2=-LN128,
                                                   op0=ALU.mult, op1=ALU.add), inc="dv")
        lnskd = s.coldup(s.lnsk)
        lnsqd = s.coldup(s.lnsq)
        rr2 = lambda x: x.rearrange("p (h d) -> p h d", d=2)
        s.emit("scalar", lambda e: e.activation(out=T[:, 16:32], in_=s.bcol[:],
                                                func=AF.Ln), waits=["dv"], inc="ac")
        s.emit("vector", lambda e: e.tensor_add(out=s.cmt[0][:], in0=s.gccol[:],
                                                in1=T[:, 16:32]), waits=["ac"], inc="dv",
               dr=True)
        s.emit("vector", lambda e, a=lnskd: e.tensor_tensor(
            out=rr2(s.cmt[0][:]), in0=rr2(s.cmt[0][:]), in1=a, op=ALU.add), inc="dv",
               dr=True)
        s.emit("vector", lambda e: e.tensor_scalar_max(out=s.cmt[0][:], in0=s.cmt[0][:],
                                                       scalar1=0.8 * MNEG), inc="dv")
        s.emit("vector", lambda e, a=lnskd: e.scalar_tensor_tensor(
            out=rr2(s.cmt[1][:]), in0=rr2(s.gccol[:]), scalar=-1.0, in1=a,
            op0=ALU.mult, op1=ALU.add), inc="dv")
        s.emit("vector", lambda e, a=lnsqd: e.tensor_tensor(
            out=rr2(s.cmt[2][:]), in0=rr2(s.gccol[:]), in1=a, op=ALU.add), inc="dv")
        s.emit("scalar", lambda e: e.activation(out=s.nr2col[:], in_=s.cmt[0][:],
                                                func=AF.Exp), waits=["dv"], inc="ac")
        s.emit("vector", lambda e: e.tensor_scalar_mul(out=s.nr2col[:], in0=s.nr2col[:],
                                                       scalar1=-1.0),
               waits=["ac"], inc="dv")
        s.emit("vector", lambda e: e.tensor_add(out=T[:, 32:48], in0=s.cmt[1][:],
                                                in1=s.glastb[:]), inc="dv")
        s.emit("scalar", lambda e: e.activation(out=s.r3col[:], in_=T[:, 32:48],
                                                func=AF.Exp), waits=["dv"], inc="ac")

        if s.lvl < 4:
            return

        masks = [s.maskL, s.maskU, s.maskUI]
        colv = [s.cmt[0], s.cmt[1], s.cmt[1]]
        rowv = [s.cmt[1], s.cmt[0], s.cmt[2]]
        for v in range(3):
            pst = s.psW if v < 2 else s.psW2
            for half in range(2):
                w0 = [("dv", C["dv"]), ("ac", C["ac"])]
                for hh in range(8):
                    h = half * 8 + hh
                    s.mm(s.psW2[0:1, 1024 + hh * 128: 1024 + (hh + 1) * 128],
                         rowv[v][:, h:h + 1], s.identf[:], waits=w0,
                         inc="pe" if hh == 7 else None)
                    w0 = []
                pev = C["pe"]
                s.emit("scalar", lambda e: e.activation(
                    out=s.rstf[0:1, 0:1024], in_=s.psW2[0:1, 1024:2048],
                    func=AF.Copy), waits=[("pe", pev)], inc="ac")
                acv = C["ac"]
                for j in range(2):
                    s.mm(pst[:, half * 1024 + j * 512: half * 1024 + (j + 1) * 512],
                         s.onesrow[:], s.rstf[0:1, j * 512:(j + 1) * 512],
                         start=True, stop=(v == 2),
                         waits=[("ac", acv)] if j == 0 else [],
                         inc="pe" if j == 1 else None)
            if v == 2:
                pev = C["pe"]
                s.emit("scalar", lambda e: e.activation(out=s.sqg[:], in_=s.psW2[:],
                                                        func=AF.Exp),
                       waits=[("pe", pev)], inc="ac")
                mAP = s.bc_ap(s.maskUI, 0, 128, 0, [[0, NHV], [1, 128]])
                s.emit("vector", lambda e, a=mAP: e.tensor_tensor(
                    out=s.r3(s.psW2[:]), in0=s.r3(s.psW2[:]), in1=a, op=ALU.add),
                    waits=["ac"], inc="dv", dr=True)
            else:
                for j in range(4):
                    s.mm(pst[:, j * 512:(j + 1) * 512], s.ident[:], masks[v][:],
                         start=False, stop=True,
                         inc="pe" if j == 3 else None)
            pev = C["pe"]
            cd = s.colbc(colv[v], NHV, 128)
            s.emit("vector", lambda e, a=cd, p=pst: e.tensor_tensor(
                out=s.r3(p[:]), in0=s.r3(p[:]), in1=a, op=ALU.add),
                waits=[("pe", pev)], inc="dv")
            s.emit("scalar", lambda e, v=v, p=pst: e.activation(
                out=s.dec[v][:], in_=p[:], func=AF.Exp),
                   waits=["dv"], inc="ac")
        dec_ac = C["ac"]

        if s.lvl < 5:
            return
        w0 = [("ac", dec_ac)]
        for h in range(NHQ):
            s.mm(s.psW2[:, h * 128:(h + 1) * 128], ak(h), ak(h), waits=w0)
            w0 = []
            s.mm(s.psW2[:, 1024 + h * 128: 1024 + (h + 1) * 128], ak(h), aq(h),
                 inc="pe" if h == NHQ - 1 else None)
        pev = C["pe"]
        kkd = s.dupq4(s.psW2, 0, 128)
        qkd = s.dupq4(s.psW2, 1024, 128)
        s.emit("vector", lambda e, a=kkd: e.tensor_tensor(
            out=s.r4(s.Mst[:]), in0=a, in1=s.r4(s.dec[0][:]), op=ALU.mult),
            waits=[("pe", pev)], inc="dv")
        s.emit("vector", lambda e, a=kkd: e.tensor_tensor(
            out=s.r4(s.Ust[:]), in0=a, in1=s.r4(s.dec[1][:]), op=ALU.mult), inc="dv")
        s.emit("vector", lambda e, a=qkd: e.tensor_tensor(
            out=s.r4(s.sqsc[:]), in0=a, in1=s.r4(s.dec[2][:]), op=ALU.mult), inc="dv")
        build_dv = C["dv"]

        if s.lvl < 6:
            return
        w0 = [("dv", build_dv), ("ac", dec_ac), ("po", s.po_init)]
        for ih in range(NHV):
            s.mm(s.psW[:, ih * 128:(ih + 1) * 128], ak(ih // 2),
                 s.Sbf[:, ih * 128:(ih + 1) * 128], waits=w0,
                 inc="pe" if ih == NHV - 1 else None)
            w0 = []
        pev = C["pe"]
        nr2d = s.colbc(s.nr2col, NHV, 128)
        s.emit("vector", lambda e, a=nr2d: e.tensor_tensor(
            out=s.r3(s.yst[0][:]), in0=s.r3(s.psW[:]), in1=a, op=ALU.mult),
            waits=[("pe", pev), ("dv", vb_dv)], inc="dv", dr=True)
        s.emit("vector", lambda e: e.tensor_add(out=s.zst[:], in0=s.yst[0][:],
                                                in1=s.vbtok[:]), inc="dv")
        z_dv = C["dv"]

        if s.lvl < 7:
            return
        w0 = [("dv", z_dv)]
        for ih in range(NHV):
            sl = slice(ih * 128, (ih + 1) * 128)
            s.mm(s.psW[:, sl], s.Ust[:, sl], s.zst[:, sl], waits=w0,
                 inc="pe" if ih == NHV - 1 else None)
            w0 = []
        pev = C["pe"]
        s.emit("vector", lambda e: e.scalar_tensor_tensor(
            out=s.yst[0][:], in0=s.psW[:], scalar=-1.0, in1=s.zst[:],
            op0=ALU.mult, op1=ALU.add), waits=[("pe", pev)], inc="dv")
        y_dv = C["dv"]
        Mp, Up = s.Mst, s.Ust
        ycur, pp = 0, 0
        for lev in range(1, 7):
            w_m2 = [("ac", C["ac"]), ("dv", C["dv"])]
            for ih in range(NHV):
                sl = slice(ih * 128, (ih + 1) * 128)
                s.mm(s.psW2[:, sl], Mp[:, sl], Up[:, sl], waits=w_m2,
                     inc="pe" if ih == NHV - 1 else None)
                w_m2 = []
            pev = C["pe"]
            s.emit("scalar", lambda e, p=pp: e.activation(out=s.Uw[p][:], in_=s.psW2[:],
                                                          func=AF.Copy),
                   waits=[("pe", pev)], inc="ac")
            if lev < 6:
                w_u2 = [("dv", y_dv)]
                for ih in range(NHV):
                    sl = slice(ih * 128, (ih + 1) * 128)
                    s.mm(s.psW[:, sl], Up[:, sl], Mp[:, sl], waits=w_u2,
                         inc="pe" if ih == NHV - 1 else None)
                    w_u2 = []
                pev = C["pe"]
                s.emit("scalar", lambda e, p=pp: e.activation(out=s.Pw[p][:],
                                                              in_=s.psW[:],
                                                              func=AF.Copy),
                       waits=[("pe", pev)], inc="ac")
            m2_ac = C["ac"]
            w_ap = [("ac", m2_ac)]
            for ih in range(NHV):
                sl = slice(ih * 128, (ih + 1) * 128)
                s.mm(s.psW[:, sl], s.Uw[pp][:, sl], s.yst[ycur][:, sl], waits=w_ap,
                     inc="pe" if ih == NHV - 1 else None)
                w_ap = []
            pev = C["pe"]
            s.emit("vector", lambda e, a=ycur: e.tensor_add(
                out=s.yst[1 - a][:], in0=s.psW[:], in1=s.yst[a][:]),
                waits=[("pe", pev)], inc="dv")
            y_dv = C["dv"]
            ycur = 1 - ycur
            Mp, Up = s.Pw[pp], s.Uw[pp]
            pp = 1 - pp
        vnew = s.yst[ycur]

        if s.lvl < 8:
            return
        qdup = s.dupq4(s.act, off, TA)
        s.emit("vector", lambda e, a=qdup: e.tensor_tensor(
            out=s.r4(s.qs[:]), in0=a, in1=s.r4(s.sqg[:]), op=ALU.mult),
            waits=["ac"], inc="dv")
        qs_dv = C["dv"]
        w0 = [("dv", qs_dv)]
        for ih in range(NHV):
            sl = slice(ih * 128, (ih + 1) * 128)
            s.mm(s.psW[:, sl], s.Sbf[:, sl], s.qs[:, sl], start=True, stop=False,
                 waits=w0)
            w0 = []
            s.mm(s.psW[:, sl], vnew[:, sl], s.sqsc[:, sl], start=False, stop=True,
                 inc="pe" if ih == NHV - 1 else None)
        pev = C["pe"]
        whs = [s.hseg_toks[-1]] if s.hseg_toks else []
        s.emit("scalar", lambda e: e.activation(out=s.oib[:], in_=s.psW[:],
                                                func=AF.Copy),
               waits=[("pe", pev)] + whs + s.qall("qh")[-1:], inc="ac")
        s.emit("vector", lambda e: None, waits=["ac"], inc="dv")
        oi_dv = C["dv"]

        if s.lvl < 9:
            return
        s.emit("scalar", lambda e: e.activation(out=s.sqg[:], in_=s.oib[:],
                                                func=AF.Square),
               waits=[("dv", oi_dv)], inc="ac")
        acv = C["ac"]
        for j in range(4):
            s.mm(s.psW2[0:1, j * 512:(j + 1) * 512], s.onescol[:],
                 s.sqg[:, j * 512:(j + 1) * 512],
                 waits=[("ac", acv), ("dv", C["dv"])] if j == 0 else [],
                 inc="pe" if j == 3 else None)
        pev = C["pe"]
        s.emit("scalar", lambda e: e.activation(out=s.rowstage[0:1, :],
                                                in_=s.psW2[0:1, 0:2048], func=AF.Copy),
               waits=[("pe", pev)], inc="ac")
        acv = C["ac"]
        for j in range(4):
            s.mm(s.psW[:, j * 512:(j + 1) * 512], s.onesrowb[:],
                 s.rowstage[0:1, j * 512:(j + 1) * 512],
                 waits=[("ac", acv), ("dv", oi_dv)] if j == 0 else [],
                 inc="pe" if j == 3 else None)
        pev = C["pe"]
        for jh in range(2):
            s.emit("scalar", lambda e, j=jh: e.activation(
                out=s.rstf[:], in_=s.psW[:, j * 1024:(j + 1) * 1024], func=AF.Ln,
                scale=1.0 / 128, bias=EPS),
                waits=[("pe", pev)] if jh == 0 else [], inc="ac")
            s.emit("scalar", lambda e, j=jh: e.activation(
                out=s.sqg[:, j * 1024:(j + 1) * 1024], in_=s.rstf[:],
                func=AF.Exp, scale=-0.5), inc="ac")
        s.emit("vector", lambda e: e.scalar_tensor_tensor(
            out=s.oib[:], in0=s.oib[:], scalar=s.normw_c[:], in1=s.sqg[:],
            op0=ALU.mult, op1=ALU.mult), waits=["ac"], inc="dv")
        h_dv = C["dv"]
        seg, col0 = n // 8, (n % 8) * 128
        s.hseg_toks.append(s.dma("gpsimd", lambda e, g=seg, c=col0: e.dma_start(
            out=s.hseg[g][:, c:c + 128].rearrange("(h e) t -> e h t", e=128),
            in_=s.oib[:].rearrange("p (h t) -> p h t", t=128)),
            "qh", waits=[("dv", h_dv)]))
        if s.dev:
            s.dma("gpsimd", lambda e, c=n * 128: e.dma_start(
                out=s.hdump[:, c:c + 128].rearrange("(h e) t -> e h t", e=128),
                in_=s.oib[:].rearrange("p (h t) -> p h t", t=128)), "qh")
        if n % 8 == 7:
            toks = s.hseg_toks[-8:]
            uniq = {}
            for sem, vv in toks:
                uniq[sem] = max(uniq.get(sem, 0), vv)
            s.emit("gpsimd", lambda e, g=seg: e.collective_compute(
                "AllGather", ALU.bypass,
                replica_groups=[[0, 1], [2, 3], [4, 5], [6, 7]],
                ins=[s.hseg[g]], outs=[s.gseg[g]]),
                waits=sorted(uniq.items()), inc="cc%d" % seg)

        if s.lvl < 10:
            return
        r3d = s.colbc(s.r3col, NHV, 128)
        vn3 = s.yst[1 - ycur]
        s.emit("vector", lambda e, a=r3d, vn=vnew, o=vn3: e.tensor_tensor(
            out=s.r3(o[:]), in0=s.r3(vn[:]), in1=a, op=ALU.mult), inc="dv")
        v3_dv = C["dv"]
        w0 = [("dv", v3_dv), ("ac", C["ac"])]
        for ih in range(NHV):
            sl = slice(ih * 128, (ih + 1) * 128)
            s.mm(s.psW2[:, sl], s.ktok[:, (ih // 2) * 128:(ih // 2) * 128 + 128],
                 vn3[:, sl], waits=w0, inc="pe" if ih == NHV - 1 else None)
            w0 = []
        pev = C["pe"]
        gamd = s.colbc(s.gamcol, NHV, 128)
        s.emit("vector", lambda e, a=gamd: e.tensor_tensor(
            out=s.r3(s.Sst[:]), in0=s.r3(s.Sst[:]), in1=a, op=ALU.mult), inc="dv")
        s.emit("vector", lambda e: e.tensor_add(out=s.Sst[:], in0=s.Sst[:],
                                                in1=s.psW2[:]),
               waits=[("pe", pev)], inc="dv", dr=True)
        s.emit("vector", lambda e: e.tensor_copy(out=s.Sbf[:], in_=s.Sst[:]), inc="dv")

    def phase_b(self):
        s, nc, C = self, self.nc, self.count
        fence = list(self.fence)
        bctx = contextlib.ExitStack()
        sbb = lambda n, sh, d: bctx.enter_context(nc.sbuf_tensor(n, sh, d))

        xstgB = [sbb(f"xstgB{i}", [128, 2048], F32) for i in range(4)]
        lncB = sbb("lncB", [128, 8], F32)
        xln2T = sbb("xln2T", [128, 16 * TB], BF16)
        xlnbB = sbb("xlnbB", [128, 2048], BF16)
        sqBB = sbb("sqBB", [128, 2048], BF16)
        wbufB = [sbb(f"wbufB{i}", [128, 2048], BF16) for i in range(2)]
        wbufB2 = [sbb(f"wbufB2{i}", [128, 2048], BF16) for i in range(2)]
        aofb = sbb("aofb", [128, 512], BF16)
        rstdc = sbb("rstdc", [128, 8], F32)
        outst = [sbb(f"outst{i}", [128, 128], F32) for i in range(2)]
        hoffs = sbb("hoffs", [1, 1], U32)
        wob = sbb("wob", [128, 32 * 256], BF16)

        g1 = contextlib.ExitStack()
        htil = g1.enter_context(nc.sbuf_tensor("htil", [128, 32 * TB], BF16))
        szt = g1.enter_context(nc.sbuf_tensor("szt", [128, 32 * TB], BF16))
        xln1T = g1.enter_context(nc.sbuf_tensor("xln1T", [128, 16 * TB], BF16))
        g1.close()
        qctx = contextlib.ExitStack()
        wstgB = qctx.enter_context(nc.sbuf_tensor("wstgB", [128, 2048], F32))
        wqB = qctx.enter_context(nc.sbuf_tensor("wqB", [128, 2048], BF16))
        wq2B = qctx.enter_context(nc.sbuf_tensor("wq2B", [128, 2048], BF16))
        qscB = qctx.enter_context(nc.sbuf_tensor("qscB", [128, 96], F32))
        qscBh = qctx.enter_context(nc.sbuf_tensor("qscBh", [128, 32], BF16))
        wtT = qctx.enter_context(nc.sbuf_tensor("wtT", [128, 2048], BF16))

        hoff_q = s.dma("sync", lambda e: e.dma_start(out=hoffs[:], in_=s.hoff_in),
                       "qx", waits=fence)
        s.emit("scalar", lambda e: None, waits=fence, inc="ac")
        s.emit("gpsimd", lambda e: None, waits=fence, inc="po")
        s.emit("vector", lambda e: None, waits=fence, inc="dv")
        s.emit("tensor", lambda e: None, waits=fence, inc="pe")

        s.cvt_i, s.cvt_ac, s.cvt_qh = 0, [C["ac"]], []
        s.wstg = wstgB
        s.wbuf = wbufB
        for i in range(D // 128):
            for j in range(2):
                s.cvt(s.wzT_in[i * 128:(i + 1) * 128, j * 2048:(j + 1) * 2048],
                      s.wz_bf[i * 128:(i + 1) * 128, j * 2048:(j + 1) * 2048], 2048)
        for i in range(4096 // 128):
            s.cvt(s.woutT_in[i * 128:(i + 1) * 128, :],
                  s.wout_bf[i * 128:(i + 1) * 128, :], 2048)

        grp = lambda x: x.rearrange("p (g c) -> p g c", c=64)
        gAP = lambda base: s.bc_ap(qscB, 0, 128, base, [[1, 32], [0, 64]])
        gAPh = lambda: s.bc_ap(qscBh, 0, 128, 0, [[1, 32], [0, 64]])

        s.qt_pe = 0
        s.qt_store = None
        def quant_tile(src_ap, dst_ap):
            w_st = [("ac", s.cvt_ac[-1])]
            tl = s.dma("sync", lambda e, a=src_ap: e.dma_start(out=wstgB[:], in_=a),
                       "qw", waits=w_st)
            s.emit("scalar", lambda e: e.activation(out=wqB[:], in_=wstgB[:],
                                                    func=AF.Copy),
                   waits=[tl, ("dv", C["dv"])] + s.qall("qh"), inc="ac")
            s.cvt_ac.append(C["ac"])
            s.emit("vector", lambda e: e.reduce_sum(
                out=qscB[:, 0:32], in_=grp(wqB[:]), axis=AX,
                apply_absolute_value=True), waits=["ac"], inc="dv", dr=True)
            s.emit("vector", lambda e: e.tensor_scalar(
                out=qscB[:, 32:64], in0=qscB[:, 0:32], scalar1=1.0 / 64, scalar2=1e-8,
                op0=ALU.mult, op1=ALU.max), inc="dv", dr=True)
            s.emit("vector", lambda e: e.tensor_copy(out=qscBh[:],
                                                     in_=qscB[:, 32:64]), inc="dv",
                   dr=True)
            s.emit("vector", lambda e: e.tensor_copy(out=qscB[:, 32:64],
                                                     in_=qscBh[:]), inc="dv", dr=True)
            s.emit("vector", lambda e: e.tensor_scalar_mul(
                out=qscB[:, 0:32], in0=qscB[:, 32:64], scalar1=0.5), inc="dv", dr=True)
            s.emit("vector", lambda e: e.tensor_scalar_mul(
                out=qscB[:, 64:96], in0=qscB[:, 32:64], scalar1=-0.5), inc="dv")
            s.emit("vector", lambda e, a=gAP(0): e.tensor_tensor(
                out=grp(wq2B[:]), in0=grp(wqB[:]), in1=a, op=ALU.is_gt),
                waits=[("pe", s.qt_pe)], inc="dv")
            s.emit("vector", lambda e, a=gAP(64): e.tensor_tensor(
                out=grp(sqBB[:]), in0=grp(wqB[:]), in1=a, op=ALU.is_lt), inc="dv")
            s.emit("vector", lambda e: e.tensor_sub(out=wq2B[:], in0=wq2B[:],
                                                    in1=sqBB[:]), inc="dv")
            s.emit("vector", lambda e, a=gAP(32): e.tensor_tensor(
                out=grp(wq2B[:]), in0=grp(wq2B[:]), in1=a, op=ALU.mult), inc="dv")
            s.qt_wait_store = [s.qt_store] if s.qt_store else []
            s.transpose16b(wq2B, lambda c: wtT[:, c * 128:(c + 1) * 128],
                           s.psW[:, 0:512], s.qt_wait_store)
            s.qt_pe = C["pe"]
            acv = C["ac"]
            s.qt_store = s.dma("gpsimd", lambda e, d=dst_ap: e.dma_start(
                out=d, in_=wtT[:].rearrange("p (ct f) -> p ct f", f=128)),
                  "qh", waits=[("ac", acv)])

        for rt in range(16384 // 128):
            quant_tile(s.wgu_in[rt * 128:(rt + 1) * 128, :],
                       s.wguh_d[:, rt * 128:(rt + 1) * 128].rearrange(
                           "(ct p) f -> p ct f", p=128))
        for rt in range(D // 128):
            for q4 in range(4):
                quant_tile(
                    s.wp_in[rt * 128:(rt + 1) * 128, q4 * 2048:(q4 + 1) * 2048],
                    s.wph_d[q4 * 2048:(q4 + 1) * 2048,
                            rt * 128:(rt + 1) * 128].rearrange(
                                "(ct p) f -> p ct f", p=128))
        quant_done = ([("ac", C["ac"]), ("dv", C["dv"])] + s.qall("qh")
                      + s.qall("qw"))
        if s.dev:
            s.dma("gpsimd", lambda e: e.dma_start(out=s.wgudump, in_=s.wguh_d),
                  "qb", waits=quant_done)
        qctx.close()
        hres = bctx.enter_context(nc.sbuf_tensor("hres", [128, 64 * TB], BF16))
        wpb = bctx.enter_context(nc.sbuf_tensor("wpb", [128, 64 * 128], BF16))

        repack_q = []
        for sg in range(NSEG):
            repack_q.append(s.dma("gpsimd", lambda e, g=sg: e.dma_start(
                out=s.bigH[:, g * SEG:(g + 1) * SEG], in_=s.gseg[g]),
                "qb", waits=[("cc%d" % sg, 1)]))

        s.out_toks = []
        hload_pe = 0
        for bt in range(NTB):
            tok0 = bt * TB
            def hload(e, b=bt):
                v = e.value_load(hoffs[0:1, 0:1])
                src2 = s.bigH.rearrange("(ct p) t -> p ct t", p=128)
                return e.dma_start(
                    out=htil[:].rearrange("p (ct t) -> p ct t", t=TB),
                    in_=src2[:, :, bass.ts(v + b, TB)])
            h_q = s.dma("sync", hload,
                        "qx", waits=[hoff_q, ("pe", hload_pe)] + repack_q
                        + (quant_done if bt == 0 else []))

            for sub in range(4):
                t0 = tok0 + sub * 128
                xb = xstgB[sub]
                w_x = [("dv", C["dv"])]
                xt = s.dma("sync", lambda e, t=t0, x=xb: e.dma_start(
                    out=x[:], in_=s.xh_in[t:t + 128, :]), "qx", waits=w_x)
                s.ln_tile(xb, lncB, xlnbB, sqBB, xt, LNEPS)
                s.transpose16(
                    xlnbB,
                    lambda c, su=sub: xln1T[:, c * TB + su * 128: c * TB + (su + 1) * 128],
                    0, s.psW2[:, 0:512])
            x1_ac = C["ac"]

            zf_pe = {}
            for zf in range(32):
                bu = wbufB[zf % 2]
                w_bu = [("pe", zf_pe[zf - 2])] if (zf - 2) in zf_pe else []
                src = s.wz_bf.rearrange("(ct p) f -> p ct f", p=128)[
                    :, :, zf * 128:(zf + 1) * 128]
                wt = s.dma("sync", lambda e, a=src, b=bu: e.dma_start(
                    out=b[:, 0:2048].rearrange("p (ct f) -> p ct f", ct=16), in_=a),
                    "qw", waits=w_bu)
                half = zf % 2
                ps = s.psW[:, half * 512: half * 512 + 512]
                for c in range(16):
                    w = [wt, ("ac", C["ac"]), ("dv", C["dv"])] if c == 0 else []
                    s.mm(ps[:, 0:TB], bu[:, c * 128:(c + 1) * 128],
                         xln1T[:, c * TB:(c + 1) * TB], start=(c == 0), stop=(c == 15),
                         waits=w, inc="pe" if c == 15 else None)
                zf_pe[zf] = C["pe"]
                pev = C["pe"]
                s.emit("scalar", lambda e, z=zf, p=ps: e.activation(
                    out=szt[:, z * TB:(z + 1) * TB], in_=p[:, 0:TB], func=AF.Silu),
                    waits=[("pe", pev)], inc="ac")
            sz_ac = C["ac"]
            s.emit("vector", lambda e: e.tensor_mul(out=htil[:], in0=htil[:],
                                                    in1=szt[:]),
                   waits=[("ac", sz_ac), h_q], inc="dv")
            gate_dv = C["dv"]

            for dq in range(8):
                src = s.wout_bf.rearrange("(ft p) d -> p ft d", p=128)[
                    :, :, dq * 256:(dq + 1) * 256]
                wt = s.dma("sync", lambda e, a=src: e.dma_start(
                    out=wob[:].rearrange("p (ft d) -> p ft d", d=256), in_=a),
                    "qw", waits=[("pe", C["pe"])])
                for dt in range(2):
                    ps = s.psW[:, dt * 512:(dt + 1) * 512]
                    for f in range(32):
                        w = ([wt, ("dv", gate_dv), ("ac", C["ac"])]
                             if f == 0 else [])
                        s.mm(ps[:, 0:TB], wob[:, f * 256 + dt * 128: f * 256 + dt * 128 + 128],
                             htil[:, f * TB:(f + 1) * TB], start=(f == 0),
                             stop=(f == 31), waits=w, inc="pe" if f == 31 else None)
                    pev = C["pe"]
                    s.emit("scalar", lambda e, p=ps: e.activation(
                        out=aofb[:], in_=p[:, 0:TB], func=AF.Copy),
                        waits=[("pe", pev)], inc="ac")
                    acv = C["ac"]
                    dcol = dq * 256 + dt * 128
                    for st in range(4):
                        s.mm(s.psW2[:, st * 128:(st + 1) * 128],
                             aofb[:, st * 128:(st + 1) * 128], s.ident[:],
                             waits=[("ac", acv), ("dv", C["dv"])] if st == 0 else [],
                             inc="pe" if st == 3 else None)
                    pev = C["pe"]
                    for st in range(4):
                        s.emit("vector", lambda e, st=st, dc=dcol: e.tensor_add(
                            out=xstgB[st][:, dc:dc + 128],
                            in0=s.psW2[:, st * 128:(st + 1) * 128],
                            in1=xstgB[st][:, dc:dc + 128]),
                            waits=[("pe", pev)] if st == 0 else [], inc="dv")
            res_dv = C["dv"]
            if s.dev:
                for st in range(4):
                    s.dma("gpsimd", lambda e, st=st, t=tok0: e.dma_start(
                        out=s.x2dump[t + st * 128: t + (st + 1) * 128, :],
                        in_=xstgB[st][:]), "qh", waits=[("dv", res_dv)])

            for sub in range(4):
                s.ln_tile2(xstgB[sub], lncB, xlnbB, sqBB, res_dv, LNEPS)
                s.transpose16(
                    xlnbB,
                    lambda c, su=sub: xln2T[:, c * TB + su * 128: c * TB + (su + 1) * 128],
                    0, s.psW2[:, 0:512])
            x2_ac = C["ac"]

            for gi in range(64):
                bg = wbufB[gi % 2]
                bu2 = wbufB2[gi % 2]
                w_bu = [("pe", zf_pe[("g", gi - 2)])] if ("g", gi - 2) in zf_pe else []
                srcg = s.wguh_d.rearrange("(ct p) f -> p ct f", p=128)[
                    :, :, gi * 128:(gi + 1) * 128]
                srcu = s.wguh_d.rearrange("(ct p) f -> p ct f", p=128)[
                    :, :, 8192 + gi * 128: 8192 + (gi + 1) * 128]
                wt1 = s.dma("sync", lambda e, a=srcg, b=bg: e.dma_start(
                    out=b[:, 0:2048].rearrange("p (ct f) -> p ct f", ct=16), in_=a),
                    "qw", waits=w_bu)
                wt2 = s.dma("sync", lambda e, a=srcu, b=bu2: e.dma_start(
                    out=b[:, 0:2048].rearrange("p (ct f) -> p ct f", ct=16), in_=a),
                    "qw")
                for c in range(16):
                    w = [wt1, wt2, ("ac", max(x2_ac, C["ac"])),
                         ("dv", C["dv"])] if c == 0 else []
                    s.mm(s.psW[:, 0:TB], bg[:, c * 128:(c + 1) * 128],
                         xln2T[:, c * TB:(c + 1) * TB], start=(c == 0), stop=(c == 15),
                         waits=w)
                    s.mm(s.psW[:, 512:512 + TB], bu2[:, c * 128:(c + 1) * 128],
                         xln2T[:, c * TB:(c + 1) * TB], start=(c == 0), stop=(c == 15),
                         inc="pe" if c == 15 else None)
                zf_pe[("g", gi)] = C["pe"]
                pev = C["pe"]
                s.emit("scalar", lambda e: e.activation(out=aofb[:], in_=s.psW[:, 0:512],
                                                        func=AF.Silu),
                       waits=[("pe", pev), ("dv", C["dv"])], inc="ac")
                s.emit("vector", lambda e, g=gi: e.tensor_mul(
                    out=hres[:, g * TB:(g + 1) * TB], in0=aofb[:],
                    in1=s.psW[:, 512:1024]), waits=["ac"], inc="dv")
                s.emit("scalar", lambda e, g=gi: e.activation(
                    out=sqBB[:, 0:512], in_=hres[:, g * TB:(g + 1) * TB],
                    func=AF.Square), waits=["dv", ("pe", C["pe"])], inc="ac")
                s.mm(s.psW2[0:1, 0:512], s.onescol[:], sqBB[:, 0:512],
                     start=(gi == 0), stop=(gi == 63), waits=[("ac", C["ac"])],
                     inc="pe" if gi == 63 else None)
            pev = C["pe"]
            s.emit("scalar", lambda e: e.activation(out=s.rowstage[0:1, 0:512],
                                                    in_=s.psW2[0:1, 0:512],
                                                    func=AF.Copy),
                   waits=[("pe", pev)], inc="ac")
            acv = C["ac"]
            for st in range(4):
                s.mm(s.psW2[:, 512 + st: 513 + st],
                     s.rowstage[0:1, st * 128:(st + 1) * 128], s.onesrowb[0:1, 0:1],
                     waits=[("ac", acv)] if st == 0 else [],
                     inc="pe" if st == 3 else None)
            pev = C["pe"]
            s.emit("scalar", lambda e: e.activation(out=rstdc[:, 0:4],
                                                    in_=s.psW2[:, 512:516], func=AF.Ln,
                                                    scale=1.0 / 8192, bias=F32EPS),
                   waits=[("pe", pev)], inc="ac")
            s.emit("scalar", lambda e: e.activation(out=rstdc[:, 4:8],
                                                    in_=rstdc[:, 0:4], func=AF.Exp,
                                                    scale=-0.5), inc="ac")
            rstd_ac = C["ac"]

            if s.dev and bt == 0:
                s.dma("gpsimd", lambda e: e.dma_start(out=s.hrdump, in_=hres[:]),
                      "qh", waits=[("dv", C["dv"]), ("ac", C["ac"])])
                s.dma("gpsimd", lambda e: e.dma_start(out=s.xl2dump, in_=xln2T[:]),
                      "qh")
            for dt16 in range(16):
                src = s.wph_d.rearrange("(ht p) d -> p ht d", p=128)[
                    :, :, dt16 * 128:(dt16 + 1) * 128]
                wt = s.dma("sync", lambda e, a=src: e.dma_start(
                    out=wpb[:].rearrange("p (ht f) -> p ht f", ht=64), in_=a),
                    "qw", waits=[("pe", C["pe"])])
                for hf in range(64):
                    w = [wt, ("ac", rstd_ac)] if hf == 0 else []
                    s.mm(s.psW[:, 0:TB], wpb[:, hf * 128:(hf + 1) * 128],
                         hres[:, hf * TB:(hf + 1) * TB], start=(hf == 0),
                         stop=(hf == 63), waits=w, inc="pe" if hf == 63 else None)
                pev = C["pe"]
                s.emit("scalar", lambda e: e.activation(out=aofb[:], in_=s.psW[:, 0:512],
                                                        func=AF.Copy),
                       waits=[("pe", pev)], inc="ac")
                acv = C["ac"]
                for st in range(4):
                    s.mm(s.psW2[:, 1024 + st * 128: 1024 + (st + 1) * 128],
                         aofb[:, st * 128:(st + 1) * 128], s.ident[:],
                         waits=[("ac", acv), ("dv", C["dv"])] if st == 0 else [],
                         inc="pe" if st == 3 else None)
                pev = C["pe"]
                for st in range(4):
                    ob = outst[st % 2]
                    w = [("pe", pev)]
                    if len(s.out_toks) >= 2:
                        w.append(s.out_toks[-2])
                    s.emit("vector", lambda e, st=st, d=dt16, o=ob: e.scalar_tensor_tensor(
                        out=o[:], in0=s.psW2[:, 1024 + st * 128: 1024 + (st + 1) * 128],
                        scalar=rstdc[:, 4 + st:5 + st],
                        in1=xstgB[st][:, d * 128:(d + 1) * 128],
                        op0=ALU.mult, op1=ALU.add), waits=w, inc="dv")
                    dvv = C["dv"]
                    s.out_toks.append(s.dma(
                        "gpsimd", lambda e, st=st, d=dt16, t=tok0, o=ob: e.dma_start(
                            out=s.out[t + st * 128: t + (st + 1) * 128,
                                      d * 128:(d + 1) * 128], in_=o[:]),
                        "qo", waits=[("dv", dvv)]))
            hload_pe = C["pe"]

        s.emit("gpsimd", lambda e: None, waits=s.qall("qo"), inc="po")
        bctx.close()

    def ln_tile2(self, xb, lnc, xout, sq_t, dvw, eps):
        s = self
        s.emit("vector", lambda e: e.reduce_sum(out=lnc[:, 0:1], in_=xb[:], axis=AX),
               waits=[("dv", dvw)], inc="dv")
        s.emit("scalar", lambda e: e.activation(out=sq_t[:], in_=xb[:], func=AF.Square),
               waits=[("dv", dvw)], inc="ac")
        s.emit("vector", lambda e: e.reduce_sum(out=lnc[:, 1:2], in_=sq_t[:], axis=AX),
               waits=["ac"], inc="dv")
        s.emit("vector", lambda e: e.tensor_scalar_mul(out=lnc[:, 2:3], in0=lnc[:, 0:1],
                                                       scalar1=1.0 / D), inc="dv", dr=True)
        s.emit("vector", lambda e: e.tensor_mul(out=lnc[:, 3:4], in0=lnc[:, 2:3],
                                                in1=lnc[:, 2:3]), inc="dv", dr=True)
        s.emit("vector", lambda e: e.scalar_tensor_tensor(
            out=lnc[:, 4:5], in0=lnc[:, 1:2], scalar=1.0 / D, in1=lnc[:, 3:4],
            op0=ALU.mult, op1=ALU.subtract), inc="dv")
        s.emit("scalar", lambda e: e.activation(out=lnc[:, 5:6], in_=lnc[:, 4:5],
                                                func=AF.Ln, bias=eps),
               waits=["dv"], inc="ac")
        s.emit("scalar", lambda e: e.activation(out=lnc[:, 6:7], in_=lnc[:, 5:6],
                                                func=AF.Exp, scale=-0.5), inc="ac")
        s.emit("vector", lambda e: e.scalar_tensor_tensor(
            out=lnc[:, 7:8], in0=lnc[:, 2:3], scalar=-1.0, in1=lnc[:, 6:7],
            op0=ALU.mult, op1=ALU.mult), waits=["ac"], inc="dv")
        s.emit("scalar", lambda e: e.activation(out=xout[:], in_=xb[:],
                                                func=AF.Identity,
                                                bias=lnc[:, 7:8], scale=lnc[:, 6:7]),
               waits=["dv"], inc="ac")


_NC_CACHE = {}


def _get_nc(dev=False, stop_a=False, nta_lim=NTA, lvl=99):
    key = (dev, stop_a, nta_lim, lvl)
    if key not in _NC_CACHE:
        _NC_CACHE[key] = KB(dev_dump=dev, stop_after_a=stop_a,
                            nta_lim=nta_lim, lvl=lvl).build()
    return _NC_CACHE[key]


def _core_inputs(inputs, b, hf):
    x = np.ascontiguousarray(inputs["L0_attn_x"][b])
    Wqkvz = inputs["L0_attn_W_qkvz"]
    Wba = inputs["L0_attn_W_ba"]
    hq = range(8 * hf, 8 * hf + 8)
    hv = list(range(16 * hf, 16 * hf + 16))
    rows_q = np.concatenate([np.arange(768 * h, 768 * h + 128) for h in hq])
    rows_k = np.concatenate([np.arange(768 * h + 128, 768 * h + 256) for h in hq])
    rows_v = np.concatenate([np.arange(768 * h + 256, 768 * h + 512) for h in hq])
    rows_qkv = np.concatenate([rows_q, rows_k, rows_v])
    rows_ba = np.concatenate([[4 * h, 4 * h + 1] for h in hq]
                             + [[4 * h + 2, 4 * h + 3] for h in hq])
    rows_z = np.concatenate([np.arange(768 * h + 512, 768 * h + 768)
                             for h in range(16)])
    ch_q = np.concatenate([np.arange(128 * h, 128 * h + 128) for h in hq])
    ch = np.concatenate([ch_q, 2048 + ch_q,
                         np.concatenate([4096 + np.arange(256 * h, 256 * h + 256)
                                         for h in hq])])
    return {
        "x": x,
        "xh": np.ascontiguousarray(x[hf * 4096:(hf + 1) * 4096]),
        "wqkvT": np.ascontiguousarray(Wqkvz[rows_qkv].T),
        "wbaT": np.ascontiguousarray(Wba[rows_ba].T),
        "convw": np.ascontiguousarray(inputs["L0_attn_conv_w"][ch]),
        "dtb": np.ascontiguousarray(inputs["L0_attn_dt_bias"][hv][None, :]),
        "alog": np.ascontiguousarray(inputs["L0_attn_A_log"][hv][None, :]),
        "normw": np.ascontiguousarray(inputs["L0_attn_norm_w"][:, None]),
        "wzT": np.ascontiguousarray(Wqkvz[rows_z].T),
        "woutT": np.ascontiguousarray(inputs["L0_attn_W_out"].T),
        "wgu": np.ascontiguousarray(inputs["L0_ffn_w_gate_up"]),
        "wp": np.ascontiguousarray(inputs["L0_ffn_w_proj"]),
        "hoff": np.array([[hf * NTB]], dtype=np.uint32),
    }


def kernel(**inputs):
    nc = _get_nc(False)
    in_maps = []
    for c in range(8):
        in_maps.append(_core_inputs(inputs, c // 2, c % 2))
    res = run_bass_kernel_spmd(nc, in_maps, core_ids=list(range(8)))
    out = np.empty((B, S, D), np.float32)
    for c in range(8):
        b, hf = c // 2, c % 2
        out[b, hf * 4096:(hf + 1) * 4096] = res.results[c]["out"]
    return out


# version 03
# speedup vs baseline: 1.0575x; 1.0411x over previous
import contextlib

import numpy as np
import ml_dtypes

import concourse.bass as bass
import concourse.mybir as mybir
from concourse.bass_utils import run_bass_kernel_spmd

F32 = mybir.dt.float32
BF16 = mybir.dt.bfloat16
U32 = mybir.dt.uint32
AF = mybir.ActivationFunctionType
ALU = mybir.AluOpType
AX = mybir.AxisListType.X
BF = ml_dtypes.bfloat16
AP = bass.AP

B, S, D = 4, 8192, 2048
HD = 128
NHQ, NHV = 8, 16
CH = 128
NCH = S // CH
TA = 512
NTA = S // TA
SEG = 1024
NSEG = S // SEG
TB = 512
NTB = (S // 2) // TB
FQKV = 4096
NF = FQKV // 128
EPS = 1e-6
LNEPS = 1e-5
F32EPS = float(np.finfo(np.float32).eps)
MNEG = -1.0e9
LN128 = float(np.log(128.0))

ENGINES = ("sync", "tensor", "vector", "scalar", "gpsimd")
DR_ALL = False


class KB:
    def __init__(self, dev_dump=False, stop_after_a=False, nta_lim=NTA, lvl=99):
        self.dev = dev_dump
        self.stop_a = stop_after_a
        self.nta_lim = nta_lim
        self.lvl = lvl
        self.nc = bass.Bass("TRN2", target_bir_lowering=False, debug=False,
                            num_devices=8, detect_race_conditions=False)
        self.ops = {e: [] for e in ENGINES}
        names = ["pe", "dv", "ac", "po"]
        for cls in ("qx", "qw", "qh", "qb", "qo"):
            names += [cls + str(i) for i in range(4)]
        names += ["cc" + str(i) for i in range(NSEG)]
        self.count = {s: 0 for s in names}
        self.rot = {cls: 0 for cls in ("qx", "qw", "qh", "qb", "qo")}
        self.sems = {}
        self.octx = contextlib.ExitStack()

    def emit(self, eng, fn, waits=(), inc=None, amt=1, dr=False):
        wl = []
        for w in waits:
            if isinstance(w, tuple):
                if w[1] > 0:
                    wl.append(w)
            elif self.count[w] > 0:
                wl.append((w, self.count[w]))
        if inc is not None:
            self.count[inc] += amt
            self.ops[eng].append((fn, wl, (inc, amt), dr))
        else:
            self.ops[eng].append((fn, wl, None, dr))

    def dma(self, eng, fn, cls, waits=()):
        i = self.rot[cls]
        self.rot[cls] = (i + 1) % 4
        sem = cls + str(i)
        sw = [(sem, self.count[sem])] if self.count[sem] > 0 else []
        self.emit(eng, fn, waits=list(waits) + sw, inc=sem, amt=16)
        return (sem, self.count[sem])

    def qall(self, cls):
        return [(cls + str(i), self.count[cls + str(i)]) for i in range(4)
                if self.count[cls + str(i)] > 0]

    def mm(self, out_ap, lhsT, rhs, start=True, stop=True, waits=(), inc=None):
        self.emit("tensor",
                  lambda e, o=out_ap, l=lhsT, r=rhs, s1=start, s2=stop:
                  e.matmul(o, l, r, start=s1, stop=s2), waits=waits, inc=inc)

    def realize(self):
        block = self.octx.enter_context(self.nc.Block())
        sems = self.sems

        def make_body(eng_name):
            ops = self.ops[eng_name]

            def body(eng):
                wm = {}
                for fn, waits, inc, dr in ops:
                    for sn, v in waits:
                        if wm.get(sn, 0) < v:
                            eng.wait_ge(sems[sn], v)
                            wm[sn] = v
                    ins = fn(eng)
                    if dr or (eng_name == "vector" and DR_ALL):
                        dins = eng.drain()
                        if ins is not None:
                            ins = dins
                    if inc is not None:
                        sn, a = inc
                        if ins is not None:
                            ins.then_inc(sems[sn], a)
                        else:
                            eng.sem_inc(sems[sn], a)

            return body

        block.sync(make_body("sync"))
        block.tensor(make_body("tensor"))
        block.vector(make_body("vector"))
        block.scalar(make_body("scalar"))
        block.gpsimd(make_body("gpsimd"))

    @staticmethod
    def r3(ap_):
        return ap_.rearrange("p (h c) -> p h c", c=128)

    @staticmethod
    def r4(ap_):
        return ap_.rearrange("p (h d c) -> p h d c", d=2, c=128)

    @staticmethod
    def bc_ap(t, p0, plen, base, dims):
        a = t[p0:p0 + plen, base:base + 1]
        return AP(a.tensor, a.offset, [list(a.ap[0])] + [list(d) for d in dims])

    def dupq4(self, t, base, hstep):
        return self.bc_ap(t, 0, 128, base, [[hstep, NHQ], [0, 2], [1, 128]])

    def colbc(self, t, nh, rep, base=0):
        return self.bc_ap(t, 0, 128, base, [[1, nh], [0, rep]])

    def coldup(self, t, base=0):
        return self.bc_ap(t, 0, 128, base, [[1, NHQ], [0, 2]])

    def build(self):
        nc = self.nc
        dt = nc.dram_tensor
        s = self
        s.x_in = dt("x", [S, D], F32, kind="ExternalInput").ap()
        s.xh_in = dt("xh", [S // 2, D], F32, kind="ExternalInput").ap()
        s.wqkvT = dt("wqkvT", [D, FQKV], F32, kind="ExternalInput").ap()
        s.wbaT = dt("wbaT", [D, 32], F32, kind="ExternalInput").ap()
        s.convw_in = dt("convw", [FQKV, 4], F32, kind="ExternalInput").ap()
        s.dtb_in = dt("dtb", [1, NHV], F32, kind="ExternalInput").ap()
        s.alog_in = dt("alog", [1, NHV], F32, kind="ExternalInput").ap()
        s.normw_in = dt("normw", [HD, 1], F32, kind="ExternalInput").ap()
        s.wzT_in = dt("wzT", [D, 4096], F32, kind="ExternalInput").ap()
        s.woutT_in = dt("woutT", [4096, D], F32, kind="ExternalInput").ap()
        s.wgu_in = dt("wgu", [16384, D], F32, kind="ExternalInput").ap()
        s.wp_in = dt("wp", [D, 8192], F32, kind="ExternalInput").ap()
        s.hoff_in = dt("hoff", [1, 1], U32, kind="ExternalInput").ap()
        s.out = dt("out", [S // 2, D], F32, kind="ExternalOutput").ap()

        s.wqkv_bf = dt("wqkv_bf", [D, FQKV], BF16, kind="Internal").ap()
        s.wba_bf = dt("wba_bf", [D, 32], BF16, kind="Internal").ap()
        s.wz_bf = dt("wz_bf", [D, 4096], BF16, kind="Internal").ap()
        s.wout_bf = dt("wout_bf", [4096, D], BF16, kind="Internal").ap()
        s.wguh_d = dt("wguh", [D, 16384], BF16, kind="Internal").ap()
        s.wph_d = dt("wph", [8192, D], BF16, kind="Internal").ap()
        s.hseg = [dt(f"hseg{i}", [2048, SEG], BF16, kind="Internal").ap()
                  for i in range(NSEG)]
        s.gseg = [dt(f"gseg{i}", [4096, SEG], BF16, kind="Internal").ap()
                  for i in range(NSEG)]
        s.bigH = dt("bigH", [4096, S], BF16, kind="Internal").ap()
        s.bounce = dt("bounce", [CH, 144], BF16, kind="Internal").ap()
        if s.dev:
            s.hdump = dt("hdump", [2048, S], BF16, kind="ExternalOutput").ap()
            s.x2dump = dt("x2dump", [S // 2, D], F32, kind="ExternalOutput").ap()
            s.hrdump = dt("hrdump", [128, 64 * TB], BF16, kind="ExternalOutput").ap()
            s.xl2dump = dt("xl2dump", [128, 16 * TB], BF16, kind="ExternalOutput").ap()
            s.wgudump = dt("wgudump", [D, 16384], BF16, kind="ExternalOutput").ap()

        for name in s.count:
            s.sems[name] = s.octx.enter_context(nc.semaphore(name))
        sb = lambda n, sh, d: s.octx.enter_context(nc.sbuf_tensor(n, sh, d))
        s.psW = s.octx.enter_context(nc.psum_tensor("psW", [128, 2048], F32))
        s.psW2 = s.octx.enter_context(nc.psum_tensor("psW2", [128, 2048], F32))

        s.ident = sb("ident", [128, 128], BF16)
        s.identf = sb("identf", [128, 128], F32)
        s.uincl = sb("uincl", [128, 128], F32)
        s.onescol = sb("onescol", [128, 1], BF16)
        s.onescolf = sb("onescolf", [128, 1], F32)
        s.onesrow = sb("onesrow", [1, 128], F32)
        s.onesrowb = sb("onesrowb", [1, 128], BF16)
        s.maskL = sb("maskL", [128, 512], BF16)
        s.maskU = sb("maskU", [128, 512], BF16)
        s.maskUI = sb("maskUI", [128, 512], BF16)
        s.normw_c = sb("normw_c", [HD, 1], F32)
        s.dtbB = sb("dtbB", [128, NHV], F32)
        s.neaB = sb("neaB", [128, NHV], F32)
        s.rowstage = sb("rowstage", [65, 2048], BF16)
        s.ones65 = sb("ones65", [65, 128], BF16)
        s.tinyf = sb("tinyf", [128, 96], F32)
        s.constc = sb("constc", [128, 8], F32)

        for i, v in enumerate([LNEPS, 1.0, EPS, F32EPS]):
            nc.const_aps.register(F32, float(v), s.constc[:, i:i + 1])
        s.setup_consts()
        s.phase_a()
        if s.stop_a:
            ccw = [("cc%d" % i, s.count["cc%d" % i]) for i in range(NSEG)
                   if s.count["cc%d" % i] > 0]
            s.emit("gpsimd", lambda e: None,
                   waits=s.qall("qh") + s.qall("qb") + ccw, inc="po")
        else:
            s.phase_b()
        s.realize()
        s.octx.close()
        return nc

    def setup_consts(self):
        s, C = self, self.count

        def c_memsets(e):
            e.memset(s.ident[:], 1.0)
            e.memset(s.identf[:], 1.0)
            e.memset(s.uincl[:], 1.0)
            e.memset(s.onescol[:], 1.0)
            e.memset(s.onescolf[:], 1.0)
            e.memset(s.onesrow[:], 1.0)
            e.memset(s.onesrowb[:], 1.0)
            e.memset(s.ones65[:], 1.0)
            e.memset(s.constc[:, 0:1], LNEPS)
            e.memset(s.constc[:, 1:2], 1.0)
            e.memset(s.constc[:, 2:3], EPS)
            e.memset(s.constc[:, 3:4], F32EPS)
            e.memset(s.maskL[:], 0.0)
            e.memset(s.maskU[:], 0.0)
            e.memset(s.maskUI[:], 0.0)
            return e.engine_nop()

        def c_selects(e):
            e.affine_select(out=s.ident[:], in_=s.ident[:], compare_op=ALU.is_equal,
                            fill=0.0, base=0, pattern=[[-1, 128]], channel_multiplier=1)
            e.affine_select(out=s.identf[:], in_=s.identf[:], compare_op=ALU.is_equal,
                            fill=0.0, base=0, pattern=[[-1, 128]], channel_multiplier=1)
            e.affine_select(out=s.uincl[:], in_=s.uincl[:], compare_op=ALU.is_ge,
                            fill=0.0, base=0, pattern=[[1, 128]], channel_multiplier=-1)
            for r in range(4):
                sl = slice(r * 128, r * 128 + 128)
                e.affine_select(out=s.maskL[:, sl], in_=s.maskL[:, sl],
                                compare_op=ALU.is_ge, fill=MNEG, base=-1,
                                pattern=[[-1, 128]], channel_multiplier=1)
                e.affine_select(out=s.maskU[:, sl], in_=s.maskU[:, sl],
                                compare_op=ALU.is_ge, fill=MNEG, base=-1,
                                pattern=[[1, 128]], channel_multiplier=-1)
                e.affine_select(out=s.maskUI[:, sl], in_=s.maskUI[:, sl],
                                compare_op=ALU.is_ge, fill=MNEG, base=0,
                                pattern=[[1, 128]], channel_multiplier=-1)
            return e.engine_nop()

        s.emit("gpsimd", c_memsets, inc="po")
        s.emit("gpsimd", c_selects, waits=["po"], inc="po")
        t1 = s.dma("sync", lambda e: e.dma_start(out=s.normw_c[:], in_=s.normw_in), "qx")
        t2 = s.dma("sync", lambda e: e.dma_start(out=s.tinyf[0:1, 0:NHV], in_=s.dtb_in), "qx")
        t3 = s.dma("sync", lambda e: e.dma_start(out=s.tinyf[0:1, 16:32], in_=s.alog_in), "qx")
        s.mm(s.psW[:, 0:NHV], s.onesrow[:], s.tinyf[0:1, 0:NHV],
             waits=[t1, t2, t3, "po"])
        s.mm(s.psW[:, 128:128 + NHV], s.onesrow[:], s.tinyf[0:1, 16:32], inc="pe")
        pev = C["pe"]
        s.emit("vector", lambda e: e.tensor_copy(out=s.dtbB[:], in_=s.psW[:, 0:NHV]),
               waits=[("pe", pev)], inc="dv", dr=True)
        s.emit("scalar", lambda e: e.activation(out=s.neaB[:],
                                                in_=s.psW[:, 128:128 + NHV],
                                                func=AF.Exp),
               waits=[("pe", pev)], inc="ac")
        s.emit("vector", lambda e: e.tensor_scalar_mul(out=s.neaB[:], in0=s.neaB[:],
                                                       scalar1=-1.0),
               waits=["ac"], inc="dv", dr=True)

    def cvt(self, src_ap, dst_ap, cols):
        s, C = self, self.count
        i = s.cvt_i
        bu = s.wbuf[i % 2]
        w_st = [("ac", s.cvt_ac[-1])] if s.cvt_ac else []
        w_bu = [s.cvt_qh[-2]] if len(s.cvt_qh) >= 2 else []
        tl = s.dma("sync", lambda e, a=src_ap, c=cols: e.dma_start(
            out=s.wstg[:, 0:c], in_=a), "qw", waits=w_st)
        s.emit("scalar", lambda e, b=bu, c=cols: e.activation(
            out=b[:, 0:c], in_=s.wstg[:, 0:c], func=AF.Copy),
            waits=[tl] + w_bu, inc="ac")
        s.cvt_ac.append(C["ac"])
        av = C["ac"]
        ts = s.dma("gpsimd", lambda e, b=bu, d=dst_ap, c=cols: e.dma_start(
            out=d, in_=b[:, 0:c]), "qh", waits=[("ac", av)])
        s.cvt_qh.append(ts)
        s.cvt_i += 1

    def ln_tile(self, xb, lnc, xout, sq_t, tok, eps):
        s = self
        s.emit("vector", lambda e: e.reduce_sum(out=lnc[:, 0:1], in_=xb[:], axis=AX),
               waits=[tok], inc="dv")
        s.emit("scalar", lambda e: e.activation(out=sq_t[:], in_=xb[:], func=AF.Square),
               waits=[tok], inc="ac")
        s.emit("vector", lambda e: e.reduce_sum(out=lnc[:, 1:2], in_=sq_t[:], axis=AX),
               waits=["ac"], inc="dv")
        s.emit("vector", lambda e: e.tensor_scalar_mul(out=lnc[:, 2:3], in0=lnc[:, 0:1],
                                                       scalar1=1.0 / D), inc="dv", dr=True)
        s.emit("vector", lambda e: e.tensor_mul(out=lnc[:, 3:4], in0=lnc[:, 2:3],
                                                in1=lnc[:, 2:3]), inc="dv", dr=True)
        s.emit("vector", lambda e: e.scalar_tensor_tensor(
            out=lnc[:, 4:5], in0=lnc[:, 1:2], scalar=1.0 / D, in1=lnc[:, 3:4],
            op0=ALU.mult, op1=ALU.subtract), inc="dv")
        s.emit("scalar", lambda e: e.activation(out=lnc[:, 5:6], in_=lnc[:, 4:5],
                                                func=AF.Ln, bias=eps),
               waits=["dv"], inc="ac")
        s.emit("scalar", lambda e: e.activation(out=lnc[:, 6:7], in_=lnc[:, 5:6],
                                                func=AF.Exp, scale=-0.5), inc="ac")
        s.emit("vector", lambda e: e.scalar_tensor_tensor(
            out=lnc[:, 7:8], in0=lnc[:, 2:3], scalar=-1.0, in1=lnc[:, 6:7],
            op0=ALU.mult, op1=ALU.mult), waits=["ac"], inc="dv")
        s.emit("scalar", lambda e: e.activation(out=xout[:], in_=xb[:],
                                                func=AF.Identity,
                                                bias=lnc[:, 7:8], scale=lnc[:, 6:7]),
               waits=["dv"], inc="ac")

    def transpose16b(self, src_bf, dst, ps, evac_extra):
        s, C = self, self.count
        for cg in range(4):
            w0 = [("ac", C["ac"]), ("dv", C["dv"]), ("po", C["po"])]
            for k in range(4):
                c = cg * 4 + k
                s.mm(ps[:, k * 128:(k + 1) * 128], src_bf[:, c * 128:(c + 1) * 128],
                     s.ident[:], waits=w0, inc="pe" if k == 3 else None)
                w0 = []
            pev = C["pe"]
            for k in range(4):
                c = cg * 4 + k
                s.emit("scalar", lambda e, c=c, k=k, d=dst: e.activation(
                    out=d(c), in_=ps[:, k * 128:(k + 1) * 128], func=AF.Copy),
                    waits=[("pe", pev)] + (evac_extra if cg == 0 and k == 0 else []),
                    inc="ac")

    def transpose16(self, src_bf, dst, dst_col, ps):
        s, C = self, self.count
        for cg in range(4):
            w0 = [("ac", C["ac"]), ("dv", C["dv"]), ("po", C["po"])]
            for k in range(4):
                c = cg * 4 + k
                s.mm(ps[:, k * 128:(k + 1) * 128], src_bf[:, c * 128:(c + 1) * 128],
                     s.ident[:], waits=w0, inc="pe" if k == 3 else None)
                w0 = []
            pev = C["pe"]
            for k in range(4):
                c = cg * 4 + k
                s.emit("scalar", lambda e, c=c, k=k: e.activation(
                    out=dst(c), in_=ps[:, k * 128:(k + 1) * 128], func=AF.Copy),
                    waits=[("pe", pev)], inc="ac")

    def phase_a(self):
        s, nc, C = self, self.nc, self.count
        actx = contextlib.ExitStack()
        sa = lambda n, sh, d: actx.enter_context(nc.sbuf_tensor(n, sh, d))

        s.wstg = sa("wstg", [128, 2048], F32)
        s.wbuf = [sa(f"wbuf{i}", [128, 2048], BF16) for i in range(2)]
        s.cwcol = sa("cwcol", [128, 128], F32)
        s.xstg = sa("xstg", [128, D], F32)
        s.lncol = sa("lncol", [128, 8], F32)
        s.xlnT = sa("xlnT", [128, 16 * TA], BF16)
        s.prest = sa("prest", [128, 32 * (TA + 4)], BF16)
        s.convacc = sa("convacc", [128, TA], F32)
        s.act = sa("act", [128, 32 * TA], BF16)
        s.bast = sa("bast", [128, TA], F32)
        s.batok = sa("batok", [128, 32], F32)
        s.gcol = sa("gcol", [128, NHV], F32)
        s.bcol = sa("bcol", [128, NHV], F32)
        s.gccol = sa("gccol", [128, NHV], F32)
        s.glastb = sa("glastb", [128, NHV], F32)
        s.gamcol = sa("gamcol", [128, NHV], F32)
        s.lnsk = sa("lnsk", [128, NHQ], F32)
        s.lnsq = sa("lnsq", [128, NHQ], F32)
        s.cmt = [sa(f"cm{v}", [128, NHV], F32) for v in range(3)]
        s.r3col = sa("r3col", [128, NHV], BF16)
        s.nr2col = sa("nr2col", [128, NHV], F32)
        s.qtok = sa("qtok", [128, NHQ * 128], BF16)
        s.ktok = sa("ktok", [128, NHQ * 128], BF16)
        s.vbtok = sa("vbtok", [128, NHV * 128], BF16)
        s.sqsc = sa("sqsc", [128, NHV * 128], BF16)
        s.msqq = sa("msqq", [128, NHQ], F32)
        s.msqk = sa("msqk", [128, NHQ], F32)
        s.dec = [sa(f"dec{v}", [128, NHV * 128], BF16) for v in range(3)]
        s.Mst = sa("Mst", [128, NHV * 128], BF16)
        s.Ust = sa("Ust", [128, NHV * 128], BF16)
        s.Pw = [sa(f"Pw{i}", [128, NHV * 128], BF16) for i in range(2)]
        s.Uw = [sa(f"Uw{i}", [128, NHV * 128], BF16) for i in range(2)]
        s.sqg = sa("sqg", [128, NHV * 128], BF16)
        s.qs = sa("qs", [128, NHV * 128], BF16)
        s.zst = sa("zst", [128, NHV * 128], BF16)
        s.yst = [sa(f"yst{i}", [128, NHV * 128], BF16) for i in range(2)]
        s.oib = sa("oib", [128, NHV * 128], BF16)
        s.rstf = sa("rstf", [128, NHV * 64], F32)
        s.Sst = sa("Sst", [128, NHV * 128], F32)
        s.Sbf = sa("Sbf", [128, NHV * 128], BF16)
        s.glrow = sa("glrow", [1, NHV], F32)

        s.emit("gpsimd", lambda e: e.memset(s.Sst[:], 0.0), waits=["po"], inc="po")
        s.emit("gpsimd", lambda e: e.memset(s.Sbf[:], 0.0), inc="po")
        s.emit("gpsimd", lambda e: e.memset(s.prest[:], 0.0), inc="po")
        s.po_init = C["po"]
        s.cw_tok = s.dma("sync", lambda e: e.dma_start(
            out=s.cwcol[:].rearrange("p (t j) -> p t j", j=4),
            in_=s.convw_in.rearrange("(t p) j -> p t j", p=128)), "qx")

        s.cvt_i, s.cvt_ac, s.cvt_qh = 0, [], []
        for i in range(D // 128):
            for j in range(FQKV // 2048):
                s.cvt(s.wqkvT[i * 128:(i + 1) * 128, j * 2048:(j + 1) * 2048],
                      s.wqkv_bf[i * 128:(i + 1) * 128, j * 2048:(j + 1) * 2048], 2048)
        for i in range(D // 128):
            s.cvt(s.wbaT[i * 128:(i + 1) * 128, :], s.wba_bf[i * 128:(i + 1) * 128, :], 32)

        s.xstg_free = 0
        s.wbuf_pe = {}
        s.hseg_toks = []

        for ta in range(self.nta_lim):
            for sub in range(4):
                tok0 = ta * TA + sub * 128
                w_x = [("ac", s.xstg_free)] if s.xstg_free else []
                xt = s.dma("sync", lambda e, t=tok0: e.dma_start(
                    out=s.xstg[:], in_=s.x_in[t:t + 128, :]), "qx", waits=w_x)
                s.ln_tile(s.xstg, s.lncol, s.qs, s.sqsc[:, 0:D], xt, LNEPS)
                s.xstg_free = C["ac"]
                s.transpose16(
                    s.qs,
                    lambda c, su=sub: s.xlnT[:, c * TA + su * 128: c * TA + (su + 1) * 128],
                    0, s.psW2[:, 0:512])
            xlnT_ready = C["ac"]

            evac_half = [0, 0]
            for f in range(NF + 1):
                bu = s.wbuf[f % 2]
                w_bu = [("pe", s.wbuf_pe[f - 2])] if (f - 2) in s.wbuf_pe else []
                if ta == 0 and f < 2:
                    w_bu += [("ac", s.cvt_ac[-1]), s.cvt_qh[-1]]
                if f < NF:
                    src = s.wqkv_bf.rearrange("(ct p) f -> p ct f", p=128)[
                        :, :, f * 128:(f + 1) * 128]
                    ncol = 128
                else:
                    src = s.wba_bf.rearrange("(ct p) f -> p ct f", p=128)
                    ncol = 32
                wt = s.dma("sync", lambda e, a=src, b=bu, n=ncol: e.dma_start(
                    out=b[:].rearrange("p (ct f) -> p ct f", ct=16)[:, :, 0:n], in_=a),
                    "qw", waits=w_bu)
                half = f % 2
                ps = s.psW[:, half * 512: half * 512 + 512]
                for c in range(16):
                    w = ([wt, ("ac", max(xlnT_ready, evac_half[half]))]
                         if c == 0 else [])
                    if ta == 0 and f <= 1 and c == 0:
                        w.append(("dv", C["dv"]))
                    s.mm(ps[0:ncol, 0:TA], bu[:, c * 128:c * 128 + ncol],
                         s.xlnT[:, c * TA:(c + 1) * TA], start=(c == 0),
                         stop=(c == 15), waits=w, inc="pe" if c == 15 else None)
                s.wbuf_pe[f] = C["pe"]
                pev = C["pe"]
                if f < NF:
                    wtail = ([("dv", s.conv_tail_dv)]
                             if getattr(s, "conv_tail_dv", 0) else [])
                    s.emit("scalar", lambda e, f=f, p=ps: e.activation(
                        out=s.prest[:, f * (TA + 4) + 4: f * (TA + 4) + 4 + TA],
                        in_=p[:, 0:TA], func=AF.Copy),
                        waits=[("pe", pev)] + wtail, inc="ac")
                    evac_half[half] = C["ac"]
                else:
                    s.emit("vector", lambda e, p=ps: e.tensor_copy(
                        out=s.bast[0:32, 0:TA], in_=p[0:32, 0:TA]),
                        waits=[("pe", pev)], inc="dv")
            pre_ac = C["ac"]

            for f in range(NF if self.lvl >= 1 else 0):
                base = f * (TA + 4)
                w0 = ([("ac", pre_ac), s.cw_tok, ("dv", C["dv"]),
                       ("po", s.po_init)] if f == 0 else [])
                s.emit("scalar", lambda e, b=base, f=f: e.activation(
                    out=s.convacc[:], in_=s.prest[:, b + 1: b + 1 + TA],
                    func=AF.Identity,
                    scale=s.cwcol[:, f * 4: f * 4 + 1]), waits=w0, inc="ac")
                for j in (1, 2, 3):
                    s.emit("vector", lambda e, b=base, f=f, j=j: e.scalar_tensor_tensor(
                        out=s.convacc[:], in0=s.prest[:, b + 1 + j: b + 1 + j + TA],
                        scalar=s.cwcol[:, f * 4 + j: f * 4 + j + 1], in1=s.convacc[:],
                        op0=ALU.mult, op1=ALU.add),
                        waits=["ac"] if j == 1 else [], inc="dv")
                s.emit("scalar", lambda e, f=f: e.activation(
                    out=s.act[:, f * TA:(f + 1) * TA], in_=s.convacc[:], func=AF.Silu),
                    waits=["dv"], inc="ac")
                s.emit("vector", lambda e, b=base: e.tensor_copy(
                    out=s.prest[:, b + 1: b + 4],
                    in_=s.prest[:, b + 4 + TA - 3: b + 4 + TA]),
                    waits=["ac"], inc="dv")
            if f == NF - 1:
                s.conv_tail_dv = C["dv"]

            for sub in range(4):
                if self.lvl >= 2:
                    self.chunk(ta, sub)

        self.fence = ([(k, C[k]) for k in ("pe", "dv", "ac", "po")]
                      + s.qall("qx") + s.qall("qw") + s.qall("qh") + s.qall("qb"))
        actx.close()

    def chunk(self, ta, sub):
        s, C = self, self.count
        n = ta * 4 + sub
        off = sub * 128
        aq = lambda h: s.act[:, h * TA + off: h * TA + off + 128]
        ak = lambda h: s.act[:, (8 + h) * TA + off: (8 + h) * TA + off + 128]
        av = lambda ih: s.act[:, (16 + ih) * TA + off: (16 + ih) * TA + off + 128]
        T = s.tinyf

        s.mm(s.psW2[:, 0:32], s.bast[0:32, off:off + 128], s.identf[0:32, 0:32],
             waits=["dv", "ac"], inc="pe")
        pev = C["pe"]
        s.emit("vector", lambda e: e.tensor_copy(out=s.batok[:], in_=s.psW2[:, 0:32]),
               waits=[("pe", pev)], inc="dv", dr=True)
        s.emit("scalar", lambda e: e.activation(out=T[:, 32:48], in_=s.batok[:, 0:16],
                                                func=AF.Tanh, scale=0.5),
               waits=["dv"], inc="ac")
        s.emit("vector", lambda e: e.tensor_scalar(out=s.bcol[:], in0=T[:, 32:48],
                                                   scalar1=0.5, scalar2=0.5,
                                                   op0=ALU.mult, op1=ALU.add),
               waits=["ac"], inc="dv")
        s.emit("vector", lambda e: e.tensor_add(out=T[:, 0:16], in0=s.batok[:, 16:32],
                                                in1=s.dtbB[:]), inc="dv")
        s.emit("scalar", lambda e: e.activation(out=T[:, 16:32], in_=T[:, 0:16],
                                                func=AF.Abs), waits=["dv"], inc="ac")
        s.emit("scalar", lambda e: e.activation(out=T[:, 32:48], in_=T[:, 16:32],
                                                func=AF.Exp, scale=-1.0), inc="ac")
        s.emit("scalar", lambda e: e.activation(out=T[:, 48:64], in_=T[:, 32:48],
                                                func=AF.Ln, bias=1.0), inc="ac")
        s.emit("vector", lambda e: e.tensor_scalar_max(out=T[:, 64:80], in0=T[:, 0:16],
                                                       scalar1=0.0),
               waits=["ac"], inc="dv", dr=True)
        s.emit("vector", lambda e: e.tensor_add(out=T[:, 64:80], in0=T[:, 64:80],
                                                in1=T[:, 48:64]), inc="dv", dr=True)
        s.emit("vector", lambda e: e.tensor_mul(out=s.gcol[:], in0=T[:, 64:80],
                                                in1=s.neaB[:]), inc="dv")
        dvv = C["dv"]
        s.mm(s.psW2[:, 64:80], s.uincl[:], s.gcol[:], waits=[("dv", dvv)])
        s.mm(s.psW2[0:1, 96:112], s.onescolf[:], s.gcol[:], inc="pe")
        pev = C["pe"]
        s.emit("vector", lambda e: e.tensor_copy(out=s.gccol[:], in_=s.psW2[:, 64:80]),
               waits=[("pe", pev)], inc="dv")
        s.emit("scalar", lambda e: e.activation(out=s.glrow[:], in_=s.psW2[0:1, 96:112],
                                                func=AF.Copy),
               waits=[("pe", pev)], inc="ac")
        s.mm(s.psW2[:, 64:80], s.onesrow[:], s.glrow[:], waits=["ac", "dv"], inc="pe")
        pev = C["pe"]
        s.emit("vector", lambda e: e.tensor_copy(out=s.glastb[:], in_=s.psW2[:, 64:80]),
               waits=[("pe", pev)], inc="dv")
        s.emit("scalar", lambda e: e.activation(out=s.gamcol[:], in_=s.glastb[:],
                                                func=AF.Exp), waits=["dv"], inc="ac")

        if s.lvl < 3:
            return
        w0 = ["ac", "dv"]
        for h in range(NHQ):
            s.mm(s.psW[:, h * 128:(h + 1) * 128], aq(h), s.ident[:], waits=w0)
            w0 = []
            s.mm(s.psW[:, 1024 + h * 128: 1024 + (h + 1) * 128], ak(h), s.ident[:],
                 inc="pe" if h == NHQ - 1 else None)
        pev = C["pe"]
        s.emit("scalar", lambda e: e.activation(out=s.qtok[:], in_=s.psW[:, 0:1024],
                                                func=AF.Copy),
               waits=[("pe", pev)], inc="ac")
        s.emit("scalar", lambda e: e.activation(out=s.ktok[:], in_=s.psW[:, 1024:2048],
                                                func=AF.Copy), inc="ac")
        acv = C["ac"]
        for ih in range(NHV):
            s.mm(s.psW[:, ih * 128:(ih + 1) * 128], av(ih), s.ident[:],
                 waits=[("ac", acv)] if ih == 0 else [],
                 inc="pe" if ih == NHV - 1 else None)
        pev = C["pe"]
        for ih in range(NHV):
            s.emit("vector", lambda e, ih=ih: e.tensor_scalar_mul(
                out=s.vbtok[:, ih * 128:(ih + 1) * 128],
                in0=s.psW[:, ih * 128:(ih + 1) * 128],
                scalar1=s.bcol[:, ih:ih + 1]),
                waits=[("pe", pev)] if ih == 0 else [], inc="dv")
        vb_dv = C["dv"]

        s.emit("scalar", lambda e: e.activation(out=s.sqsc[:, 0:1024], in_=s.qtok[:],
                                                func=AF.Square), inc="ac")
        s.emit("vector", lambda e: e.reduce_sum(
            out=s.msqq[:], in_=s.sqsc[:, 0:1024].rearrange("p (h c) -> p h c", c=128),
            axis=AX), waits=["ac"], inc="dv")
        s.emit("scalar", lambda e: e.activation(out=s.sqsc[:, 1024:2048], in_=s.ktok[:],
                                                func=AF.Square), inc="ac")
        s.emit("vector", lambda e: e.reduce_sum(
            out=s.msqk[:], in_=s.sqsc[:, 1024:2048].rearrange("p (h c) -> p h c", c=128),
            axis=AX), waits=["ac"], inc="dv")
        s.emit("scalar", lambda e: e.activation(out=T[:, 0:8], in_=s.msqk[:], func=AF.Ln,
                                                scale=1.0 / 128, bias=EPS),
               waits=["dv"], inc="ac")
        s.emit("scalar", lambda e: e.activation(out=T[:, 8:16], in_=s.msqq[:],
                                                func=AF.Ln, scale=1.0 / 128, bias=EPS),
               inc="ac")
        s.emit("vector", lambda e: e.tensor_scalar(out=s.lnsk[:], in0=T[:, 0:8],
                                                   scalar1=-0.5, scalar2=-0.5 * LN128,
                                                   op0=ALU.mult, op1=ALU.add),
               waits=["ac"], inc="dv")
        s.emit("vector", lambda e: e.tensor_scalar(out=s.lnsq[:], in0=T[:, 8:16],
                                                   scalar1=-0.5, scalar2=-LN128,
                                                   op0=ALU.mult, op1=ALU.add), inc="dv")
        lnskd = s.coldup(s.lnsk)
        lnsqd = s.coldup(s.lnsq)
        rr2 = lambda x: x.rearrange("p (h d) -> p h d", d=2)
        s.emit("scalar", lambda e: e.activation(out=T[:, 16:32], in_=s.bcol[:],
                                                func=AF.Ln), waits=["dv"], inc="ac")
        s.emit("vector", lambda e: e.tensor_add(out=s.cmt[0][:], in0=s.gccol[:],
                                                in1=T[:, 16:32]), waits=["ac"], inc="dv",
               dr=True)
        s.emit("vector", lambda e, a=lnskd: e.tensor_tensor(
            out=rr2(s.cmt[0][:]), in0=rr2(s.cmt[0][:]), in1=a, op=ALU.add), inc="dv",
               dr=True)
        s.emit("vector", lambda e: e.tensor_scalar_max(out=s.cmt[0][:], in0=s.cmt[0][:],
                                                       scalar1=0.8 * MNEG), inc="dv")
        s.emit("vector", lambda e, a=lnskd: e.scalar_tensor_tensor(
            out=rr2(s.cmt[1][:]), in0=rr2(s.gccol[:]), scalar=-1.0, in1=a,
            op0=ALU.mult, op1=ALU.add), inc="dv")
        s.emit("vector", lambda e, a=lnsqd: e.tensor_tensor(
            out=rr2(s.cmt[2][:]), in0=rr2(s.gccol[:]), in1=a, op=ALU.add), inc="dv")
        s.emit("scalar", lambda e: e.activation(out=s.nr2col[:], in_=s.cmt[0][:],
                                                func=AF.Exp), waits=["dv"], inc="ac")
        s.emit("vector", lambda e: e.tensor_scalar_mul(out=s.nr2col[:], in0=s.nr2col[:],
                                                       scalar1=-1.0),
               waits=["ac"], inc="dv")
        s.emit("vector", lambda e: e.tensor_add(out=T[:, 32:48], in0=s.cmt[1][:],
                                                in1=s.glastb[:]), inc="dv")
        s.emit("scalar", lambda e: e.activation(out=s.r3col[:], in_=T[:, 32:48],
                                                func=AF.Exp), waits=["dv"], inc="ac")

        if s.lvl < 4:
            return

        masks = [s.maskL, s.maskU, s.maskUI]
        colv = [s.cmt[0], s.cmt[1], s.cmt[1]]
        rowv = [s.cmt[1], s.cmt[0], s.cmt[2]]
        for v in range(3):
            pst = s.psW if v < 2 else s.psW2
            for half in range(2):
                w0 = [("dv", C["dv"]), ("ac", C["ac"])]
                for hh in range(8):
                    h = half * 8 + hh
                    s.mm(s.psW2[0:1, 1024 + hh * 128: 1024 + (hh + 1) * 128],
                         rowv[v][:, h:h + 1], s.identf[:], waits=w0,
                         inc="pe" if hh == 7 else None)
                    w0 = []
                pev = C["pe"]
                s.emit("scalar", lambda e: e.activation(
                    out=s.rstf[0:1, 0:1024], in_=s.psW2[0:1, 1024:2048],
                    func=AF.Copy), waits=[("pe", pev)], inc="ac")
                acv = C["ac"]
                for j in range(2):
                    s.mm(pst[:, half * 1024 + j * 512: half * 1024 + (j + 1) * 512],
                         s.onesrow[:], s.rstf[0:1, j * 512:(j + 1) * 512],
                         start=True, stop=(v == 2),
                         waits=[("ac", acv)] if j == 0 else [],
                         inc="pe" if j == 1 else None)
            if v == 2:
                pev = C["pe"]
                s.emit("scalar", lambda e: e.activation(out=s.sqg[:], in_=s.psW2[:],
                                                        func=AF.Exp),
                       waits=[("pe", pev)], inc="ac")
                mAP = s.bc_ap(s.maskUI, 0, 128, 0, [[0, NHV], [1, 128]])
                s.emit("vector", lambda e, a=mAP: e.tensor_tensor(
                    out=s.r3(s.psW2[:]), in0=s.r3(s.psW2[:]), in1=a, op=ALU.add),
                    waits=["ac"], inc="dv", dr=True)
            else:
                for j in range(4):
                    s.mm(pst[:, j * 512:(j + 1) * 512], s.ident[:], masks[v][:],
                         start=False, stop=True,
                         inc="pe" if j == 3 else None)
            pev = C["pe"]
            cd = s.colbc(colv[v], NHV, 128)
            s.emit("vector", lambda e, a=cd, p=pst: e.tensor_tensor(
                out=s.r3(p[:]), in0=s.r3(p[:]), in1=a, op=ALU.add),
                waits=[("pe", pev)], inc="dv")
            s.emit("scalar", lambda e, v=v, p=pst: e.activation(
                out=s.dec[v][:], in_=p[:], func=AF.Exp),
                   waits=["dv"], inc="ac")
        dec_ac = C["ac"]

        if s.lvl < 5:
            return
        w0 = [("ac", dec_ac)]
        for h in range(NHQ):
            s.mm(s.psW2[:, h * 128:(h + 1) * 128], ak(h), ak(h), waits=w0)
            w0 = []
            s.mm(s.psW2[:, 1024 + h * 128: 1024 + (h + 1) * 128], ak(h), aq(h),
                 inc="pe" if h == NHQ - 1 else None)
        pev = C["pe"]
        kkd = s.dupq4(s.psW2, 0, 128)
        qkd = s.dupq4(s.psW2, 1024, 128)
        s.emit("vector", lambda e, a=kkd: e.tensor_tensor(
            out=s.r4(s.Mst[:]), in0=a, in1=s.r4(s.dec[0][:]), op=ALU.mult),
            waits=[("pe", pev)], inc="dv")
        s.emit("vector", lambda e, a=kkd: e.tensor_tensor(
            out=s.r4(s.Ust[:]), in0=a, in1=s.r4(s.dec[1][:]), op=ALU.mult), inc="dv")
        s.emit("vector", lambda e, a=qkd: e.tensor_tensor(
            out=s.r4(s.sqsc[:]), in0=a, in1=s.r4(s.dec[2][:]), op=ALU.mult), inc="dv")
        build_dv = C["dv"]

        if s.lvl < 6:
            return
        w0 = [("dv", build_dv), ("ac", dec_ac), ("po", s.po_init)]
        for ih in range(NHV):
            s.mm(s.psW[:, ih * 128:(ih + 1) * 128], ak(ih // 2),
                 s.Sbf[:, ih * 128:(ih + 1) * 128], waits=w0,
                 inc="pe" if ih == NHV - 1 else None)
            w0 = []
        pev = C["pe"]
        nr2d = s.colbc(s.nr2col, NHV, 128)
        s.emit("vector", lambda e, a=nr2d: e.tensor_tensor(
            out=s.r3(s.yst[0][:]), in0=s.r3(s.psW[:]), in1=a, op=ALU.mult),
            waits=[("pe", pev), ("dv", vb_dv)], inc="dv", dr=True)
        s.emit("vector", lambda e: e.tensor_add(out=s.zst[:], in0=s.yst[0][:],
                                                in1=s.vbtok[:]), inc="dv")
        z_dv = C["dv"]

        if s.lvl < 7:
            return
        w0 = [("dv", z_dv)]
        for ih in range(NHV):
            sl = slice(ih * 128, (ih + 1) * 128)
            s.mm(s.psW[:, sl], s.Ust[:, sl], s.zst[:, sl], waits=w0,
                 inc="pe" if ih == NHV - 1 else None)
            w0 = []
        pev = C["pe"]
        s.emit("vector", lambda e: e.scalar_tensor_tensor(
            out=s.yst[0][:], in0=s.psW[:], scalar=-1.0, in1=s.zst[:],
            op0=ALU.mult, op1=ALU.add), waits=[("pe", pev)], inc="dv")
        y_dv = C["dv"]
        Mp, Up = s.Mst, s.Ust
        ycur, pp = 0, 0
        for lev in range(1, 7):
            w_m2 = [("ac", C["ac"]), ("dv", C["dv"])]
            for ih in range(NHV):
                sl = slice(ih * 128, (ih + 1) * 128)
                s.mm(s.psW2[:, sl], Mp[:, sl], Up[:, sl], waits=w_m2,
                     inc="pe" if ih == NHV - 1 else None)
                w_m2 = []
            pev = C["pe"]
            s.emit("scalar", lambda e, p=pp: e.activation(out=s.Uw[p][:], in_=s.psW2[:],
                                                          func=AF.Copy),
                   waits=[("pe", pev)], inc="ac")
            if lev < 6:
                w_u2 = [("dv", y_dv)]
                for ih in range(NHV):
                    sl = slice(ih * 128, (ih + 1) * 128)
                    s.mm(s.psW[:, sl], Up[:, sl], Mp[:, sl], waits=w_u2,
                         inc="pe" if ih == NHV - 1 else None)
                    w_u2 = []
                pev = C["pe"]
                s.emit("scalar", lambda e, p=pp: e.activation(out=s.Pw[p][:],
                                                              in_=s.psW[:],
                                                              func=AF.Copy),
                       waits=[("pe", pev)], inc="ac")
            m2_ac = C["ac"]
            w_ap = [("ac", m2_ac)]
            for ih in range(NHV):
                sl = slice(ih * 128, (ih + 1) * 128)
                s.mm(s.psW[:, sl], s.Uw[pp][:, sl], s.yst[ycur][:, sl], waits=w_ap,
                     inc="pe" if ih == NHV - 1 else None)
                w_ap = []
            pev = C["pe"]
            s.emit("vector", lambda e, a=ycur: e.tensor_add(
                out=s.yst[1 - a][:], in0=s.psW[:], in1=s.yst[a][:]),
                waits=[("pe", pev)], inc="dv")
            y_dv = C["dv"]
            ycur = 1 - ycur
            Mp, Up = s.Pw[pp], s.Uw[pp]
            pp = 1 - pp
        vnew = s.yst[ycur]

        if s.lvl < 8:
            return
        qdup = s.dupq4(s.act, off, TA)
        s.emit("vector", lambda e, a=qdup: e.tensor_tensor(
            out=s.r4(s.qs[:]), in0=a, in1=s.r4(s.sqg[:]), op=ALU.mult),
            waits=["ac"], inc="dv")
        qs_dv = C["dv"]
        w0 = [("dv", qs_dv)]
        for ih in range(NHV):
            sl = slice(ih * 128, (ih + 1) * 128)
            s.mm(s.psW[:, sl], s.Sbf[:, sl], s.qs[:, sl], start=True, stop=False,
                 waits=w0)
            w0 = []
            s.mm(s.psW[:, sl], vnew[:, sl], s.sqsc[:, sl], start=False, stop=True,
                 inc="pe" if ih == NHV - 1 else None)
        pev = C["pe"]
        whs = [s.hseg_toks[-1]] if s.hseg_toks else []
        s.emit("scalar", lambda e: e.activation(out=s.oib[:], in_=s.psW[:],
                                                func=AF.Copy),
               waits=[("pe", pev)] + whs + s.qall("qh")[-1:], inc="ac")
        s.emit("vector", lambda e: None, waits=["ac"], inc="dv")
        oi_dv = C["dv"]

        if s.lvl < 9:
            return
        s.emit("scalar", lambda e: e.activation(out=s.sqg[:], in_=s.oib[:],
                                                func=AF.Square),
               waits=[("dv", oi_dv)], inc="ac")
        acv = C["ac"]
        for j in range(4):
            s.mm(s.psW2[0:1, j * 512:(j + 1) * 512], s.onescol[:],
                 s.sqg[:, j * 512:(j + 1) * 512],
                 waits=[("ac", acv), ("dv", C["dv"])] if j == 0 else [],
                 inc="pe" if j == 3 else None)
        pev = C["pe"]
        s.emit("scalar", lambda e: e.activation(out=s.rowstage[0:1, :],
                                                in_=s.psW2[0:1, 0:2048], func=AF.Copy),
               waits=[("pe", pev)], inc="ac")
        acv = C["ac"]
        for j in range(4):
            s.mm(s.psW[:, j * 512:(j + 1) * 512], s.onesrowb[:],
                 s.rowstage[0:1, j * 512:(j + 1) * 512],
                 waits=[("ac", acv), ("dv", oi_dv)] if j == 0 else [],
                 inc="pe" if j == 3 else None)
        pev = C["pe"]
        for jh in range(2):
            s.emit("scalar", lambda e, j=jh: e.activation(
                out=s.rstf[:], in_=s.psW[:, j * 1024:(j + 1) * 1024], func=AF.Ln,
                scale=1.0 / 128, bias=EPS),
                waits=[("pe", pev)] if jh == 0 else [], inc="ac")
            s.emit("scalar", lambda e, j=jh: e.activation(
                out=s.sqg[:, j * 1024:(j + 1) * 1024], in_=s.rstf[:],
                func=AF.Exp, scale=-0.5), inc="ac")
        s.emit("vector", lambda e: e.scalar_tensor_tensor(
            out=s.oib[:], in0=s.oib[:], scalar=s.normw_c[:], in1=s.sqg[:],
            op0=ALU.mult, op1=ALU.mult), waits=["ac"], inc="dv")
        h_dv = C["dv"]
        seg, col0 = n // 8, (n % 8) * 128
        s.hseg_toks.append(s.dma("gpsimd", lambda e, g=seg, c=col0: e.dma_start(
            out=s.hseg[g][:, c:c + 128].rearrange("(h e) t -> e h t", e=128),
            in_=s.oib[:].rearrange("p (h t) -> p h t", t=128)),
            "qh", waits=[("dv", h_dv)]))
        if s.dev:
            s.dma("gpsimd", lambda e, c=n * 128: e.dma_start(
                out=s.hdump[:, c:c + 128].rearrange("(h e) t -> e h t", e=128),
                in_=s.oib[:].rearrange("p (h t) -> p h t", t=128)), "qh")
        if n % 8 == 7:
            toks = s.hseg_toks[-8:]
            uniq = {}
            for sem, vv in toks:
                uniq[sem] = max(uniq.get(sem, 0), vv)
            s.emit("gpsimd", lambda e, g=seg: e.collective_compute(
                "AllGather", ALU.bypass,
                replica_groups=[[0, 1], [2, 3], [4, 5], [6, 7]],
                ins=[s.hseg[g]], outs=[s.gseg[g]]),
                waits=sorted(uniq.items()), inc="cc%d" % seg)

        if s.lvl < 10:
            return
        r3d = s.colbc(s.r3col, NHV, 128)
        vn3 = s.yst[1 - ycur]
        s.emit("vector", lambda e, a=r3d, vn=vnew, o=vn3: e.tensor_tensor(
            out=s.r3(o[:]), in0=s.r3(vn[:]), in1=a, op=ALU.mult), inc="dv")
        v3_dv = C["dv"]
        w0 = [("dv", v3_dv), ("ac", C["ac"])]
        for ih in range(NHV):
            sl = slice(ih * 128, (ih + 1) * 128)
            s.mm(s.psW2[:, sl], s.ktok[:, (ih // 2) * 128:(ih // 2) * 128 + 128],
                 vn3[:, sl], waits=w0, inc="pe" if ih == NHV - 1 else None)
            w0 = []
        pev = C["pe"]
        gamd = s.colbc(s.gamcol, NHV, 128)
        s.emit("vector", lambda e, a=gamd: e.tensor_tensor(
            out=s.r3(s.Sst[:]), in0=s.r3(s.Sst[:]), in1=a, op=ALU.mult), inc="dv")
        s.emit("vector", lambda e: e.tensor_add(out=s.Sst[:], in0=s.Sst[:],
                                                in1=s.psW2[:]),
               waits=[("pe", pev)], inc="dv", dr=True)
        s.emit("vector", lambda e: e.tensor_copy(out=s.Sbf[:], in_=s.Sst[:]), inc="dv")

    def phase_b(self):
        s, nc, C = self, self.nc, self.count
        fence = list(self.fence)
        bctx = contextlib.ExitStack()
        sbb = lambda n, sh, d: bctx.enter_context(nc.sbuf_tensor(n, sh, d))

        xstgB = [sbb(f"xstgB{i}", [128, 2048], F32) for i in range(4)]
        lncB = sbb("lncB", [128, 8], F32)
        xln2T = sbb("xln2T", [128, 16 * TB], BF16)
        xlnbB = sbb("xlnbB", [128, 2048], BF16)
        sqBB = sbb("sqBB", [128, 2048], BF16)
        wbufB = [sbb(f"wbufB{i}", [128, 2048], BF16) for i in range(2)]
        wbufB2 = [sbb(f"wbufB2{i}", [128, 2048], BF16) for i in range(2)]
        aofb = sbb("aofb", [128, 1024], BF16)
        rstdc = sbb("rstdc", [128, 8], F32)
        outst = [sbb(f"outst{i}", [128, 128], F32) for i in range(2)]
        hoffs = sbb("hoffs", [1, 1], U32)
        wob = sbb("wob", [128, 32 * 256], BF16)

        g1 = contextlib.ExitStack()
        htil = g1.enter_context(nc.sbuf_tensor("htil", [128, 32 * TB], BF16))
        szt = g1.enter_context(nc.sbuf_tensor("szt", [128, 32 * TB], BF16))
        xln1T = g1.enter_context(nc.sbuf_tensor("xln1T", [128, 16 * TB], BF16))
        g1.close()
        qctx = contextlib.ExitStack()
        wstgB = qctx.enter_context(nc.sbuf_tensor("wstgB", [128, 2048], F32))
        wqB = qctx.enter_context(nc.sbuf_tensor("wqB", [128, 2048], BF16))
        wq2B = qctx.enter_context(nc.sbuf_tensor("wq2B", [128, 2048], BF16))
        qscB = qctx.enter_context(nc.sbuf_tensor("qscB", [128, 96], F32))
        qscBh = qctx.enter_context(nc.sbuf_tensor("qscBh", [128, 32], BF16))
        wtT = qctx.enter_context(nc.sbuf_tensor("wtT", [128, 2048], BF16))

        hoff_q = s.dma("sync", lambda e: e.dma_start(out=hoffs[:], in_=s.hoff_in),
                       "qx", waits=fence)
        s.emit("scalar", lambda e: None, waits=fence, inc="ac")
        s.emit("gpsimd", lambda e: None, waits=fence, inc="po")
        s.emit("vector", lambda e: None, waits=fence, inc="dv")
        s.emit("tensor", lambda e: None, waits=fence, inc="pe")

        s.cvt_i, s.cvt_ac, s.cvt_qh = 0, [C["ac"]], []
        s.wstg = wstgB
        s.wbuf = wbufB
        for i in range(D // 128):
            for j in range(2):
                s.cvt(s.wzT_in[i * 128:(i + 1) * 128, j * 2048:(j + 1) * 2048],
                      s.wz_bf[i * 128:(i + 1) * 128, j * 2048:(j + 1) * 2048], 2048)
        for i in range(4096 // 128):
            s.cvt(s.woutT_in[i * 128:(i + 1) * 128, :],
                  s.wout_bf[i * 128:(i + 1) * 128, :], 2048)

        grp = lambda x: x.rearrange("p (g c) -> p g c", c=64)
        gAP = lambda base: s.bc_ap(qscB, 0, 128, base, [[1, 32], [0, 64]])
        gAPh = lambda: s.bc_ap(qscBh, 0, 128, 0, [[1, 32], [0, 64]])

        s.qt_pe = 0
        s.qt_store = None
        def quant_tile(src_ap, dst_ap):
            w_st = [("ac", s.cvt_ac[-1])]
            tl = s.dma("sync", lambda e, a=src_ap: e.dma_start(out=wstgB[:], in_=a),
                       "qw", waits=w_st)
            s.emit("scalar", lambda e: e.activation(out=wqB[:], in_=wstgB[:],
                                                    func=AF.Copy),
                   waits=[tl, ("dv", C["dv"])] + s.qall("qh"), inc="ac")
            s.cvt_ac.append(C["ac"])
            s.emit("vector", lambda e: e.reduce_sum(
                out=qscB[:, 0:32], in_=grp(wqB[:]), axis=AX,
                apply_absolute_value=True), waits=["ac"], inc="dv", dr=True)
            s.emit("vector", lambda e: e.tensor_scalar(
                out=qscB[:, 32:64], in0=qscB[:, 0:32], scalar1=1.0 / 64, scalar2=1e-8,
                op0=ALU.mult, op1=ALU.max), inc="dv", dr=True)
            s.emit("vector", lambda e: e.tensor_copy(out=qscBh[:],
                                                     in_=qscB[:, 32:64]), inc="dv",
                   dr=True)
            s.emit("vector", lambda e: e.tensor_copy(out=qscB[:, 32:64],
                                                     in_=qscBh[:]), inc="dv", dr=True)
            s.emit("vector", lambda e: e.tensor_scalar_mul(
                out=qscB[:, 0:32], in0=qscB[:, 32:64], scalar1=0.5), inc="dv", dr=True)
            s.emit("vector", lambda e: e.tensor_scalar_mul(
                out=qscB[:, 64:96], in0=qscB[:, 32:64], scalar1=-0.5), inc="dv")
            s.emit("vector", lambda e, a=gAP(0): e.tensor_tensor(
                out=grp(wq2B[:]), in0=grp(wqB[:]), in1=a, op=ALU.is_gt),
                waits=[("pe", s.qt_pe)], inc="dv")
            s.emit("vector", lambda e, a=gAP(64): e.tensor_tensor(
                out=grp(sqBB[:]), in0=grp(wqB[:]), in1=a, op=ALU.is_lt), inc="dv")
            s.emit("vector", lambda e: e.tensor_sub(out=wq2B[:], in0=wq2B[:],
                                                    in1=sqBB[:]), inc="dv")
            s.emit("vector", lambda e, a=gAP(32): e.tensor_tensor(
                out=grp(wq2B[:]), in0=grp(wq2B[:]), in1=a, op=ALU.mult), inc="dv")
            s.qt_wait_store = [s.qt_store] if s.qt_store else []
            s.transpose16b(wq2B, lambda c: wtT[:, c * 128:(c + 1) * 128],
                           s.psW[:, 0:512], s.qt_wait_store)
            s.qt_pe = C["pe"]
            acv = C["ac"]
            s.qt_store = s.dma("gpsimd", lambda e, d=dst_ap: e.dma_start(
                out=d, in_=wtT[:].rearrange("p (ct f) -> p ct f", f=128)),
                  "qh", waits=[("ac", acv)])

        for rt in range(16384 // 128):
            quant_tile(s.wgu_in[rt * 128:(rt + 1) * 128, :],
                       s.wguh_d[:, rt * 128:(rt + 1) * 128].rearrange(
                           "(ct p) f -> p ct f", p=128))
        for rt in range(D // 128):
            for q4 in range(4):
                quant_tile(
                    s.wp_in[rt * 128:(rt + 1) * 128, q4 * 2048:(q4 + 1) * 2048],
                    s.wph_d[q4 * 2048:(q4 + 1) * 2048,
                            rt * 128:(rt + 1) * 128].rearrange(
                                "(ct p) f -> p ct f", p=128))
        quant_done = ([("ac", C["ac"]), ("dv", C["dv"])] + s.qall("qh")
                      + s.qall("qw"))
        if s.dev:
            s.dma("gpsimd", lambda e: e.dma_start(out=s.wgudump, in_=s.wguh_d),
                  "qb", waits=quant_done)
        qctx.close()
        hres = bctx.enter_context(nc.sbuf_tensor("hres", [128, 64 * TB], BF16))
        wpb = bctx.enter_context(nc.sbuf_tensor("wpb", [128, 64 * 128], BF16))

        repack_q = []
        for sg in range(NSEG):
            repack_q.append(s.dma("gpsimd", lambda e, g=sg: e.dma_start(
                out=s.bigH[:, g * SEG:(g + 1) * SEG], in_=s.gseg[g]),
                "qb", waits=[("cc%d" % sg, 1)]))

        s.out_toks = []
        hload_pe = 0
        for bt in range(NTB):
            tok0 = bt * TB
            def hload(e, b=bt):
                v = e.value_load(hoffs[0:1, 0:1])
                src2 = s.bigH.rearrange("(ct p) t -> p ct t", p=128)
                return e.dma_start(
                    out=htil[:].rearrange("p (ct t) -> p ct t", t=TB),
                    in_=src2[:, :, bass.ts(v + b, TB)])
            h_q = s.dma("sync", hload,
                        "qx", waits=[hoff_q, ("pe", hload_pe)] + repack_q
                        + (quant_done if bt == 0 else []))

            for sub in range(4):
                t0 = tok0 + sub * 128
                xb = xstgB[sub]
                w_x = [("dv", C["dv"])]
                xt = s.dma("sync", lambda e, t=t0, x=xb: e.dma_start(
                    out=x[:], in_=s.xh_in[t:t + 128, :]), "qx", waits=w_x)
                s.ln_tile(xb, lncB, xlnbB, sqBB, xt, LNEPS)
                s.transpose16(
                    xlnbB,
                    lambda c, su=sub: xln1T[:, c * TB + su * 128: c * TB + (su + 1) * 128],
                    0, s.psW2[:, 0:512])
            x1_ac = C["ac"]

            zf_pe = {}
            sil_ac = {}
            for zf in range(32):
                bu = wbufB[zf % 2]
                w_bu = [("pe", zf_pe[zf - 2])] if (zf - 2) in zf_pe else []
                src = s.wz_bf.rearrange("(ct p) f -> p ct f", p=128)[
                    :, :, zf * 128:(zf + 1) * 128]
                wt = s.dma("sync", lambda e, a=src, b=bu: e.dma_start(
                    out=b[:, 0:2048].rearrange("p (ct f) -> p ct f", ct=16), in_=a),
                    "qw", waits=w_bu)
                half = zf % 2
                ps = s.psW[:, half * 512: half * 512 + 512]
                for c in range(16):
                    if c == 0:
                        w = [wt] + ([("ac", C["ac"]), ("dv", C["dv"])] if zf < 2
                                    else [("ac", sil_ac[zf - 2])])
                    else:
                        w = []
                    s.mm(ps[:, 0:TB], bu[:, c * 128:(c + 1) * 128],
                         xln1T[:, c * TB:(c + 1) * TB], start=(c == 0), stop=(c == 15),
                         waits=w, inc="pe" if c == 15 else None)
                zf_pe[zf] = C["pe"]
                pev = C["pe"]
                s.emit("scalar", lambda e, z=zf, p=ps: e.activation(
                    out=szt[:, z * TB:(z + 1) * TB], in_=p[:, 0:TB], func=AF.Silu),
                    waits=[("pe", pev)], inc="ac")
                sil_ac[zf] = C["ac"]
            sz_ac = C["ac"]
            s.emit("vector", lambda e: e.tensor_mul(out=htil[:], in0=htil[:],
                                                    in1=szt[:]),
                   waits=[("ac", sz_ac), h_q], inc="dv")
            gate_dv = C["dv"]

            op_ev = {}

            def op_tail(j):
                acv, dcol = op_ev[j]
                af = aofb[:, (j % 2) * 512:(j % 2) * 512 + 512]
                for st in range(4):
                    s.mm(s.psW2[:, st * 128:(st + 1) * 128],
                         af[:, st * 128:(st + 1) * 128], s.ident[:],
                         waits=[("ac", acv), ("dv", C["dv"])] if st == 0 else [],
                         inc="pe" if st == 3 else None)
                pev = C["pe"]
                for st in range(4):
                    s.emit("vector", lambda e, st=st, dc=dcol: e.tensor_add(
                        out=xstgB[st][:, dc:dc + 128],
                        in0=s.psW2[:, st * 128:(st + 1) * 128],
                        in1=xstgB[st][:, dc:dc + 128]),
                        waits=[("pe", pev)] if st == 0 else [], inc="dv")

            idx = 0
            for dq in range(8):
                src = s.wout_bf.rearrange("(ft p) d -> p ft d", p=128)[
                    :, :, dq * 256:(dq + 1) * 256]
                wt = s.dma("sync", lambda e, a=src: e.dma_start(
                    out=wob[:].rearrange("p (ft d) -> p ft d", d=256), in_=a),
                    "qw", waits=[("pe", C["pe"])])
                for dt in range(2):
                    pb = (idx % 2) * 1024
                    for f in range(32):
                        if f == 0:
                            w = [wt, ("dv", gate_dv)]
                            w.append(("ac", C["ac"]) if idx < 2 else
                                     ("ac", op_ev[idx - 2][0]))
                        else:
                            w = []
                        s.mm(s.psW[:, pb:pb + TB],
                             wob[:, f * 256 + dt * 128: f * 256 + dt * 128 + 128],
                             htil[:, f * TB:(f + 1) * TB], start=(f == 0),
                             stop=(f == 31), waits=w, inc="pe" if f == 31 else None)
                    pev = C["pe"]
                    if idx - 1 in op_ev:
                        op_tail(idx - 1)
                    s.emit("scalar", lambda e, pb=pb, sl=(idx % 2): e.activation(
                        out=aofb[:, sl * 512: sl * 512 + 512],
                        in_=s.psW[:, pb:pb + TB], func=AF.Copy),
                        waits=[("pe", pev)], inc="ac")
                    op_ev[idx] = (C["ac"], dq * 256 + dt * 128)
                    idx += 1
            op_tail(15)
            res_dv = C["dv"]
            if s.dev:
                for st in range(4):
                    s.dma("gpsimd", lambda e, st=st, t=tok0: e.dma_start(
                        out=s.x2dump[t + st * 128: t + (st + 1) * 128, :],
                        in_=xstgB[st][:]), "qh", waits=[("dv", res_dv)])

            for sub in range(4):
                s.ln_tile2(xstgB[sub], lncB, xlnbB, sqBB, res_dv, LNEPS)
                s.transpose16(
                    xlnbB,
                    lambda c, su=sub: xln2T[:, c * TB + su * 128: c * TB + (su + 1) * 128],
                    0, s.psW2[:, 0:512])
            x2_ac = C["ac"]

            gu_evac = {}
            for gi in range(64):
                bg = wbufB[gi % 2]
                bu2 = wbufB2[gi % 2]
                pb = (gi % 2) * 1024
                w_bu = [("pe", zf_pe[("g", gi - 2)])] if ("g", gi - 2) in zf_pe else []
                srcg = s.wguh_d.rearrange("(ct p) f -> p ct f", p=128)[
                    :, :, gi * 128:(gi + 1) * 128]
                srcu = s.wguh_d.rearrange("(ct p) f -> p ct f", p=128)[
                    :, :, 8192 + gi * 128: 8192 + (gi + 1) * 128]
                wt1 = s.dma("sync", lambda e, a=srcg, b=bg: e.dma_start(
                    out=b[:, 0:2048].rearrange("p (ct f) -> p ct f", ct=16), in_=a),
                    "qw", waits=w_bu)
                wt2 = s.dma("sync", lambda e, a=srcu, b=bu2: e.dma_start(
                    out=b[:, 0:2048].rearrange("p (ct f) -> p ct f", ct=16), in_=a),
                    "qw")
                for c in range(16):
                    w = [wt1, wt2] if c == 0 else []
                    if c == 0 and gi < 2:
                        w += [("ac", max(x2_ac, C["ac"])), ("dv", C["dv"])]
                    elif c == 0:
                        w += [("ac", gu_evac[("sq", gi - 2)]),
                              ("dv", gu_evac[("h", gi - 2)])]
                    s.mm(s.psW[:, pb:pb + TB], bg[:, c * 128:(c + 1) * 128],
                         xln2T[:, c * TB:(c + 1) * TB], start=(c == 0), stop=(c == 15),
                         waits=w)
                    s.mm(s.psW[:, pb + 512:pb + 512 + TB],
                         bu2[:, c * 128:(c + 1) * 128],
                         xln2T[:, c * TB:(c + 1) * TB], start=(c == 0), stop=(c == 15),
                         inc="pe" if c == 15 else None)
                zf_pe[("g", gi)] = C["pe"]
                pev = C["pe"]
                if gi >= 2:
                    j = gi - 2
                    s.mm(s.psW2[0:1, 0:512], s.onescol[:],
                         sqBB[:, (j % 4) * 512:(j % 4) * 512 + 512],
                         start=(j == 0), stop=False,
                         waits=[("ac", gu_evac[("sq", j)])])
                sl = (gi % 2) * 512
                s.emit("scalar", lambda e, pb=pb, sl=sl: e.activation(
                    out=aofb[:, sl:sl + 512], in_=s.psW[:, pb:pb + 512], func=AF.Silu),
                       waits=[("pe", pev), ("dv", C["dv"])], inc="ac")
                s.emit("vector", lambda e, g=gi, pb=pb, sl=sl: e.tensor_mul(
                    out=hres[:, g * TB:(g + 1) * TB], in0=aofb[:, sl:sl + 512],
                    in1=s.psW[:, pb + 512:pb + 1024]), waits=["ac"], inc="dv")
                gu_evac[("h", gi)] = C["dv"]
                s.emit("scalar", lambda e, g=gi, q=(gi % 4) * 512: e.activation(
                    out=sqBB[:, q:q + 512], in_=hres[:, g * TB:(g + 1) * TB],
                    func=AF.Square), waits=["dv", ("pe", C["pe"])], inc="ac")
                gu_evac[("sq", gi)] = C["ac"]
            for j in (62, 63):
                s.mm(s.psW2[0:1, 0:512], s.onescol[:],
                     sqBB[:, (j % 4) * 512:(j % 4) * 512 + 512],
                     start=False, stop=(j == 63),
                     waits=[("ac", gu_evac[("sq", j)])],
                     inc="pe" if j == 63 else None)
            pev = C["pe"]
            s.emit("scalar", lambda e: e.activation(out=s.rowstage[0:1, 0:512],
                                                    in_=s.psW2[0:1, 0:512],
                                                    func=AF.Copy),
                   waits=[("pe", pev)], inc="ac")
            acv = C["ac"]
            for st in range(4):
                s.mm(s.psW2[:, 512 + st: 513 + st],
                     s.rowstage[0:1, st * 128:(st + 1) * 128], s.onesrowb[0:1, 0:1],
                     waits=[("ac", acv)] if st == 0 else [],
                     inc="pe" if st == 3 else None)
            pev = C["pe"]
            s.emit("scalar", lambda e: e.activation(out=rstdc[:, 0:4],
                                                    in_=s.psW2[:, 512:516], func=AF.Ln,
                                                    scale=1.0 / 8192, bias=F32EPS),
                   waits=[("pe", pev)], inc="ac")
            s.emit("scalar", lambda e: e.activation(out=rstdc[:, 4:8],
                                                    in_=rstdc[:, 0:4], func=AF.Exp,
                                                    scale=-0.5), inc="ac")
            rstd_ac = C["ac"]

            if s.dev and bt == 0:
                s.dma("gpsimd", lambda e: e.dma_start(out=s.hrdump, in_=hres[:]),
                      "qh", waits=[("dv", C["dv"]), ("ac", C["ac"])])
                s.dma("gpsimd", lambda e: e.dma_start(out=s.xl2dump, in_=xln2T[:]),
                      "qh")
            pj_ev = {}

            def pj_tail(j):
                acv = pj_ev[j]
                af = aofb[:, (j % 2) * 512:(j % 2) * 512 + 512]
                for st in range(4):
                    s.mm(s.psW2[:, 1024 + st * 128: 1024 + (st + 1) * 128],
                         af[:, st * 128:(st + 1) * 128], s.ident[:],
                         waits=[("ac", acv), ("dv", C["dv"])] if st == 0 else [],
                         inc="pe" if st == 3 else None)
                pev = C["pe"]
                for st in range(4):
                    ob = outst[st % 2]
                    w = [("pe", pev)]
                    if len(s.out_toks) >= 2:
                        w.append(s.out_toks[-2])
                    s.emit("vector", lambda e, st=st, d=j, o=ob: e.scalar_tensor_tensor(
                        out=o[:], in0=s.psW2[:, 1024 + st * 128: 1024 + (st + 1) * 128],
                        scalar=rstdc[:, 4 + st:5 + st],
                        in1=xstgB[st][:, d * 128:(d + 1) * 128],
                        op0=ALU.mult, op1=ALU.add), waits=w, inc="dv")
                    dvv = C["dv"]
                    s.out_toks.append(s.dma(
                        "gpsimd", lambda e, st=st, d=j, t=tok0, o=ob: e.dma_start(
                            out=s.out[t + st * 128: t + (st + 1) * 128,
                                      d * 128:(d + 1) * 128], in_=o[:]),
                        "qo", waits=[("dv", dvv)]))

            for dt16 in range(16):
                src = s.wph_d.rearrange("(ht p) d -> p ht d", p=128)[
                    :, :, dt16 * 128:(dt16 + 1) * 128]
                wt = s.dma("sync", lambda e, a=src: e.dma_start(
                    out=wpb[:].rearrange("p (ht f) -> p ht f", ht=64), in_=a),
                    "qw", waits=[("pe", C["pe"])])
                pb = (dt16 % 2) * 1024
                for hf in range(64):
                    if hf == 0:
                        w = [wt, ("ac", rstd_ac if dt16 < 2 else pj_ev[dt16 - 2])]
                    else:
                        w = []
                    s.mm(s.psW[:, pb:pb + TB], wpb[:, hf * 128:(hf + 1) * 128],
                         hres[:, hf * TB:(hf + 1) * TB], start=(hf == 0),
                         stop=(hf == 63), waits=w, inc="pe" if hf == 63 else None)
                pev = C["pe"]
                if dt16 - 1 in pj_ev:
                    pj_tail(dt16 - 1)
                s.emit("scalar", lambda e, pb=pb, sl=(dt16 % 2): e.activation(
                    out=aofb[:, sl * 512: sl * 512 + 512],
                    in_=s.psW[:, pb:pb + 512], func=AF.Copy),
                       waits=[("pe", pev)], inc="ac")
                pj_ev[dt16] = C["ac"]
            pj_tail(15)
            hload_pe = C["pe"]

        s.emit("gpsimd", lambda e: None, waits=s.qall("qo"), inc="po")
        bctx.close()

    def ln_tile2(self, xb, lnc, xout, sq_t, dvw, eps):
        s = self
        s.emit("vector", lambda e: e.reduce_sum(out=lnc[:, 0:1], in_=xb[:], axis=AX),
               waits=[("dv", dvw)], inc="dv")
        s.emit("scalar", lambda e: e.activation(out=sq_t[:], in_=xb[:], func=AF.Square),
               waits=[("dv", dvw)], inc="ac")
        s.emit("vector", lambda e: e.reduce_sum(out=lnc[:, 1:2], in_=sq_t[:], axis=AX),
               waits=["ac"], inc="dv")
        s.emit("vector", lambda e: e.tensor_scalar_mul(out=lnc[:, 2:3], in0=lnc[:, 0:1],
                                                       scalar1=1.0 / D), inc="dv", dr=True)
        s.emit("vector", lambda e: e.tensor_mul(out=lnc[:, 3:4], in0=lnc[:, 2:3],
                                                in1=lnc[:, 2:3]), inc="dv", dr=True)
        s.emit("vector", lambda e: e.scalar_tensor_tensor(
            out=lnc[:, 4:5], in0=lnc[:, 1:2], scalar=1.0 / D, in1=lnc[:, 3:4],
            op0=ALU.mult, op1=ALU.subtract), inc="dv")
        s.emit("scalar", lambda e: e.activation(out=lnc[:, 5:6], in_=lnc[:, 4:5],
                                                func=AF.Ln, bias=eps),
               waits=["dv"], inc="ac")
        s.emit("scalar", lambda e: e.activation(out=lnc[:, 6:7], in_=lnc[:, 5:6],
                                                func=AF.Exp, scale=-0.5), inc="ac")
        s.emit("vector", lambda e: e.scalar_tensor_tensor(
            out=lnc[:, 7:8], in0=lnc[:, 2:3], scalar=-1.0, in1=lnc[:, 6:7],
            op0=ALU.mult, op1=ALU.mult), waits=["ac"], inc="dv")
        s.emit("scalar", lambda e: e.activation(out=xout[:], in_=xb[:],
                                                func=AF.Identity,
                                                bias=lnc[:, 7:8], scale=lnc[:, 6:7]),
               waits=["dv"], inc="ac")


_NC_CACHE = {}


def _get_nc(dev=False, stop_a=False, nta_lim=NTA, lvl=99):
    key = (dev, stop_a, nta_lim, lvl)
    if key not in _NC_CACHE:
        _NC_CACHE[key] = KB(dev_dump=dev, stop_after_a=stop_a,
                            nta_lim=nta_lim, lvl=lvl).build()
    return _NC_CACHE[key]


def _core_inputs(inputs, b, hf):
    x = np.ascontiguousarray(inputs["L0_attn_x"][b])
    Wqkvz = inputs["L0_attn_W_qkvz"]
    Wba = inputs["L0_attn_W_ba"]
    hq = range(8 * hf, 8 * hf + 8)
    hv = list(range(16 * hf, 16 * hf + 16))
    rows_q = np.concatenate([np.arange(768 * h, 768 * h + 128) for h in hq])
    rows_k = np.concatenate([np.arange(768 * h + 128, 768 * h + 256) for h in hq])
    rows_v = np.concatenate([np.arange(768 * h + 256, 768 * h + 512) for h in hq])
    rows_qkv = np.concatenate([rows_q, rows_k, rows_v])
    rows_ba = np.concatenate([[4 * h, 4 * h + 1] for h in hq]
                             + [[4 * h + 2, 4 * h + 3] for h in hq])
    rows_z = np.concatenate([np.arange(768 * h + 512, 768 * h + 768)
                             for h in range(16)])
    ch_q = np.concatenate([np.arange(128 * h, 128 * h + 128) for h in hq])
    ch = np.concatenate([ch_q, 2048 + ch_q,
                         np.concatenate([4096 + np.arange(256 * h, 256 * h + 256)
                                         for h in hq])])
    return {
        "x": x,
        "xh": np.ascontiguousarray(x[hf * 4096:(hf + 1) * 4096]),
        "wqkvT": np.ascontiguousarray(Wqkvz[rows_qkv].T),
        "wbaT": np.ascontiguousarray(Wba[rows_ba].T),
        "convw": np.ascontiguousarray(inputs["L0_attn_conv_w"][ch]),
        "dtb": np.ascontiguousarray(inputs["L0_attn_dt_bias"][hv][None, :]),
        "alog": np.ascontiguousarray(inputs["L0_attn_A_log"][hv][None, :]),
        "normw": np.ascontiguousarray(inputs["L0_attn_norm_w"][:, None]),
        "wzT": np.ascontiguousarray(Wqkvz[rows_z].T),
        "woutT": np.ascontiguousarray(inputs["L0_attn_W_out"].T),
        "wgu": np.ascontiguousarray(inputs["L0_ffn_w_gate_up"]),
        "wp": np.ascontiguousarray(inputs["L0_ffn_w_proj"]),
        "hoff": np.array([[hf * NTB]], dtype=np.uint32),
    }


def kernel(**inputs):
    nc = _get_nc(False)
    in_maps = []
    for c in range(8):
        in_maps.append(_core_inputs(inputs, c // 2, c % 2))
    res = run_bass_kernel_spmd(nc, in_maps, core_ids=list(range(8)))
    out = np.empty((B, S, D), np.float32)
    for c in range(8):
        b, hf = c // 2, c % 2
        out[b, hf * 4096:(hf + 1) * 4096] = res.results[c]["out"]
    return out
